# version 03
# baseline (speedup 1.0000x reference)
import contextlib
import math
import numpy as np

import concourse.bass as bass
import concourse.mybir as mybir
from concourse.bass_utils import run_bass_kernel_spmd

F32 = mybir.dt.float32
BF16 = mybir.dt.bfloat16
AF = mybir.ActivationFunctionType
ALU = mybir.AluOpType
AX = mybir.AxisListType.X

D = 1024
T = 2048
NCORE = 8
NH_A = 16
DH_A = 64
SCALE_A = DH_A ** -0.5
H_B = 4
DK = 256
DV = 256
CHUNK = 32
NCHUNK = T // CHUNK
GRP = 4
NGRP = NCHUNK // GRP
FORGET_MIN = 0.55
MAX_FLOOR = 0.075
E_FF = 8
R_FF = 4
ER = E_FF * R_FF
DMID = 2048
GATE_IN = D + 16 * H_B
LFT = 256
KLONG = 63
NTT = T // 128
KSTAGE = 99
SUBSTAGE = 99
DBG_SPILLS = False


def _host_constants():
    c = {}
    c["ident"] = np.eye(128, dtype=np.float32)
    i32 = np.arange(32)
    sl = (i32[:, None] > i32[None, :]).astype(np.float32)
    il = (i32[:, None] >= i32[None, :]).astype(np.float32)
    msl = np.zeros((128, 128), np.float32)
    mil = np.zeros((128, 128), np.float32)
    for b in range(4):
        msl[b * 32:(b + 1) * 32, b * 32:(b + 1) * 32] = sl
        mil[b * 32:(b + 1) * 32, b * 32:(b + 1) * 32] = il
    c["mask_sl"] = -msl
    c["mask_su"] = np.ascontiguousarray(-msl.T)
    c["mask_il"] = mil
    c["mask_iu"] = np.ascontiguousarray(mil.T)
    L = LFT
    s = np.arange(L)
    f = np.arange(128)
    ang = 2.0 * np.pi * np.outer(s, f) / L
    fwdA = np.cos(ang)
    fwdB = np.zeros((L, 128))
    fwdB[:, 0] = np.cos(np.pi * s)
    fwdB[:, 1:] = -np.sin(ang[:, 1:])
    c["dft_fwdA"] = fwdA.astype(np.float32)
    c["dft_fwdB"] = fwdB.astype(np.float32)
    tau = np.arange(128, 256)
    angi = 2.0 * np.pi * np.outer(f, tau) / L
    invA = np.zeros((128, 128)); invA[0] = 1.0 / L
    invA[1:] = 2.0 / L * np.cos(angi[1:])
    invB = np.zeros((128, 128)); invB[0] = np.cos(np.pi * tau) / L
    invB[1:] = -2.0 / L * np.sin(angi[1:])
    c["dft_invA"] = invA.astype(np.float32)
    c["dft_invB"] = invB.astype(np.float32)
    j = np.arange(KLONG)
    lag = (KLONG - 1) - j
    angh = 2.0 * np.pi * np.outer(lag, f) / L
    hA = np.cos(angh)
    hB = np.zeros((KLONG, 128))
    hB[:, 0] = np.cos(np.pi * lag)
    hB[:, 1:] = -np.sin(angh[:, 1:])
    c["dft_hA"] = hA.astype(np.float32)
    c["dft_hB"] = hB.astype(np.float32)
    return c


CONSTS = _host_constants()


class KB:

    def __init__(self, nc, ctx):
        self.nc = nc
        self.ctx = ctx
        self.q = {"sync": [], "tensor": [], "vector": [], "scalar": [], "gpsimd": []}
        self.sems = {}
        self.cnt = {}
        for s in ("dma", "dmb", "dmc", "dmo", "pe", "dve", "act", "pool"):
            self.sems[s] = ctx.enter_context(nc.semaphore(s))
            self.cnt[s] = 0
        self.lastw = {e: {} for e in self.q}

    def on(self, engine, fn):
        self.q[engine].append(fn)

    def wait(self, engine, sem_name, v=None):
        if v is None:
            v = self.cnt[sem_name]
        if v <= 0:
            return
        if self.lastw[engine].get(sem_name, -1) >= v:
            return
        self.lastw[engine][sem_name] = v
        sem = self.sems[sem_name]
        self.on(engine, lambda e, sm=sem, vv=v: e.wait_ge(sm, vv))

    def run(self, engine, fn, inc=None, n=1):
        if inc is None:
            self.on(engine, fn)
            return self.cnt.get(engine, 0)
        self.cnt[inc] += n
        v = self.cnt[inc]
        sem = self.sems[inc]
        def wrapped(e, fn=fn, sem=sem, n=n):
            fn(e).then_inc(sem, n)
        self.on(engine, wrapped)
        return v

    def dma(self, dst_ap, src_ap, sem="dma", queue="sync", slow=False):
        self.cnt[sem] += 16
        v = self.cnt[sem]
        s = self.sems[sem]
        if slow:
            self.on(queue, lambda e, d=dst_ap, sp=src_ap, sm=s:
                    e.dma_start(out=d, in_=sp, allow_slow_non_contiguous=True).then_inc(sm, 16))
        else:
            self.on(queue, lambda e, d=dst_ap, sp=src_ap, sm=s:
                    e.dma_start(out=d, in_=sp).then_inc(sm, 16))
        return v

    def fence(self, engine, sem="dve" if True else None):
        s = {"vector": "dve", "scalar": "act", "gpsimd": "pool"}[engine]
        self.cnt[s] += 1
        v = self.cnt[s]
        sem_h = self.sems[s]
        self.on(engine, lambda e, sm=sem_h, vv=v: e.nop().then_inc(sm, vv - vv + 1))
        self.on(engine, lambda e, sm=sem_h, vv=v: e.wait_ge(sm, vv))
        return v


def ceil_div(a, b):
    return (a + b - 1) // b


def build_program(stage=99):
    nc = bass.Bass("TRN2", target_bir_lowering=False)

    din = {}
    def dt_in(name, shape):
        din[name] = nc.dram_tensor(name, list(shape), F32, kind="ExternalInput").ap()
        return din[name]

    x_in = dt_in("x_fm", [D, T])
    w_qkv = dt_in("w_qkv", [3 * D, D])
    w_out_a = dt_in("w_out_a", [D, D])
    b_out_a = dt_in("b_out_a", [1, D])
    Wq_d = dt_in("Wq", [D, D]); Wk_d = dt_in("Wk", [D, D]); Wv_d = dt_in("Wv", [D, D])
    Wb_d = dt_in("Wb", [D, H_B])
    forget_d = dt_in("forget", [1, H_B])
    conv_qd = dt_in("conv_q", [D, 4]); conv_kd = dt_in("conv_k", [D, 4]); conv_vd = dt_in("conv_v", [D, 4])
    fir_sd = dt_in("fir_s", [D, 3])
    fir_ld = dt_in("fir_l", [D, KLONG])
    log_temp_d = dt_in("log_temp", [1, H_B])
    floor_d = dt_in("floor_p", [1, H_B * 4])
    W1_d = dt_in("W1", [GATE_IN, DMID])
    b1_d = dt_in("b1", [1, DMID])
    W2_d = dt_in("W2", [DMID, H_B * 4])
    b2_d = dt_in("b2", [1, H_B * 4])
    norm_wd = dt_in("norm_w", [1, DV])
    Wo_d = dt_in("Wo", [D, D])
    W_lin_d = dt_in("W_lin", [D, D])
    b_lin_d = dt_in("b_lin", [1, D])
    gate_wd = dt_in("gate_w", [E_FF, D])
    lora_ld = dt_in("lora_l", [E_FF, D, R_FF])
    lora_rd = dt_in("lora_r", [ER, D])
    for k, v in CONSTS.items():
        dt_in("c_" + k, v.shape)

    out_fm = nc.dram_tensor("out_fm", [D, T], F32, kind="ExternalOutput").ap()
    scr_small = nc.dram_tensor("scr_small", [64, 64], F32, kind="Internal").ap()
    scr_short = nc.dram_tensor("scr_short", [T, D], F32, kind="Internal").ap()
    _spill_kind = "ExternalOutput" if DBG_SPILLS else "Internal"
    scr_short_b = nc.dram_tensor("scr_short_b", [T, D], BF16, kind=_spill_kind).ap()
    scr_long_b = nc.dram_tensor("scr_long_b", [T, D], BF16, kind=_spill_kind).ap()
    scr_delta_b = nc.dram_tensor("scr_delta_b", [T, D], BF16, kind=_spill_kind).ap()
    scr_dbg = nc.dram_tensor("scr_dbg", [128, 256], F32, kind=_spill_kind).ap()
    scr_dbg2 = nc.dram_tensor("scr_dbg2", [128, 256], F32, kind=_spill_kind).ap()
    scr_dbg3 = nc.dram_tensor("scr_dbg3", [64, 2048], BF16, kind=_spill_kind).ap()

    with contextlib.ExitStack() as ctx:
        kb = KB(nc, ctx)
        SB = lambda name, shape, dt=BF16: ctx.enter_context(nc.sbuf_tensor(name, shape, dt))

        x_res = SB("x_res", [128, 8 * T], F32)
        x_ln = SB("x_ln", [128, 8 * T], BF16)
        bt1 = SB("bt1", [128, 16 * 1024], BF16)
        bt2 = SB("bt2", [128, 4 * 1024], BF16)
        bt3 = SB("bt3", [128, 16 * 1024], BF16)
        wstream = SB("wstream", [128, 6 * 1024], BF16)
        wsmall = SB("wsmall", [128, 1024], BF16)
        wstage = SB("wstage", [128, 1024], F32)
        ident_b = SB("ident_b", [128, 128], BF16)
        ident_f = SB("ident_f", [128, 128], F32)
        masks = SB("masks", [128, 4 * 128], BF16)
        stat = SB("stat", [128, 512], F32)
        statb = SB("statb", [128, 1024], BF16)
        svt = SB("svt", [128, 16 * 65], BF16)
        bcols = SB("bcols", [128, 64], F32)
        b1cols = SB("b1cols", [128, 16], F32)
        Sst = SB("Sst", [128, 2 * 256], F32)
        stats_tok = SB("stats_tok", [128, 16 * 64], BF16)
        ptile = SB("ptile", [128, 16 * 16], F32)
        wconv = SB("wconv", [128, 660], F32)

        PS = lambda name, shape, dt=F32: ctx.enter_context(nc.psum_tensor(name, shape, dt))
        p2a = PS("p2a", [128, 1024])
        p2b = PS("p2b", [128, 1024])
        p1a = PS("p1a", [128, 512])
        p1b = PS("p1b", [128, 512])
        p1c = PS("p1c", [128, 512])
        ptr = PS("ptr", [128, 256], BF16)

        sems = kb.sems
        W = kb.wait
        R = kb.run

        def mm(e, out, lhsT, rhs, first, last, tp=None):
            return e.matmul(out, lhsT, rhs, start=first, stop=last, tile_position=tp)

        def evac(engine, dst, src, inc):
            return R(engine, lambda e, d=dst, s=src: e.tensor_copy(d, s), inc=inc)

        def transpose_to(dst, src, slot, pe_inc=True):
            v_pe = R("tensor", lambda e, s=src, sl=slot: e.transpose(
                ptr[:, sl * 128: sl * 128 + src.shape[0]], s, ident_b[:]), inc="pe")
            W("vector", "pe", v_pe)
            v_d = evac("vector", dst, ptr[:, slot * 128: slot * 128 + src.shape[0]], "dve")
            return v_pe, v_d

        for c in range(8):
            kb.dma(x_res[:, c * T:(c + 1) * T],
                   x_in.rearrange("(c p) t -> c p t", p=128)[c])
        kb.dma(wstage[:, 0:128], din["c_ident"])
        for i, mname in enumerate(("mask_sl", "mask_su", "mask_il", "mask_iu")):
            kb.dma(wstage[:, 128 * (i + 1):128 * (i + 2)], din["c_" + mname])
        v_x = kb.cnt["dma"]
        W("vector", "dma", v_x)
        R("vector", lambda e: e.tensor_copy(ident_b[:], wstage[:, 0:128]))
        R("vector", lambda e: e.tensor_copy(ident_f[:], wstage[:, 0:128]))
        v_mcast = R("vector", lambda e: e.tensor_copy(masks[:, 0:512], wstage[:, 128:640]),
                    inc="dve")

        def emit_ln_slab():
            sqb = bt2
            W("vector", "pe", kb.cnt["pe"])
            for s0 in range(0, T, 512):
                v_cast = 0
                for c in range(8):
                    v_cast = R("vector", lambda e, c=c, s0=s0: e.tensor_copy(
                        x_ln[:, c * T + s0: c * T + s0 + 512],
                        x_res[:, c * T + s0: c * T + s0 + 512]), inc="dve")
                W("scalar", "dve", v_cast)
                v_sq = 0
                for c in range(8):
                    v_sq = R("scalar", lambda e, c=c, s0=s0: e.activation(
                        sqb[:, c * 512: c * 512 + 512],
                        x_ln[:, c * T + s0: c * T + s0 + 512], AF.Square), inc="act")
                W("tensor", "dve", v_cast)
                W("tensor", "act", v_sq)
                for c in range(8):
                    R("tensor", lambda e, c=c, s0=s0: mm(
                        e, p1a[0:1, 0:512], ones_col[:, 0:1],
                        x_ln[:, c * T + s0: c * T + s0 + 512], c == 0, c == 7))
                v_pe = 0
                for c in range(8):
                    v_pe = R("tensor", lambda e, c=c: mm(
                        e, p1b[0:1, 0:512], ones_col[:, 0:1],
                        sqb[:, c * 512: c * 512 + 512], c == 0, c == 7), inc="pe")
                W("vector", "pe", v_pe)
                R("vector", lambda e: e.tensor_scalar(
                    stat[0:1, 0:512], p1a[0:1, 0:512], 1.0 / D, None, ALU.mult))
                R("vector", lambda e: e.tensor_scalar(
                    wstage[0:1, 0:512], p1b[0:1, 0:512], 1.0 / D, None, ALU.mult))
                R("vector", lambda e: e.tensor_mul(Sst[0:1, 0:512], stat[0:1, 0:512],
                                                   stat[0:1, 0:512]))
                R("vector", lambda e: e.tensor_sub(Sst[0:1, 0:512], wstage[0:1, 0:512],
                                                   Sst[0:1, 0:512]))
                v_d = R("vector", lambda e: e.tensor_scalar(
                    Sst[0:1, 0:512], Sst[0:1, 0:512], 1e-5, None, ALU.add), inc="dve")
                W("scalar", "dve", v_d)
                v_a = R("scalar", lambda e: e.activation(Sst[0:1, 0:512], Sst[0:1, 0:512],
                                                         AF.Sqrt), inc="act")
                W("vector", "act", v_a)
                R("vector", lambda e: e.reciprocal(Sst[0:1, 0:512], Sst[0:1, 0:512]))
                R("vector", lambda e: e.tensor_mul(wstage[0:1, 0:512], stat[0:1, 0:512],
                                                   Sst[0:1, 0:512]))
                R("vector", lambda e: e.tensor_copy(statb[0:1, 0:512], Sst[0:1, 0:512]))
                v_d = R("vector", lambda e: e.tensor_copy(statb[0:1, 512:1024],
                                                          wstage[0:1, 0:512]), inc="dve")
                W("tensor", "dve", v_d)
                R("tensor", lambda e: mm(e, p1b[:, 0:512], ones_row[0:1, 0:128],
                                         statb[0:1, 0:512], True, True))
                v_pe = R("tensor", lambda e: mm(e, p1c[:, 0:512], ones_row[0:1, 0:128],
                                                statb[0:1, 512:1024], True, True), inc="pe")
                W("vector", "pe", v_pe)
                vlast = 0
                for c in range(8):
                    R("vector", lambda e, c=c, s0=s0: e.tensor_tensor(
                        x_ln[:, c * T + s0: c * T + s0 + 512],
                        x_ln[:, c * T + s0: c * T + s0 + 512], p1b[:, 0:512], ALU.mult))
                    vlast = R("vector", lambda e, c=c, s0=s0: e.tensor_tensor(
                        x_ln[:, c * T + s0: c * T + s0 + 512],
                        x_ln[:, c * T + s0: c * T + s0 + 512], p1c[:, 0:512], ALU.subtract),
                        inc="dve")
                W("tensor", "dve", vlast)
            return kb.cnt["dve"]

        ones_col = SB("ones_col", [128, 2], BF16)
        zrow = SB("zrow", [1, 512], BF16)
        ones_row = SB("ones_row", [1, 128], BF16)
        R("vector", lambda e: e.memset(ones_col[:], 1.0))
        R("vector", lambda e: e.memset(zrow[:], 0.0))
        R("vector", lambda e: e.memset(ones_row[:], 1.0))
        R("vector", lambda e: e.memset(svt[:], 1.0))

        W("vector", "dma", v_x)
        if stage >= 1:
            emit_ln_slab()

        def prep_qkvT(hp, slot):
            vs = []
            for j, base in enumerate((0, D, 2 * D)):
                W("sync", "dve", kb.cnt["dve"])
                v_l = kb.dma(wstage[:, 0:1024],
                             w_qkv[base + hp * 128: base + hp * 128 + 128, :], sem="dmb")
                W("vector", "dmb", v_l)
                v_c = R("vector", lambda e: e.tensor_copy(wsmall[:, 0:1024], wstage[:, 0:1024]),
                        inc="dve")
                for c in range(8):
                    W("tensor", "dve", v_c)
                    v_pe = R("tensor", lambda e, c=c: e.transpose(
                        ptr[:, 0:128], wsmall[:, c * 128:(c + 1) * 128], ident_b[:]), inc="pe")
                    W("vector", "pe", v_pe)
                    v_d = R("vector", lambda e, c=c, j=j, slot=slot: e.tensor_copy(
                        wstream[:, slot * 3072 + c * 384 + j * 128: slot * 3072 + c * 384 + j * 128 + 128],
                        ptr[:, 0:128]), inc="dve")
                    W("tensor", "dve", v_d)
                vs.append(v_d)
            return kb.cnt["dve"]

        def prep_bias_cols(dsrc, dst_off, n_chunk):
            W("sync", "dve", kb.cnt["dve"])
            v_l = kb.dma(wstage[0:1, 0:n_chunk * 128], dsrc, sem="dmb")
            W("vector", "dmb", v_l)
            v_c = R("vector", lambda e: e.tensor_copy(statb[0:1, 0:n_chunk * 128],
                                                      wstage[0:1, 0:n_chunk * 128]), inc="dve")
            for c in range(n_chunk):
                W("tensor", "dve", v_c)
                v_pe = R("tensor", lambda e, c=c: e.transpose(
                    ptr[:, 0:1], statb[0:1, c * 128:(c + 1) * 128], ident_b[0:1, 0:1]), inc="pe")
                W("vector", "pe", v_pe)
                v_d = R("vector", lambda e, c=c: e.tensor_copy(
                    bcols[:, dst_off + c: dst_off + c + 1], ptr[:, 0:1]), inc="dve")
                W("tensor", "dve", v_d)
            return kb.cnt["dve"]


        def emit_attn_a(stage=99):
            sub = SUBSTAGE
            prep_bias_cols(b_out_a, 0, 8)
            prep_qkvT(0, 0)
            for hp in range(8 if sub >= 5 else 1):
                slot = hp % 2
                if hp < 7 and sub >= 5:
                    prep_qkvT(hp + 1, (hp + 1) % 2)
                for j in range(3):
                    for s0 in range(0, T, 512):
                        v_pe = 0
                        for c in range(8):
                            v_pe = R("tensor", lambda e, j=j, s0=s0, c=c, slot=slot: mm(
                                e, p2a[:, 0:512],
                                wstream[:, slot * 3072 + c * 384 + j * 128: slot * 3072 + c * 384 + j * 128 + 128],
                                x_ln[:, c * T + s0: c * T + s0 + 512], c == 0, c == 7),
                                inc="pe")
                        W("vector", "pe", v_pe)
                        v_d = R("vector", lambda e, j=j, s0=s0: e.tensor_copy(
                            bt3[:, j * T + s0: j * T + s0 + 512], p2a[:, 0:512]), inc="dve")
                        W("tensor", "dve", v_d)
                if sub < 2:
                    continue
                v_d = kb.cnt["dve"]
                W("vector", "dve", v_d)
                for j in range(2):
                    v_sqd = R("vector", lambda e, j=j: e.tensor_mul(
                        bt2[:, j * T: j * T + T], bt3[:, j * T: j * T + T],
                        bt3[:, j * T: j * T + T]), inc="dve")
                W("tensor", "dve", v_sqd)
                for j in range(2):
                    v_pe2 = 0
                    for s0 in range(0, T, 512):
                        v_pe2 = R("tensor", lambda e, j=j, s0=s0: mm(
                            e, p1a[0:2, 0:512], hones[:, 0:2],
                            bt2[:, j * T + s0: j * T + s0 + 512], True, True), inc="pe")
                        W("vector", "pe", v_pe2)
                        if s0 == 0:
                            v_dm = R("vector", lambda e, j=j: e.tensor_reduce(
                                stat[0:2, 8 + j: 9 + j], p1a[0:2, 0:512], AX, ALU.max),
                                inc="dve")
                        else:
                            v_dm = R("vector", lambda e: e.tensor_reduce(
                                stat[0:2, 10:11], p1a[0:2, 0:512], AX, ALU.max), inc="dve")
                            W("vector", "dve", v_dm)
                            v_dm = R("vector", lambda e, j=j: e.tensor_max(
                                stat[0:2, 8 + j: 9 + j], stat[0:2, 8 + j: 9 + j],
                                stat[0:2, 10:11]), inc="dve")
                        W("tensor", "dve", v_dm)
                v_dm = R("vector", lambda e: e.tensor_mul(
                    stat[0:2, 12:13], stat[0:2, 8:9], stat[0:2, 9:10]), inc="dve")
                W("scalar", "dve", v_dm)
                v_a = R("scalar", lambda e: e.activation(stat[0:2, 13:14], stat[0:2, 12:13],
                                                         AF.Sqrt), inc="act")
                W("vector", "act", v_a)
                v_dm = R("vector", lambda e: e.tensor_scalar(
                    stat[0:2, 14:15], stat[0:2, 13:14], -SCALE_A, None, ALU.mult), inc="dve")
                W("gpsimd", "dve", v_dm)
                v_bo = kb.dma(scr_small[0:2, 0:1], stat[0:2, 14:15], sem="dmc", queue="gpsimd",
                              slow=True)
                W("gpsimd", "dmc", v_bo)
                v_bi = kb.dma(stat[0:1, 16:18], scr_small[0:2, 0:1].rearrange("a b -> b a"),
                              sem="dmc", queue="gpsimd", slow=True)
                if sub < 3:
                    continue
                for par in range(2):
                    h = hp * 2 + par
                    W("vector", "dmc", v_bi)
                    v_c = R("vector", lambda e, par=par: e.tensor_copy(
                        statb[0:1, 16 + par: 17 + par], stat[0:1, 16 + par: 17 + par]),
                        inc="dve")
                    W("tensor", "dve", v_c)
                    v_pe = R("tensor", lambda e, par=par: mm(
                        e, p1c[:, 0:1], ones_row[0:1, 0:128],
                        statb[0:1, 16 + par: 17 + par], True, True), inc="pe")
                    W("vector", "pe", v_pe)
                    v_bias = R("vector", lambda e: e.tensor_copy(stat[:, 20:21], p1c[:, 0:1]),
                               inc="dve")
                    for i in range(NTT):
                        vp = R("tensor", lambda e, i=i, par=par: e.transpose(
                            ptr[:, 0:64], bt3[64 * par:64 * par + 64, 2 * T + i * 128: 2 * T + (i + 1) * 128],
                            ident_b[64 * par:64 * par + 64, 64 * par:64 * par + 64]), inc="pe")
                        W("vector", "pe", vp)
                        vd = R("vector", lambda e, i=i: e.tensor_copy(
                            svt[:, i * 65: i * 65 + 64], ptr[:, 0:64]), inc="dve")
                        W("tensor", "dve", vd)
                    if sub == 31:
                        continue
                    for ppv in (p1a, p1b, p1c):
                        R("tensor", lambda e, ppv=ppv: mm(
                            e, ppv[:, 0:512], ones_row[0:1, 0:128], zrow[0:1, 0:512],
                            True, True))
                    W("scalar", "dve", v_bias)
                    for i in range(NTT):
                        buf = i % 2
                        v_pe = 0
                        for half, ppp in ((0, p2a), (1, p2b)):
                            for s4 in range(2):
                                v_pe = R("tensor", lambda e, i=i, par=par, half=half, s4=s4, ppp=ppp: mm(
                                    e, ppp[:, s4 * 512:(s4 + 1) * 512],
                                    bt3[64 * par:64 * par + 64, T + i * 128: T + (i + 1) * 128],
                                    bt3[64 * par:64 * par + 64,
                                        half * 1024 + s4 * 512: half * 1024 + s4 * 512 + 512],
                                    True, True), inc="pe")
                        W("scalar", "pe", v_pe)
                        v_act = 0
                        for half, ppp in ((0, p2a), (1, p2b)):
                            v_act = R("scalar", lambda e, i=i, buf=buf, half=half, ppp=ppp: e.activation(
                                bt2[:, buf * 2048 + half * 1024: buf * 2048 + half * 1024 + 1024],
                                ppp[:, 0:1024], AF.Exp, bias=stat[:, 20:21], scale=SCALE_A),
                                inc="act")
                        W("tensor", "act", v_act)
                        if sub == 32:
                            if i == NTT - 1:
                                v_pv = R("tensor", lambda e: e.nop(), inc="pe")
                            continue
                        for tq in range(NTT):
                            bank, sl = divmod(tq, 7)
                            ppv = (p1a, p1b, p1c)[bank]
                            inc = "pe" if (i == NTT - 1 and tq == NTT - 1) else None
                            v = R("tensor", lambda e, i=i, tq=tq, buf=buf, ppv=ppv, sl=sl:
                                  e.matmul(
                                    ppv[:, sl * 68: sl * 68 + 65],
                                    bt2[:, buf * 2048 + tq * 128: buf * 2048 + tq * 128 + 128],
                                    svt[:, i * 65: i * 65 + 65], start=False, stop=False,
                                    skip_group_check=True), inc=inc)
                            if inc:
                                v_pv = v
                    W("vector", "pe", v_pv)
                    for tq in range(0 if sub in (32, 33) else NTT):
                        bank, sl = divmod(tq, 7)
                        ppv = (p1a, p1b, p1c)[bank]
                        if sub == 36:
                            v_d = R("vector", lambda e, ppv=ppv: e.tensor_copy(
                                stat[:, 256:320], ppv[:, 0:64]), inc="dve")
                            W("vector", "dve", v_d)
                            continue
                        if sub in (34, 35):
                            v_d = R("vector", lambda e, ppv=ppv, sl=sl: e.tensor_copy(
                                stat[:, 22:23], ppv[:, sl * 68 + 64: sl * 68 + 65]),
                                inc="dve")
                            W("vector", "dve", v_d)
                            v_d = R("vector", lambda e, ppv=ppv, sl=sl: e.tensor_copy(
                                stat[:, 256:320], ppv[:, sl * 68: sl * 68 + 64]), inc="dve")
                            W("vector", "dve", v_d)
                            if sub == 35:
                                v_d = R("vector", lambda e: e.reciprocal(
                                    stat[:, 21:22], stat[:, 22:23]), inc="dve")
                                W("vector", "dve", v_d)
                            continue
                        v_d = R("vector", lambda e, ppv=ppv, sl=sl: e.tensor_copy(
                            stat[:, 22:23], ppv[:, sl * 68 + 64: sl * 68 + 65]), inc="dve")
                        W("vector", "dve", v_d)
                        v_d = R("vector", lambda e: e.reciprocal(
                            stat[:, 21:22], stat[:, 22:23]), inc="dve")
                        W("vector", "dve", v_d)
                        v_d = R("vector", lambda e, ppv=ppv, sl=sl: e.tensor_copy(
                            stat[:, 256:320], ppv[:, sl * 68: sl * 68 + 64]), inc="dve")
                        W("vector", "dve", v_d)
                        v_d = R("vector", lambda e, tq=tq, h=h: e.tensor_scalar(
                            bt1[:, tq * 1024 + h * 64: tq * 1024 + h * 64 + 64],
                            stat[:, 256:320], stat[:, 21:22], None, ALU.mult),
                            inc="dve")
                        W("vector", "dve", v_d)
                    W("tensor", "dve", kb.cnt["dve"])
            if sub < 6:
                return
            v_last = kb.cnt["dve"]
            W("tensor", "dve", v_last)
            for i in range(NTT):
                for c in range(8):
                    vp = R("tensor", lambda e, i=i, c=c: e.transpose(
                        ptr[:, (c % 2) * 128:(c % 2) * 128 + 128],
                        bt1[:, i * 1024 + c * 128: i * 1024 + (c + 1) * 128], ident_b[:]),
                        inc="pe")
                    W("vector", "pe", vp)
                    vd = R("vector", lambda e, i=i, c=c: e.tensor_copy(
                        x_ln[:, c * T + i * 128: c * T + i * 128 + 128],
                        ptr[:, (c % 2) * 128:(c % 2) * 128 + 128]), inc="dve")
                    W("tensor", "dve", vd)
            for o in range(8):
                W("sync", "dve", kb.cnt["dve"])
                v_l = kb.dma(wstage[:, 0:1024], w_out_a[o * 128:(o + 1) * 128, :], sem="dmb")
                W("vector", "dmb", v_l)
                v_c = R("vector", lambda e: e.tensor_copy(wsmall[:, 0:1024], wstage[:, 0:1024]),
                        inc="dve")
                for c in range(8):
                    W("tensor", "dve", v_c)
                    vp = R("tensor", lambda e, c=c: e.transpose(
                        ptr[:, 0:128], wsmall[:, c * 128:(c + 1) * 128], ident_b[:]), inc="pe")
                    W("vector", "pe", vp)
                    vd = R("vector", lambda e, c=c: e.tensor_copy(
                        wstream[:, c * 128:(c + 1) * 128], ptr[:, 0:128]), inc="dve")
                    W("tensor", "dve", vd)
                for s0 in range(0, T, 512):
                    v_pe = 0
                    for c in range(8):
                        v_pe = R("tensor", lambda e, c=c, s0=s0: mm(
                            e, p2a[:, 0:512], wstream[:, c * 128:(c + 1) * 128],
                            x_ln[:, c * T + s0: c * T + s0 + 512], c == 0, c == 7), inc="pe")
                    W("vector", "pe", v_pe)
                    W("vector", "dve", kb.cnt["dve"])
                    v_d = R("vector", lambda e, o=o, s0=s0: e.scalar_tensor_tensor(
                        x_res[:, o * T + s0: o * T + s0 + 512], p2a[:, 0:512],
                        bcols[:, 0 + o: 1 + o], x_res[:, o * T + s0: o * T + s0 + 512],
                        ALU.add, ALU.add), inc="dve")
                    W("vector", "dve", v_d)
                    W("tensor", "dve", v_d)

        hones = SB("hones", [128, 2], BF16)
        R("vector", lambda e: e.memset(hones[:], 0.0))
        R("vector", lambda e: e.memset(hones[0:64, 0:1], 1.0))
        R("vector", lambda e: e.memset(hones[64:128, 1:2], 1.0))

        if stage >= 2:
            emit_attn_a(stage)

        def prep_ffn_weights():
            prep_bias_cols(b_lin_d, 8, 8)
            W("sync", "dve", kb.cnt["dve"])
            v_l = kb.dma(wstage[0:8, 0:1024], gate_wd, sem="dmb")
            W("vector", "dmb", v_l)
            v_c = R("vector", lambda e: e.tensor_copy(statb[0:8, 0:1024],
                                                      wstage[0:8, 0:1024]), inc="dve")
            for c in range(8):
                W("tensor", "dve", v_c)
                vp = R("tensor", lambda e, c=c: e.transpose(
                    ptr[:, 0:8], statb[0:8, c * 128:(c + 1) * 128],
                    ident_b[0:8, 0:8]), inc="pe")
                W("vector", "pe", vp)
                v_c2 = R("vector", lambda e, c=c: e.tensor_copy(
                    wsmall[:, c * 8:(c + 1) * 8], ptr[:, 0:8]), inc="dve")
                W("tensor", "dve", v_c2)
            W("sync", "dve", kb.cnt["dve"])
            for e_i in range(E_FF):
                for c in range(8):
                    kb.dma(wstage[:, c * 32 + 4 * e_i: c * 32 + 4 * e_i + 4],
                           lora_ld[e_i, c * 128:(c + 1) * 128, :], sem="dmb")
            v_l = kb.cnt["dmb"]
            W("vector", "dmb", v_l)
            v_c = R("vector", lambda e: e.tensor_copy(wsmall[:, 64:320], wstage[:, 0:256]),
                    inc="dve")
            W("sync", "dve", kb.cnt["dve"])
            v_l = kb.dma(wstage[0:32, 0:1024], lora_rd, sem="dmb")
            W("vector", "dmb", v_l)
            v_c = R("vector", lambda e: e.tensor_copy(bt3[0:32, 2048:3072],
                                                      wstage[0:32, 0:1024]), inc="dve")
            return v_c

        def emit_ffn():
            prep_ffn_weights()
            for sl4 in range(4):
                v_pe = 0
                for c in range(8):
                    v_pe = R("tensor", lambda e, c=c, sl4=sl4: mm(
                        e, p1a[0:8, 0:512], wsmall[:, c * 8:(c + 1) * 8],
                        x_ln[:, c * T + sl4 * 512: c * T + sl4 * 512 + 512],
                        c == 0, c == 7), inc="pe")
                W("vector", "pe", v_pe)
                v_d = R("vector", lambda e: e.tensor_copy(stat[0:8, 0:512], p1a[0:8, 0:512]),
                        inc="dve")
                W("tensor", "dve", v_d)
                for st in range(4):
                    i = sl4 * 4 + st
                    vp = R("tensor", lambda e, st=st: e.transpose(
                        p1b[:, 0:8], stat[0:8, st * 128:(st + 1) * 128],
                        ident_f[0:8, 0:8]), inc="pe")
                    W("vector", "pe", vp)
                    v_d = R("vector", lambda e, i=i: e.tensor_copy(
                        ptile[:, i * 16: i * 16 + 8], p1b[:, 0:8]), inc="dve")
                    W("tensor", "dve", v_d)
            for i in range(NTT):
                o0 = i * 16
                v_d = R("vector", lambda e, o0=o0: e.tensor_reduce(
                    ptile[:, o0 + 8: o0 + 9], ptile[:, o0: o0 + 8], AX, ALU.max,
                    negate=True), inc="dve")
                W("scalar", "dve", v_d)
                v_a = R("scalar", lambda e, o0=o0: e.activation(
                    ptile[:, o0: o0 + 8], ptile[:, o0: o0 + 8], AF.Exp,
                    bias=ptile[:, o0 + 8: o0 + 9], scale=1.0,
                    accum_out=ptile[:, o0 + 9: o0 + 10]), inc="act")
                W("vector", "act", v_a)
                v_d = R("vector", lambda e, o0=o0: e.reciprocal(
                    ptile[:, o0 + 10: o0 + 11], ptile[:, o0 + 9: o0 + 10]), inc="dve")
                W("vector", "dve", v_d)
                v_d = R("vector", lambda e, o0=o0: e.tensor_scalar(
                    ptile[:, o0: o0 + 8], ptile[:, o0: o0 + 8],
                    ptile[:, o0 + 10: o0 + 11], None, ALU.mult), inc="dve")
                W("vector", "dve", v_d)
                v_d = R("vector", lambda e, o0=o0: e.tensor_reduce(
                    stat[:, 8:9], ptile[:, o0: o0 + 8], AX, ALU.max), inc="dve")
                W("vector", "dve", v_d)
                v_d = R("vector", lambda e, o0=o0: e.tensor_scalar(
                    stat[:, 0:8], ptile[:, o0: o0 + 8], stat[:, 8:9], None,
                    ALU.is_equal), inc="dve")
                W("vector", "dve", v_d)
                v_d = R("vector", lambda e, o0=o0: e.scalar_tensor_tensor(
                    stat[:, 0:8], stat[:, 0:8], -1e30, ptile[:, o0: o0 + 8],
                    ALU.mult, ALU.add), inc="dve")
                W("vector", "dve", v_d)
                v_d = R("vector", lambda e: e.tensor_reduce(
                    stat[:, 9:10], stat[:, 0:8], AX, ALU.max), inc="dve")
                W("vector", "dve", v_d)
                v_d = R("vector", lambda e, o0=o0: e.tensor_scalar(
                    stat[:, 0:8], ptile[:, o0: o0 + 8], stat[:, 9:10], None,
                    ALU.is_ge), inc="dve")
                W("vector", "dve", v_d)
                v_d = R("vector", lambda e, o0=o0: e.tensor_mul(
                    stat[:, 0:8], stat[:, 0:8], ptile[:, o0: o0 + 8]), inc="dve")
                W("vector", "dve", v_d)
                for r in range(4):
                    v_d = R("vector", lambda e, r=r: e.tensor_copy(
                        statb[:, 32:64].rearrange("p (x r) -> p x r", r=4)[:, :, r],
                        stat[:, 0:8]), inc="dve")
                W("tensor", "dve", v_d)
                vp = R("tensor", lambda e: e.transpose(
                    ptr[0:32, 0:128], statb[:, 32:64], ident_b[:]), inc="pe")
                W("vector", "pe", vp)
                v_d = R("vector", lambda e, i=i: e.tensor_copy(
                    bt2[0:32, 2048 + i * 128: 2048 + (i + 1) * 128], ptr[0:32, 0:128]),
                    inc="dve")
                W("tensor", "dve", v_d)
            for sl4 in range(4):
                v_pe = 0
                for c in range(8):
                    v_pe = R("tensor", lambda e, c=c, sl4=sl4: mm(
                        e, p1b[0:32, 0:512], wsmall[:, 64 + c * 32: 64 + (c + 1) * 32],
                        x_ln[:, c * T + sl4 * 512: c * T + sl4 * 512 + 512],
                        c == 0, c == 7), inc="pe")
                W("vector", "pe", v_pe)
                v_d = R("vector", lambda e, sl4=sl4: e.tensor_copy(
                    bt3[0:32, sl4 * 512:(sl4 + 1) * 512], p1b[0:32, 0:512]), inc="dve")
                W("tensor", "dve", v_d)
            v_th = R("vector", lambda e: e.tensor_mul(
                bt3[0:32, 0:2048], bt3[0:32, 0:2048], bt2[0:32, 2048:4096]), inc="dve")
            W("tensor", "dve", v_th)
            for o in range(8):
                W("sync", "dve", kb.cnt["dve"])
                v_l = kb.dma(wstage[:, 0:1024], W_lin_d[o * 128:(o + 1) * 128, :], sem="dmb")
                W("vector", "dmb", v_l)
                v_c = R("vector", lambda e: e.tensor_copy(wsmall[:, 320:320 + 1024] if False
                                                          else statb[:, 0:1024],
                                                          wstage[:, 0:1024]), inc="dve")
                for c in range(8):
                    W("tensor", "dve", v_c)
                    vp = R("tensor", lambda e, c=c: e.transpose(
                        ptr[:, 0:128], statb[:, c * 128:(c + 1) * 128], ident_b[:]),
                        inc="pe")
                    W("vector", "pe", vp)
                    vd = R("vector", lambda e, c=c: e.tensor_copy(
                        wstream[:, c * 128:(c + 1) * 128], ptr[:, 0:128]), inc="dve")
                    W("tensor", "dve", vd)
                for s0 in range(0, T, 512):
                    v_pe = 0
                    for c in range(8):
                        R("tensor", lambda e, c=c, s0=s0: mm(
                            e, p2a[:, 0:512], wstream[:, c * 128:(c + 1) * 128],
                            x_ln[:, c * T + s0: c * T + s0 + 512], c == 0, False))
                    v_pe = R("tensor", lambda e, o=o, s0=s0: mm(
                        e, p2a[:, 0:512], bt3[0:32, 2048 + o * 128: 2048 + (o + 1) * 128],
                        bt3[0:32, s0: s0 + 512], False, True), inc="pe")
                    W("vector", "pe", v_pe)
                    W("vector", "dve", kb.cnt["dve"])
                    v_d = R("vector", lambda e, o=o, s0=s0: e.scalar_tensor_tensor(
                        x_res[:, o * T + s0: o * T + s0 + 512], p2a[:, 0:512],
                        bcols[:, 8 + o: 9 + o], x_res[:, o * T + s0: o * T + s0 + 512],
                        ALU.add, ALU.add), inc="dve")
                    W("vector", "dve", v_d)
                    W("tensor", "dve", v_d)

        if stage >= 3:
            emit_ln_slab()
            emit_ffn()
        if stage >= 4:
            emit_ln_slab()


        def emit_attn_b(stage=99):
            W("sync", "dve", kb.cnt["dve"])
            v_l = kb.dma(wstage[:, 0:32].rearrange("p (c k) -> p c k", k=4),
                         Wb_d.rearrange("(c p) k -> p c k", p=128), sem="dmb")
            W("vector", "dmb", v_l)
            v_c = R("vector", lambda e: e.tensor_copy(wsmall[:, 0:32], wstage[:, 0:32]),
                    inc="dve")
            W("sync", "dve", kb.cnt["dve"])
            v_l = kb.dma(wstage[:, 32:56].rearrange("p (c k) -> p c k", k=3),
                         fir_sd.rearrange("(c p) k -> p c k", p=128), sem="dmb")
            v_l = kb.dma(wstage[:, 56:560].rearrange("p (c k) -> p c k", k=63),
                         fir_ld.rearrange("(c p) k -> p c k", p=128), sem="dmb")
            W("vector", "dmb", v_l)
            v_c = R("vector", lambda e: e.tensor_copy(wconv[:, 32:560], wstage[:, 32:560]),
                    inc="dve")
            v_c = R("vector", lambda e: e.tensor_copy(wsmall[:, 56:560], wstage[:, 56:560]),
                    inc="dve")
            W("sync", "dve", kb.cnt["dve"])
            v_l = kb.dma(wstage[0:1, 40:44], forget_d, sem="dmb")
            W("vector", "dmb", v_l)
            v_c = R("vector", lambda e: e.tensor_copy(stat[0:1, 40:44], wstage[0:1, 40:44]),
                    inc="dve")
            W("scalar", "dve", v_c)
            v_a = R("scalar", lambda e: e.activation(stat[0:1, 336:340], stat[0:1, 40:44],
                                                     AF.Sigmoid), inc="act")
            W("vector", "act", v_a)
            v_c = R("vector", lambda e: e.tensor_scalar(
                stat[0:1, 336:340], stat[0:1, 336:340], 0.45, 0.55, ALU.mult, ALU.add),
                inc="dve")
            W("vector", "dve", v_c)
            v_c = R("vector", lambda e: e.tensor_copy(statb[0:1, 0:4], stat[0:1, 336:340]),
                    inc="dve")
            W("vector", "dve", v_c)
            v_c = R("vector", lambda e: e.tensor_copy(stat[0:1, 340:344], statb[0:1, 0:4]),
                    inc="dve")
            W("vector", "dve", v_c)
            v_c = R("vector", lambda e: e.tensor_sub(stat[0:1, 344:348], stat[0:1, 336:340],
                                                     stat[0:1, 340:344]), inc="dve")
            W("vector", "dve", v_c)
            v_c = R("vector", lambda e: e.tensor_copy(statb[0:1, 4:8], stat[0:1, 344:348]),
                    inc="dve")
            W("tensor", "dve", v_c)
            for h in range(H_B):
                for which, col in ((0, h), (1, 4 + h)):
                    v_pe = R("tensor", lambda e, col=col: mm(
                        e, p1a[:, 0:1], ones_row[0:1, 0:128], statb[0:1, col:col + 1],
                        True, True), inc="pe")
                    W("vector", "pe", v_pe)
                    v_c = R("vector", lambda e: e.tensor_copy(stat[:, 350:351], p1a[:, 0:1]),
                            inc="dve")
                    W("vector", "dve", v_c)
                    v_c = R("vector", lambda e, h=h, which=which: e.tensor_tensor(
                        wstream[:, 2048 + (h * 2 + which) * 128: 2048 + (h * 2 + which + 1) * 128],
                        ident_b[:], stat[:, 350:351].broadcast_to((128, 128)), ALU.mult),
                        inc="dve")
                    W("tensor", "dve", v_c)
            for s0 in range(0, T, 512):
                v_pe = 0
                for c in range(8):
                    v_pe = R("tensor", lambda e, c=c, s0=s0: mm(
                        e, p1a[0:4, 0:512], wsmall[:, c * 4: (c + 1) * 4],
                        x_ln[:, c * T + s0: c * T + s0 + 512], c == 0, c == 7), inc="pe")
                W("scalar", "pe", v_pe)
                v_a = R("scalar", lambda e, s0=s0: e.activation(
                    statb[0:4, 0:512], p1a[0:4, 0:512], AF.Sigmoid), inc="act")
                W("tensor", "act", v_a)
                for st in range(4):
                    i = s0 // 128 + st
                    vp = R("tensor", lambda e, st=st: e.transpose(
                        ptr[:, 0:4], statb[0:4, st * 128:(st + 1) * 128],
                        ident_b[0:4, 0:4]), inc="pe")
                    W("vector", "pe", vp)
                    v_d = R("vector", lambda e, i=i: e.tensor_copy(
                        svt[:, i * 4:(i + 1) * 4], ptr[0:4, 0:128].rearrange("p f -> f p")
                        if False else ptr[0:128, 0:4] if False else ptr[:, 0:4]),
                        inc="dve")
                    W("tensor", "dve", v_d)

            for h in range(H_B):
                emit_attn_b_head(h, stage)

            if stage >= 6:
                emit_attn_b_gate()

        def load_w_chunks(wd, h, jslot):
            vs = 0
            for c in range(8):
                W("sync", "dve", kb.cnt["dve"])
                v_l = kb.dma(wstage[:, 0:256], wd[c * 128:(c + 1) * 128,
                                                  h * 256:(h + 1) * 256], sem="dmb")
                W("vector", "dmb", v_l)
                vs = R("vector", lambda e, c=c: e.tensor_copy(
                    wstream[:, c * 256:(c + 1) * 256], wstage[:, 0:256]), inc="dve")
            return vs

        def emit_attn_b_head(h, stage=99):
            for j, wd in ((0, Wq_d), (1, Wk_d), (2, Wv_d)):
                v_w = load_w_chunks(wd, h, 0)
                W("tensor", "dve", v_w)
                for mt in range(2):
                    for s0 in range(0, T, 512):
                        v_pe = 0
                        for c in range(8):
                            v_pe = R("tensor", lambda e, c=c, mt=mt, s0=s0: mm(
                                e, p2a[:, 0:512],
                                wstream[:, c * 256 + mt * 128: c * 256 + mt * 128 + 128],
                                x_ln[:, c * T + s0: c * T + s0 + 512], c == 0, c == 7),
                                inc="pe")
                        W("vector", "pe", v_pe)
                        v_d = R("vector", lambda e, mt=mt, s0=s0: e.tensor_copy(
                            bt2[:, mt * 2048 + s0: mt * 2048 + s0 + 512], p2a[:, 0:512]),
                            inc="dve")
                        W("tensor", "dve", v_d)
                for mt in range(2):
                    cc = 2 * h + mt
                    dst0 = j * 4096 + mt * 2048
                    v_d = R("vector", lambda e, mt=mt, cc=cc, dst0=dst0, j=j: e.tensor_scalar(
                        bt3[:, dst0: dst0 + T], bt2[:, mt * 2048: mt * 2048 + T],
                        wsmall[:, 32 + 24 * 0 + 0: 0] if False else
                        conv_w_col(j, cc, 3), None, ALU.mult), inc="dve")
                    W("vector", "dve", v_d)
                    for jj in range(1, 4):
                        v_d = R("vector", lambda e, mt=mt, cc=cc, dst0=dst0, j=j, jj=jj:
                                e.scalar_tensor_tensor(
                                    bt3[:, dst0 + jj: dst0 + T],
                                    bt2[:, mt * 2048: mt * 2048 + T - jj],
                                    conv_w_col(j, cc, 3 - jj),
                                    bt3[:, dst0 + jj: dst0 + T], ALU.mult, ALU.add),
                                inc="dve")
                        W("vector", "dve", v_d)
                    W("scalar", "dve", v_d)
                    v_a = R("scalar", lambda e, dst0=dst0: e.activation(
                        bt3[:, dst0: dst0 + T], bt3[:, dst0: dst0 + T], AF.Silu),
                        inc="act")
                    W("vector", "act", v_a)
                    W("tensor", "act", v_a)
            for i in range(NTT):
                for mt in range(2):
                    vp = R("tensor", lambda e, i=i, mt=mt: e.transpose(
                        ptr[:, 0:128], bt3[:, 8192 + mt * 2048 + i * 128: 8192 + mt * 2048 + (i + 1) * 128],
                        ident_b[:]), inc="pe")
                    W("vector", "pe", vp)
                    v_d = R("vector", lambda e, i=i, mt=mt, h=h: e.tensor_copy(
                        bt1[:, i * 1024 + h * 256 + mt * 128: i * 1024 + h * 256 + (mt + 1) * 128],
                        ptr[:, 0:128]), inc="dve")
                    W("tensor", "dve", v_d)
            for mt in range(2):
                cc = 2 * h + mt
                v_d = R("vector", lambda e, mt=mt, cc=cc: e.tensor_scalar(
                    bt2[:, mt * 2048: mt * 2048 + T], bt3[:, 8192 + mt * 2048: 8192 + mt * 2048 + T],
                    fir_w_col(0, cc, 2), None, ALU.mult), inc="dve")
                W("vector", "dve", v_d)
                for jj in range(1, 3):
                    v_d = R("vector", lambda e, mt=mt, cc=cc, jj=jj: e.scalar_tensor_tensor(
                        bt2[:, mt * 2048 + jj: mt * 2048 + T],
                        bt3[:, 8192 + mt * 2048: 8192 + mt * 2048 + T - jj],
                        fir_w_col(0, cc, 2 - jj),
                        bt2[:, mt * 2048 + jj: mt * 2048 + T], ALU.mult, ALU.add),
                        inc="dve")
                    W("vector", "dve", v_d)
            v_short = kb.cnt["dve"]
            for i in range(NTT):
                for mt in range(2):
                    W("tensor", "dve", v_short)
                    vp = R("tensor", lambda e, i=i, mt=mt: e.transpose(
                        ptr[:, 0:128], bt2[:, mt * 2048 + i * 128: mt * 2048 + (i + 1) * 128],
                        ident_b[:]), inc="pe")
                    W("vector", "pe", vp)
                    v_d = R("vector", lambda e, mt=mt: e.tensor_copy(
                        svt[:, 64 + mt * 128: 64 + (mt + 1) * 128], ptr[:, 0:128]),
                        inc="dve")
                    W("tensor", "dve", v_d)
                emit_stats_tile(0, h, i, svt[:, 64:320])
                W("gpsimd", "dve", kb.cnt["dve"])
                v_sp = kb.dma(scr_short_b[i * 128:(i + 1) * 128, h * 256:(h + 1) * 256],
                              svt[:, 64:320], sem="dmc", queue="gpsimd")
                W("vector", "dmc", v_sp)
            emit_fir_long(h)
            if stage >= 5:
                emit_delta_head(h)
            for i in range(NTT):
                emit_stats_tile(3, h, i, bt1[:, i * 1024 + h * 256: i * 1024 + (h + 1) * 256])

        def conv_w_col(j, cc, k):
            return wconv[:, 560 + j * 32 + cc * 4 + k: 560 + j * 32 + cc * 4 + k + 1]

        def fir_w_col(which, cc, k):
            if which == 0:
                return wconv[:, 32 + cc * 3 + k: 32 + cc * 3 + k + 1]
            return wconv[:, 56 + cc * 63 + k: 56 + cc * 63 + k + 1]

        def emit_stats_tile(branch, h, i, src_ap):
            base = i * 64 + branch * 16 + h * 4
            W("vector", "dve", kb.cnt["dve"])
            v_sum = R("vector", lambda e: e.tensor_reduce(stat[:, 352:353], src_ap, AX,
                                                          ALU.add), inc="dve")
            v_sq = R("vector", lambda e: e.tensor_tensor(stat[:, 0:256], src_ap, src_ap,
                                                         ALU.mult), inc="dve")
            W("vector", "dve", v_sq)
            v_ssq = R("vector", lambda e: e.tensor_reduce(stat[:, 353:354], stat[:, 0:256],
                                                          AX, ALU.add), inc="dve")
            R("vector", lambda e: e.tensor_reduce(stat[:, 354:355], src_ap, AX, ALU.max))
            W("vector", "dve", v_sum)
            R("vector", lambda e, base=base: e.tensor_scalar(
                stats_tok[:, base: base + 1], stat[:, 352:353], 1.0 / DV, None, ALU.mult))
            W("vector", "dve", v_ssq)
            v_d = R("vector", lambda e: e.tensor_scalar(stat[:, 356:357], stat[:, 352:353],
                                                        1.0 / DV, None, ALU.mult), inc="dve")
            W("vector", "dve", v_d)
            v_d = R("vector", lambda e: e.tensor_mul(stat[:, 356:357], stat[:, 356:357],
                                                     stat[:, 356:357]), inc="dve")
            W("vector", "dve", v_d)
            v_d = R("vector", lambda e: e.scalar_tensor_tensor(
                stat[:, 357:358], stat[:, 353:354], 1.0 / DV, stat[:, 356:357],
                ALU.mult, ALU.subtract), inc="dve")
            W("vector", "dve", v_d)
            R("vector", lambda e, base=base: e.tensor_copy(stats_tok[:, base + 1: base + 2],
                                                           stat[:, 357:358]))
            R("vector", lambda e, base=base: e.tensor_copy(stats_tok[:, base + 2: base + 3],
                                                           stat[:, 354:355]))
            W("scalar", "dve", v_ssq)
            v_a = R("scalar", lambda e: e.activation(stat[:, 355:356], stat[:, 353:354],
                                                     AF.Sqrt), inc="act")
            W("vector", "act", v_a)
            v_d = R("vector", lambda e, base=base: e.tensor_copy(
                stats_tok[:, base + 3: base + 4], stat[:, 355:356]), inc="dve")
            return v_d

        def emit_fir_long(h):
            for mt in range(2):
                cc = 2 * h + mt
                vp = R("tensor", lambda e, cc=cc: e.transpose(
                    ptr[0:63, 0:128], wsmall[:, 56 + cc * 63: 56 + (cc + 1) * 63],
                    ident_b[:]), inc="pe")
                W("vector", "pe", vp)
                v_d = R("vector", lambda e, mt=mt: e.tensor_copy(
                    statb[0:63, mt * 128:(mt + 1) * 128], ptr[0:63, 0:128]), inc="dve")
                W("tensor", "dve", v_d)
            W("sync", "dve", kb.cnt["dve"])
            kb.dma(wstage[0:63, 0:128], din["c_dft_hA"], sem="dmb")
            v_l = kb.dma(wstage[0:63, 128:256], din["c_dft_hB"], sem="dmb")
            W("vector", "dmb", v_l)
            v_c = R("vector", lambda e: e.tensor_copy(statb[0:63, 256:512],
                                                      wstage[0:63, 0:256]), inc="dve")
            W("tensor", "dve", v_c)
            v_pe = R("tensor", lambda e: mm(e, p1a[:, 0:256], statb[0:63, 256:384],
                                            statb[0:63, 0:256], True, True), inc="pe")
            v_pe = R("tensor", lambda e: mm(e, p1b[:, 0:256], statb[0:63, 384:512],
                                            statb[0:63, 0:256], True, True), inc="pe")
            W("vector", "pe", v_pe)
            v_d = R("vector", lambda e: e.tensor_copy(bt2[:, 0:256], p1a[:, 0:256]), inc="dve")
            v_d = R("vector", lambda e: e.tensor_scalar(bt2[:, 256:512], p1b[:, 0:256],
                                                        -1.0, None, ALU.mult), inc="dve")
            v_d = R("vector", lambda e: e.tensor_copy(bt2[:, 512:768], p1b[:, 0:256]), inc="dve")
            v_d = R("vector", lambda e: e.tensor_copy(bt2[:, 768:1024], p1a[:, 0:256]), inc="dve")
            W("vector", "dve", v_d)
            v_d = R("vector", lambda e: e.memset(bt2[0:1, 256:512], 0.0), inc="dve")
            v_d = R("vector", lambda e: e.memset(bt2[0:1, 512:768], 0.0), inc="dve")
            v_d = R("vector", lambda e: e.tensor_copy(bt2[0:1, 768:1024], p1b[0:1, 0:256]),
                    inc="dve")
            W("tensor", "dve", v_d)
            W("sync", "dve", kb.cnt["dve"])
            kb.dma(wstage[:, 0:128], din["c_dft_fwdA"][0:128, :], sem="dmb")
            kb.dma(wstage[:, 128:256], din["c_dft_fwdA"][128:256, :], sem="dmb")
            kb.dma(wstage[:, 256:384], din["c_dft_fwdB"][0:128, :], sem="dmb")
            v_l = kb.dma(wstage[:, 384:512], din["c_dft_fwdB"][128:256, :], sem="dmb")
            W("vector", "dmb", v_l)
            v_c = R("vector", lambda e: e.tensor_copy(bt2[:, 1024:1536], wstage[:, 0:512]),
                    inc="dve")
            W("sync", "dve", kb.cnt["dve"])
            kb.dma(wstage[:, 0:128], din["c_dft_invA"], sem="dmb")
            v_l = kb.dma(wstage[:, 128:256], din["c_dft_invB"], sem="dmb")
            W("vector", "dmb", v_l)
            v_c = R("vector", lambda e: e.tensor_copy(bt2[:, 1536:1792], wstage[:, 0:256]),
                    inc="dve")
            W("tensor", "dve", v_c)
            for i in range(NTT):
                v_pe = 0
                for (dst, c0) in ((p1a, 1024), (p1b, 1280)):
                    for kk in range(2):
                        src_tile = i - 1 + kk
                        if src_tile < 0:
                            continue
                        first = (kk == 0) if i > 0 else True
                        last = (kk == 1)
                        v_pe = R("tensor", lambda e, dst=dst, c0=c0, kk=kk, src_tile=src_tile,
                                 first=first, last=last, h=h: mm(
                                     e, dst[:, 0:256], bt2[:, c0 + kk * 128: c0 + (kk + 1) * 128],
                                     bt1[:, src_tile * 1024 + h * 256: src_tile * 1024 + (h + 1) * 256],
                                     first, last), inc="pe" if last else None)
                W("vector", "pe", v_pe)
                v_d = R("vector", lambda e: e.tensor_copy(bt2[:, 1792:2048], p1a[:, 0:256]),
                        inc="dve")
                v_d = R("vector", lambda e: e.tensor_copy(svt[:, 320:576], p1b[:, 0:256]),
                        inc="dve")
                W("vector", "dve", v_d)
                v_d = R("vector", lambda e: e.tensor_tensor(
                    stat[:, 0:256], bt2[:, 1792:2048], bt2[:, 0:256], ALU.mult), inc="dve")
                v_d = R("vector", lambda e: e.tensor_tensor(
                    stat[:, 256:512], svt[:, 320:576], bt2[:, 256:512], ALU.mult), inc="dve")
                W("vector", "dve", v_d)
                v_d = R("vector", lambda e: e.tensor_add(stat[:, 0:256], stat[:, 0:256],
                                                         stat[:, 256:512]), inc="dve")
                W("vector", "dve", v_d)
                v_d = R("vector", lambda e: e.tensor_copy(svt[:, 576:832], stat[:, 0:256]),
                        inc="dve")
                v_d = R("vector", lambda e: e.tensor_tensor(
                    stat[:, 0:256], bt2[:, 1792:2048], bt2[:, 512:768], ALU.mult), inc="dve")
                v_d = R("vector", lambda e: e.tensor_tensor(
                    stat[:, 256:512], svt[:, 320:576], bt2[:, 768:1024], ALU.mult), inc="dve")
                W("vector", "dve", v_d)
                v_d = R("vector", lambda e: e.tensor_add(stat[:, 0:256], stat[:, 0:256],
                                                         stat[:, 256:512]), inc="dve")
                W("vector", "dve", v_d)
                v_d = R("vector", lambda e: e.tensor_copy(svt[:, 832:1040] if False else
                                                          svt[:, 832:1040],
                                                          stat[:, 0:208]) if False else
                        e.tensor_copy(bt2[:, 1792:2048], stat[:, 0:256]), inc="dve")
                W("tensor", "dve", v_d)
                R("tensor", lambda e: mm(e, p1c[:, 0:256], bt2[:, 1536:1664],
                                         svt[:, 576:832], True, False))
                v_pe = R("tensor", lambda e: mm(e, p1c[:, 0:256], bt2[:, 1664:1792],
                                                bt2[:, 1792:2048], False, True), inc="pe")
                W("vector", "pe", v_pe)
                v_d = R("vector", lambda e: e.tensor_copy(svt[:, 320:576], p1c[:, 0:256]),
                        inc="dve")
                W("tensor", "dve", v_d)
                emit_stats_tile(1, h, i, svt[:, 320:576])
                W("gpsimd", "dve", kb.cnt["dve"])
                v_sp = kb.dma(scr_long_b[i * 128:(i + 1) * 128, h * 256:(h + 1) * 256],
                              svt[:, 320:576], sem="dmc", queue="gpsimd")
                W("vector", "dmc", v_sp)

        def emit_delta_head(h):
            QT, KT, QN, KN, KBF, KBT, VB, TT0, XP, XL, QB0, ABD, ACM, WFM, QPF, UCMP = (
                0, 256, 512, 768, 1024, 1280, 1536, 1792, 1920, 2048, 2176, 2304,
                2432, 2560, 2816, 3072)
            v_z = R("vector", lambda e: e.memset(Sst[:], 0.0), inc="dve")
            v_z = R("vector", lambda e: e.memset(svt[:, 64:576], 0.0), inc="dve")
            W("gpsimd", "dve", v_z)
            W("tensor", "dve", v_z)
            for g in range(NGRP):
                t0 = g * 128
                for (src0, dst) in ((0, QT), (4096, KT)):
                    for mt in range(2):
                        W("tensor", "dve", kb.cnt["dve"])
                        vp = R("tensor", lambda e, src0=src0, mt=mt, t0=t0: e.transpose(
                            ptr[:, 0:128], bt3[:, src0 + mt * 2048 + t0: src0 + mt * 2048 + t0 + 128],
                            ident_b[:]), inc="pe")
                        W("vector", "pe", vp)
                        v_d = R("vector", lambda e, dst=dst, mt=mt: e.tensor_copy(
                            bt2[:, dst + mt * 128: dst + (mt + 1) * 128], ptr[:, 0:128]),
                            inc="dve")
                        W("tensor", "dve", v_d)
                for (src, col) in ((QT, 360), (KT, 361)):
                    v_d = R("vector", lambda e, src=src: e.tensor_tensor(
                        stat[:, 0:256], bt2[:, src: src + 256], bt2[:, src: src + 256],
                        ALU.mult), inc="dve")
                    W("vector", "dve", v_d)
                    v_d = R("vector", lambda e: e.tensor_reduce(
                        stat[:, 364:365], stat[:, 0:256], AX, ALU.add), inc="dve")
                    W("vector", "dve", v_d)
                    v_d = R("vector", lambda e: e.tensor_scalar(
                        stat[:, 364:365], stat[:, 364:365], 1e-6, None, ALU.add), inc="dve")
                    W("scalar", "dve", v_d)
                    v_a = R("scalar", lambda e: e.activation(
                        stat[:, 365:366], stat[:, 364:365], AF.Sqrt), inc="act")
                    W("vector", "act", v_a)
                    v_d = R("vector", lambda e, col=col: e.reciprocal(
                        stat[:, col: col + 1], stat[:, 365:366]), inc="dve")
                    W("vector", "dve", v_d)
                v_d = R("vector", lambda e, g=g, h=h: e.tensor_tensor(
                    stat[:, 362:363], stat[:, 361:362], svt[:, g * 4 + h: g * 4 + h + 1],
                    ALU.mult), inc="dve")
                W("vector", "dve", v_d)
                for (srctok, col, dst) in ((QT, 360, QN), (KT, 361, KN), (KT, 362, KBF)):
                    v_d = R("vector", lambda e, col=col: e.tensor_tensor(
                        statb[:, 0:128], ident_b[:],
                        stat[:, col: col + 1].broadcast_to((128, 128)), ALU.mult),
                        inc="dve")
                    W("tensor", "dve", v_d)
                    for mt in range(2):
                        v_pe = R("tensor", lambda e, srctok=srctok, mt=mt: mm(
                            e, p1a[:, 0:128], bt2[:, srctok + mt * 128: srctok + (mt + 1) * 128],
                            statb[:, 0:128], True, True), inc="pe")
                        W("vector", "pe", v_pe)
                        v_d = R("vector", lambda e, dst=dst, mt=mt: e.tensor_copy(
                            bt2[:, dst + mt * 128: dst + (mt + 1) * 128], p1a[:, 0:128]),
                            inc="dve")
                        W("tensor", "dve", v_d)
                v_d = R("vector", lambda e: e.tensor_tensor(
                    bt2[:, KBT: KBT + 256], bt2[:, KT: KT + 256],
                    stat[:, 362:363].broadcast_to((128, 256)), ALU.mult), inc="dve")
                v_d = R("vector", lambda e, g=g, h=h: e.tensor_tensor(
                    bt2[:, VB: VB + 256], bt1[:, g * 1024 + h * 256: g * 1024 + (h + 1) * 256],
                    svt[:, g * 4 + h: g * 4 + h + 1].broadcast_to((128, 256)), ALU.mult),
                    inc="dve")
                W("tensor", "dve", v_d)
                for mt in range(2):
                    R("tensor", lambda e, mt=mt: mm(
                        e, p1a[:, 0:128], bt2[:, KN + mt * 128: KN + (mt + 1) * 128],
                        bt2[:, KBF + mt * 128: KBF + (mt + 1) * 128], mt == 0, mt == 1))
                v_pe = 0
                for mt in range(2):
                    v_pe = R("tensor", lambda e, mt=mt: mm(
                        e, p1b[:, 0:128], bt2[:, KBF + mt * 128: KBF + (mt + 1) * 128],
                        bt2[:, KN + mt * 128: KN + (mt + 1) * 128], mt == 0, mt == 1),
                        inc="pe")
                W("vector", "pe", v_pe)
                v_d = R("vector", lambda e: e.tensor_tensor(
                    bt2[:, XP: XP + 128], p1a[:, 0:128], masks[:, 128:256], ALU.mult),
                    inc="dve")
                v_d = R("vector", lambda e: e.tensor_tensor(
                    bt2[:, XL: XL + 128], p1b[:, 0:128], masks[:, 0:128], ALU.mult),
                    inc="dve")
                v_d = R("vector", lambda e: e.tensor_add(
                    bt2[:, TT0: TT0 + 128], bt2[:, XP: XP + 128], ident_b[:]),
                    inc="dve")
                W("tensor", "dve", v_d)
                for lvl in range(4):
                    last = (lvl == 3)
                    v_pe = R("tensor", lambda e: mm(e, p1a[:, 0:128], bt2[:, XL: XL + 128],
                                                    bt2[:, XP: XP + 128], True, True),
                             inc="pe")
                    v_pe = R("tensor", lambda e: mm(e, p1b[:, 0:128], bt2[:, XP: XP + 128],
                                                    bt2[:, XL: XL + 128], True, True),
                             inc="pe")
                    W("vector", "pe", v_pe)
                    v_d = R("vector", lambda e: e.tensor_copy(bt2[:, XP: XP + 128],
                                                              p1a[:, 0:128]), inc="dve")
                    v_d = R("vector", lambda e: e.tensor_copy(bt2[:, XL: XL + 128],
                                                              p1b[:, 0:128]), inc="dve")
                    W("vector", "dve", v_d)
                    v_d = R("vector", lambda e: e.tensor_add(
                        bt2[:, QB0: QB0 + 128], bt2[:, XL: XL + 128], ident_b[:]),
                        inc="dve")
                    W("tensor", "dve", v_d)
                    v_pe = R("tensor", lambda e: mm(e, p1a[:, 0:128], bt2[:, QB0: QB0 + 128],
                                                    bt2[:, TT0: TT0 + 128], True, True),
                             inc="pe")
                    W("vector", "pe", v_pe)
                    v_d = R("vector", lambda e: e.tensor_copy(bt2[:, TT0: TT0 + 128],
                                                              p1a[:, 0:128]), inc="dve")
                    W("tensor", "dve", v_d)
                for ci in range(4):
                    v_pe = R("tensor", lambda e, ci=ci: mm(
                        e, p1a[0:32, 0:256], bt2[:, TT0 + ci * 32: TT0 + (ci + 1) * 32],
                        bt2[:, VB: VB + 256], True, True), inc="pe")
                    W("vector", "pe", v_pe)
                    v_d = R("vector", lambda e, ci=ci: e.tensor_copy(
                        bt2[0:32, UCMP + ci * 256: UCMP + (ci + 1) * 256], p1a[0:32, 0:256]),
                        inc="dve")
                    W("tensor", "dve", v_d)
                for mt in range(2):
                    v_pe = R("tensor", lambda e, mt=mt: mm(
                        e, p1b[:, 0:128], bt2[:, KBT + mt * 128: KBT + (mt + 1) * 128],
                        bt2[:, TT0: TT0 + 128], True, True), inc="pe")
                    W("vector", "pe", v_pe)
                    v_d = R("vector", lambda e, mt=mt: e.tensor_copy(
                        bt2[:, WFM + mt * 128: WFM + (mt + 1) * 128], p1b[:, 0:128]),
                        inc="dve")
                    W("tensor", "dve", v_d)
                for ci in range(4):
                    v_pe = 0
                    for mt in range(2):
                        v_pe = R("tensor", lambda e, ci=ci, mt=mt: mm(
                            e, p1b[0:32, 0:32],
                            bt2[:, KN + mt * 128 + ci * 32: KN + mt * 128 + ci * 32 + 32],
                            bt2[:, QN + mt * 128 + ci * 32: QN + mt * 128 + ci * 32 + 32],
                            mt == 0, mt == 1), inc="pe" if mt == 1 else None)
                    W("vector", "pe", v_pe)
                    v_d = R("vector", lambda e, ci=ci: e.tensor_tensor(
                        bt2[0:32, ACM + ci * 32: ACM + (ci + 1) * 32], p1b[0:32, 0:32],
                        masks[0:32, 384:416], ALU.mult), inc="dve")
                    W("tensor", "dve", v_d)
                for ci in range(4):
                    oi = ci * 32
                    for mt in range(2):
                        vp = R("tensor", lambda e, mt=mt, oi=oi: e.transpose(
                            ptr[0:32, 0:128], bt2[:, KN + mt * 128 + oi: KN + mt * 128 + oi + 32],
                            ident_b[:]), inc="pe")
                        W("vector", "pe", vp)
                        v_d = R("vector", lambda e, mt=mt: e.tensor_copy(
                            statb[0:32, 384 + mt * 128: 384 + (mt + 1) * 128],
                            ptr[0:32, 0:128]), inc="dve")
                        W("tensor", "dve", v_d)
                    v_pe = 0
                    for mt in range(2):
                        v_pe = R("tensor", lambda e, mt=mt, oi=oi: mm(
                            e, p1a[0:32, 0:256],
                            bt2[:, WFM + mt * 128 + oi: WFM + mt * 128 + oi + 32],
                            svt[:, 64 + mt * 256: 64 + (mt + 1) * 256], mt == 0, mt == 1),
                            inc="pe" if mt == 1 else None)
                    W("vector", "pe", v_pe)
                    v_d = R("vector", lambda e, ci=ci: e.tensor_sub(
                        statb[0:32, 640:896],
                        bt2[0:32, UCMP + ci * 256: UCMP + (ci + 1) * 256],
                        p1a[0:32, 0:256]), inc="dve")
                    W("tensor", "dve", v_d)
                    for mt in range(2):
                        R("tensor", lambda e, mt=mt, oi=oi: mm(
                            e, p1b[0:32, 0:256],
                            bt2[:, QN + mt * 128 + oi: QN + mt * 128 + oi + 32],
                            svt[:, 64 + mt * 256: 64 + (mt + 1) * 256], mt == 0, False))
                    v_pe = R("tensor", lambda e, ci=ci: mm(
                        e, p1b[0:32, 0:256], bt2[0:32, ACM + ci * 32: ACM + (ci + 1) * 32],
                        statb[0:32, 640:896], False, True), inc="pe")
                    W("vector", "pe", v_pe)
                    v_d = R("vector", lambda e: e.tensor_copy(
                        bt2[0:32, XP: XP + 256], p1b[0:32, 0:256]), inc="dve")
                    W("tensor", "dve", v_d)
                    for mt in range(2):
                        vp = R("tensor", lambda e, mt=mt: e.transpose(
                            ptr[:, 0:32], bt2[0:32, XP + mt * 128: XP + (mt + 1) * 128],
                            ident_b[0:32, 0:32]), inc="pe")
                        W("vector", "pe", vp)
                        v_d = R("vector", lambda e, mt=mt, t0=t0, oi=oi: e.tensor_copy(
                            bt3[:, 12288 + mt * 2048 + t0 + oi: 12288 + mt * 2048 + t0 + oi + 32],
                            ptr[:, 0:32]), inc="dve")
                        W("tensor", "dve", v_d)
                    v_pe = 0
                    for mt, pp in ((0, p2a), (1, p2b)):
                        R("tensor", lambda e, mt=mt, pp=pp, h=h: mm(
                            e, pp[:, 0:256], wstream[:, 2048 + (h * 2) * 128: 2048 + (h * 2 + 1) * 128],
                            svt[:, 64 + mt * 256: 64 + (mt + 1) * 256], True, False))
                        R("tensor", lambda e, mt=mt, pp=pp, h=h: mm(
                            e, pp[:, 0:256], wstream[:, 2048 + (h * 2 + 1) * 128: 2048 + (h * 2 + 2) * 128],
                            svt[:, 64 + mt * 256: 64 + (mt + 1) * 256], False, False))
                        v_pe = R("tensor", lambda e, mt=mt, pp=pp: mm(
                            e, pp[:, 0:256], statb[0:32, 384 + mt * 128: 384 + (mt + 1) * 128],
                            statb[0:32, 640:896], False, True), inc="pe" if mt == 1 else None)
                    W("scalar", "pe", v_pe)
                    v_a = 0
                    for mt, pp in ((0, p2a), (1, p2b)):
                        v_a = R("scalar", lambda e, mt=mt, pp=pp: e.activation(
                            Sst[:, mt * 256:(mt + 1) * 256], pp[:, 0:256], AF.Copy),
                            inc="act")
                    W("gpsimd", "act", v_a)
                    v_p = 0
                    for mt in range(2):
                        v_p = R("gpsimd", lambda e, mt=mt: e.tensor_copy(
                            svt[:, 64 + mt * 256: 64 + (mt + 1) * 256],
                            Sst[:, mt * 256:(mt + 1) * 256]), inc="pool")
                    W("tensor", "pool", v_p)
                    W("tensor", "act", v_a)
            for i in range(NTT):
                for mt in range(2):
                    W("tensor", "dve", kb.cnt["dve"])
                    vp = R("tensor", lambda e, i=i, mt=mt: e.transpose(
                        ptr[:, 0:128], bt3[:, 12288 + mt * 2048 + i * 128: 12288 + mt * 2048 + (i + 1) * 128],
                        ident_b[:]), inc="pe")
                    W("vector", "pe", vp)
                    v_d = R("vector", lambda e, mt=mt: e.tensor_copy(
                        svt[:, 576 + mt * 128: 576 + (mt + 1) * 128], ptr[:, 0:128]),
                        inc="dve")
                    W("tensor", "dve", v_d)
                emit_stats_tile(2, h, i, svt[:, 576:832])
                W("gpsimd", "dve", kb.cnt["dve"])
                v_sp = kb.dma(scr_delta_b[i * 128:(i + 1) * 128, h * 256:(h + 1) * 256],
                              svt[:, 576:832], sem="dmc", queue="gpsimd")
                W("vector", "dmc", v_sp)

        def emit_attn_b_gate():
            W("tensor", "dve", kb.cnt["dve"])
            W("vector", "pe", kb.cnt["pe"])
            W("vector", "act", kb.cnt["act"])
            W("vector", "pool", kb.cnt["pool"])
            for i in range(NTT):
                vp = R("tensor", lambda e, i=i: e.transpose(
                    ptr[0:64, 0:128], stats_tok[:, i * 64:(i + 1) * 64], ident_b[:]),
                    inc="pe")
                W("vector", "pe", vp)
                v_d = R("vector", lambda e, i=i: e.tensor_copy(
                    bt2[0:64, i * 128:(i + 1) * 128], ptr[0:64, 0:128]), inc="dve")
                W("tensor", "dve", v_d)
            W("gpsimd", "dve", kb.cnt["dve"])
            v_sp = kb.dma(scr_dbg3, bt2[0:64, 0:2048], sem="dmc", queue="gpsimd")
            W("vector", "dmc", v_sp)
            W("sync", "dve", kb.cnt["dve"])
            kb.dma(wstage[0:1, 0:4], log_temp_d, sem="dmb")
            kb.dma(wstage[0:1, 16:32], floor_d, sem="dmb")
            v_l = kb.dma(wstage[0:1, 32:48], b2_d, sem="dmb")
            W("vector", "dmb", v_l)
            W("scalar", "dmb", v_l)
            v_a = R("scalar", lambda e: e.activation(stat[0:1, 368:372], wstage[0:1, 0:4],
                                                     AF.Exp, scale=-1.0), inc="act")
            v_a = R("scalar", lambda e: e.activation(stat[0:1, 372:388], wstage[0:1, 16:32],
                                                     AF.Sigmoid), inc="act")
            W("vector", "act", v_a)
            v_d = R("vector", lambda e: e.tensor_scalar(
                stat[0:1, 372:388], stat[0:1, 372:388], MAX_FLOOR, None, ALU.mult),
                inc="dve")
            for j4 in range(4):
                v_d = R("vector", lambda e, j4=j4: e.tensor_copy(
                    stat[0:1, 388:404].rearrange("p (x r) -> p x r", r=4)[:, :, j4],
                    stat[0:1, 368:372]), inc="dve")
            W("vector", "dve", v_d)
            v_d = R("vector", lambda e: e.tensor_copy(statb[0:1, 0:16], stat[0:1, 388:404]),
                    inc="dve")
            v_d = R("vector", lambda e: e.tensor_copy(statb[0:1, 16:32], stat[0:1, 372:388]),
                    inc="dve")
            W("tensor", "dve", v_d)
            v_pe = R("tensor", lambda e: mm(e, p1c[:, 0:16], ones_row[0:1, 0:128],
                                            statb[0:1, 0:16], True, True), inc="pe")
            W("vector", "pe", v_pe)
            v_d = R("vector", lambda e: e.tensor_copy(wconv[:, 0:16], p1c[:, 0:16]),
                    inc="dve")
            W("tensor", "dve", v_d)
            v_pe = R("tensor", lambda e: mm(e, p1c[:, 0:16], ones_row[0:1, 0:128],
                                            statb[0:1, 16:32], True, True), inc="pe")
            W("vector", "pe", v_pe)
            v_d = R("vector", lambda e: e.tensor_copy(wconv[:, 16:32], p1c[:, 0:16]),
                    inc="dve")
            W("tensor", "dmb", kb.cnt["dmb"])
            vp = R("tensor", lambda e: e.transpose(p1c[0:16, 0:1], wstage[0:1, 32:48],
                                                   ident_f[0:1, 0:1]), inc="pe")
            W("vector", "pe", vp)
            v_d = R("vector", lambda e: e.tensor_copy(wconv[0:16, 656:657], p1c[0:16, 0:1]),
                    inc="dve")
            W("tensor", "dve", v_d)
            for hb in range(2):
                W("sync", "dve", kb.cnt["dve"])
                v_l = kb.dma(wstage[0:1, 0:1024], b1_d[:, hb * 1024:(hb + 1) * 1024],
                             sem="dmb")
                W("vector", "dmb", v_l)
                v_c = R("vector", lambda e: e.tensor_copy(statb[0:1, 0:1024],
                                                          wstage[0:1, 0:1024]), inc="dve")
                for c8 in range(8):
                    W("tensor", "dve", v_c)
                    vp = R("tensor", lambda e, c8=c8: e.transpose(
                        ptr[:, 0:1], statb[0:1, c8 * 128:(c8 + 1) * 128],
                        ident_b[0:1, 0:1]), inc="pe")
                    W("vector", "pe", vp)
                    v_c = R("vector", lambda e, hb=hb, c8=c8: e.tensor_copy(
                        b1cols[:, hb * 8 + c8: hb * 8 + c8 + 1], ptr[:, 0:1]), inc="dve")
                    W("tensor", "dve", v_c)
            for half in range(2):
                w2_pe = [0, 0]
                for m in range(16):
                    W("vector", "pe", w2_pe[1])
                    W("sync", "dve", kb.cnt["dve"])
                    for kc in range(9):
                        rows = 128 if kc < 8 else 64
                        kb.dma(wstage[0:rows, 0:128] if kc == 0 else wstage[0:rows, 0:128],
                               W1_d[kc * 128: kc * 128 + rows, m * 128:(m + 1) * 128],
                               sem="dmb")
                        W("vector", "dmb", kb.cnt["dmb"])
                        v_c = R("vector", lambda e, kc=kc, rows=rows: e.tensor_copy(
                            wstream[0:rows, 3072 + kc * 128: 3072 + (kc + 1) * 128],
                            wstage[0:rows, 0:128]), inc="dve")
                        W("sync", "dve", kb.cnt["dve"])
                    kb.dma(wstage[:, 128:144], W2_d[m * 128:(m + 1) * 128, :], sem="dmb")
                    W("vector", "dmb", kb.cnt["dmb"])
                    v_c = R("vector", lambda e: e.tensor_copy(
                        wstream[:, 4224:4240], wstage[:, 128:144]), inc="dve")
                    W("tensor", "dve", v_c)
                    for sl2 in range(2):
                        sl = half * 2 + sl2
                        s0 = sl * 512
                        pp = p2a if sl2 == 0 else p2b
                        for kc in range(8):
                            R("tensor", lambda e, kc=kc, m=m, s0=s0, pp=pp: mm(
                                e, pp[:, 0:512], wstream[:, 3072 + kc * 128: 3072 + (kc + 1) * 128],
                                x_ln[:, kc * T + s0: kc * T + s0 + 512], kc == 0, False))
                        v_pe = R("tensor", lambda e, m=m, s0=s0, pp=pp: mm(
                            e, pp[:, 0:512], wstream[0:64, 4096:4224],
                            bt2[0:64, s0: s0 + 512], False, True), inc="pe")
                        W("scalar", "pe", v_pe)
                        W("scalar", "pe", w2_pe[sl2])
                        v_a = R("scalar", lambda e, m=m, sl2=sl2, pp=pp: e.activation(
                            wsmall[:, sl2 * 512:(sl2 + 1) * 512],
                            pp[:, 0:512], AF.Gelu, bias=b1cols[:, m: m + 1]), inc="act")
                        W("tensor", "act", v_a)
                        ppl = p1a if sl2 == 0 else p1b
                        v_pe = R("tensor", lambda e, m=m, sl2=sl2, ppl=ppl: mm(
                            e, ppl[0:16, 0:512], wstream[:, 4224:4240],
                            wsmall[:, sl2 * 512:(sl2 + 1) * 512], m == 0, m == 15),
                            inc="pe")
                        w2_pe[sl2] = v_pe
                        W("vector", "pe", v_pe) if m == 15 else None
                for sl2 in range(2):
                    sl = half * 2 + sl2
                    ppl = p1a if sl2 == 0 else p1b
                    W("vector", "dve", kb.cnt["dve"])
                    v_d = R("vector", lambda e, ppl=ppl: e.scalar_tensor_tensor(
                        stat[0:16, 0:512], ppl[0:16, 0:512], 1.0,
                        wconv[0:16, 656:657].broadcast_to((16, 512)), ALU.mult, ALU.add),
                        inc="dve")
                    W("tensor", "dve", v_d)
                    for st in range(4):
                        i = sl * 4 + st
                        vp = R("tensor", lambda e, st=st, ppl=ppl: e.transpose(
                            p1c[:, 0:16], stat[0:16, st * 128:(st + 1) * 128],
                            ident_f[0:16, 0:16]), inc="pe")
                        W("vector", "pe", vp)
                        v_d = R("vector", lambda e, i=i: e.tensor_copy(
                            ptile[:, i * 16:(i + 1) * 16], p1c[:, 0:16]), inc="dve")
                        W("tensor", "dve", v_d)
            W("gpsimd", "dve", kb.cnt["dve"])
            v_sp = kb.dma(scr_dbg, ptile[:], sem="dmc", queue="gpsimd")
            W("vector", "dmc", v_sp)
            for i in range(NTT):
                o0 = i * 16
                v_d = R("vector", lambda e, o0=o0: e.tensor_mul(
                    ptile[:, o0: o0 + 16], ptile[:, o0: o0 + 16], wconv[:, 0:16]),
                    inc="dve")
                W("vector", "dve", v_d)
                for hh in range(4):
                    c0 = o0 + hh * 4
                    v_d = R("vector", lambda e, c0=c0: e.tensor_reduce(
                        stat[:, 400:401], ptile[:, c0: c0 + 4], AX, ALU.max, negate=True),
                        inc="dve")
                    W("scalar", "dve", v_d)
                    v_a = R("scalar", lambda e, c0=c0: e.activation(
                        ptile[:, c0: c0 + 4], ptile[:, c0: c0 + 4], AF.Exp,
                        bias=stat[:, 400:401], accum_out=stat[:, 401:402]), inc="act")
                    W("vector", "act", v_a)
                    v_d = R("vector", lambda e: e.reciprocal(stat[:, 402:403],
                                                             stat[:, 401:402]), inc="dve")
                    W("vector", "dve", v_d)
                    v_d = R("vector", lambda e, c0=c0: e.tensor_scalar(
                        ptile[:, c0: c0 + 4], ptile[:, c0: c0 + 4], stat[:, 402:403],
                        None, ALU.mult), inc="dve")
                    W("vector", "dve", v_d)
                v_d = R("vector", lambda e, o0=o0: e.tensor_max(
                    ptile[:, o0: o0 + 16], ptile[:, o0: o0 + 16], wconv[:, 16:32]),
                    inc="dve")
                W("vector", "dve", v_d)
                for hh in range(4):
                    c0 = o0 + hh * 4
                    v_d = R("vector", lambda e, c0=c0: e.tensor_reduce(
                        stat[:, 401:402], ptile[:, c0: c0 + 4], AX, ALU.add), inc="dve")
                    W("vector", "dve", v_d)
                    v_d = R("vector", lambda e: e.reciprocal(stat[:, 402:403],
                                                             stat[:, 401:402]), inc="dve")
                    W("vector", "dve", v_d)
                    v_d = R("vector", lambda e, c0=c0: e.tensor_scalar(
                        ptile[:, c0: c0 + 4], ptile[:, c0: c0 + 4], stat[:, 402:403],
                        None, ALU.mult), inc="dve")
                    W("vector", "dve", v_d)
            W("gpsimd", "dve", kb.cnt["dve"])
            v_sp = kb.dma(scr_dbg2, ptile[:], sem="dmc", queue="gpsimd")
            W("vector", "dmc", v_sp)
            W("sync", "dve", kb.cnt["dve"])
            v_l = kb.dma(wstage[0:1, 0:256], norm_wd, sem="dmb")
            W("vector", "dmb", v_l)
            v_c = R("vector", lambda e: e.tensor_copy(statb[0:1, 576:832], wstage[0:1, 0:256]),
                    inc="dve")
            W("tensor", "dve", v_c)
            v_pe = R("tensor", lambda e: mm(e, p2a[:, 0:256], ones_row[0:1, 0:128],
                                            statb[0:1, 576:832], True, True), inc="pe")
            W("vector", "pe", v_pe)
            v_d = R("vector", lambda e: e.tensor_copy(statb[:, 0:256], p2a[:, 0:256]),
                    inc="dve")
            W("tensor", "dve", v_d)
            for i in range(NTT):
                W("sync", "dve", kb.cnt["dve"])
                W("sync", "pe", kb.cnt["pe"])
                kb.dma(bt3[:, 0:1024], scr_short_b[i * 128:(i + 1) * 128, :], sem="dmb")
                kb.dma(bt3[:, 1024:2048], scr_long_b[i * 128:(i + 1) * 128, :], sem="dmb")
                v_l = kb.dma(bt3[:, 2048:3072], scr_delta_b[i * 128:(i + 1) * 128, :],
                             sem="dmb")
                W("vector", "dmb", v_l)
                for hh in range(4):
                    d0 = 3072 + hh * 256
                    srcs = (bt3[:, hh * 256:(hh + 1) * 256],
                            bt3[:, 1024 + hh * 256: 1024 + (hh + 1) * 256],
                            bt3[:, 2048 + hh * 256: 2048 + (hh + 1) * 256],
                            bt1[:, i * 1024 + hh * 256: i * 1024 + (hh + 1) * 256])
                    for j4 in range(4):
                        pcol = ptile[:, i * 16 + hh * 4 + j4: i * 16 + hh * 4 + j4 + 1]
                        if j4 == 0:
                            v_d = R("vector", lambda e, d0=d0, src=srcs[0], pcol=pcol:
                                    e.tensor_tensor(bt3[:, d0: d0 + 256], src,
                                                    pcol.broadcast_to((128, 256)), ALU.mult),
                                    inc="dve")
                        else:
                            v_d = R("vector", lambda e, src=srcs[j4], pcol=pcol:
                                    e.tensor_tensor(stat[:, 0:256], src,
                                                    pcol.broadcast_to((128, 256)), ALU.mult),
                                    inc="dve")
                            W("vector", "dve", v_d)
                            v_d = R("vector", lambda e, d0=d0: e.tensor_tensor(
                                bt3[:, d0: d0 + 256], bt3[:, d0: d0 + 256], stat[:, 0:256],
                                ALU.add), inc="dve")
                        W("vector", "dve", v_d)
                    v_d = R("vector", lambda e, d0=d0: e.tensor_tensor(
                        stat[:, 0:256], bt3[:, d0: d0 + 256], bt3[:, d0: d0 + 256],
                        ALU.mult), inc="dve")
                    W("vector", "dve", v_d)
                    v_d = R("vector", lambda e: e.tensor_reduce(
                        stat[:, 400:401], stat[:, 0:256], AX, ALU.add), inc="dve")
                    W("vector", "dve", v_d)
                    v_d = R("vector", lambda e: e.tensor_scalar(
                        stat[:, 400:401], stat[:, 400:401], 1.0 / DV, 1e-5, ALU.mult,
                        ALU.add), inc="dve")
                    W("scalar", "dve", v_d)
                    v_a = R("scalar", lambda e: e.activation(
                        stat[:, 401:402], stat[:, 400:401], AF.Sqrt), inc="act")
                    W("vector", "act", v_a)
                    v_d = R("vector", lambda e: e.reciprocal(stat[:, 402:403],
                                                             stat[:, 401:402]), inc="dve")
                    W("vector", "dve", v_d)
                    v_d = R("vector", lambda e, d0=d0: e.tensor_tensor(
                        bt3[:, d0: d0 + 256], bt3[:, d0: d0 + 256],
                        stat[:, 402:403].broadcast_to((128, 256)), ALU.mult), inc="dve")
                    W("vector", "dve", v_d)
                    v_d = R("vector", lambda e, d0=d0: e.tensor_tensor(
                        bt3[:, d0: d0 + 256], bt3[:, d0: d0 + 256], statb[:, 0:256],
                        ALU.mult), inc="dve")
                    W("vector", "dve", v_d)
                for c in range(8):
                    W("tensor", "dve", kb.cnt["dve"])
                    vp = R("tensor", lambda e, c=c: e.transpose(
                        ptr[:, 0:128], bt3[:, 3072 + c * 128: 3072 + (c + 1) * 128],
                        ident_b[:]), inc="pe")
                    W("vector", "pe", vp)
                    v_d = R("vector", lambda e, c=c, i=i: e.tensor_copy(
                        x_ln[:, c * T + i * 128: c * T + i * 128 + 128], ptr[:, 0:128]),
                        inc="dve")
                    W("tensor", "dve", v_d)
            for o in range(8):
                W("sync", "dve", kb.cnt["dve"])
                for kc in range(8):
                    kb.dma(wstage[:, 0:128] if kc == 0 else wstage[:, 0:128],
                           Wo_d[kc * 128:(kc + 1) * 128, o * 128:(o + 1) * 128], sem="dmb")
                    W("vector", "dmb", kb.cnt["dmb"])
                    v_c = R("vector", lambda e, kc=kc: e.tensor_copy(
                        wstream[:, 3072 + kc * 128: 3072 + (kc + 1) * 128],
                        wstage[:, 0:128]), inc="dve")
                    W("sync", "dve", kb.cnt["dve"])
                W("tensor", "dve", kb.cnt["dve"])
                for s0 in range(0, T, 512):
                    v_pe = 0
                    for kc in range(8):
                        v_pe = R("tensor", lambda e, kc=kc, s0=s0: mm(
                            e, p2a[:, 0:512], wstream[:, 3072 + kc * 128: 3072 + (kc + 1) * 128],
                            x_ln[:, kc * T + s0: kc * T + s0 + 512], kc == 0, kc == 7),
                            inc="pe")
                    W("vector", "pe", v_pe)
                    W("vector", "dve", kb.cnt["dve"])
                    v_d = R("vector", lambda e, o=o, s0=s0: e.tensor_tensor(
                        x_res[:, o * T + s0: o * T + s0 + 512],
                        x_res[:, o * T + s0: o * T + s0 + 512], p2a[:, 0:512], ALU.add),
                        inc="dve")
                    W("vector", "dve", v_d)
                    W("tensor", "dve", v_d)

        def prep_conv_w():
            W("sync", "dve", kb.cnt["dve"])
            kb.dma(wstage[:, 560:592].rearrange("p (c k) -> p c k", k=4),
                   conv_qd.rearrange("(c p) k -> p c k", p=128), sem="dmb")
            kb.dma(wstage[:, 592:624].rearrange("p (c k) -> p c k", k=4),
                   conv_kd.rearrange("(c p) k -> p c k", p=128), sem="dmb")
            v_l = kb.dma(wstage[:, 624:656].rearrange("p (c k) -> p c k", k=4),
                         conv_vd.rearrange("(c p) k -> p c k", p=128), sem="dmb")
            W("vector", "dmb", v_l)
            v_c = R("vector", lambda e: e.tensor_copy(wconv[:, 560:656], wstage[:, 560:656]),
                    inc="dve")
            return v_c

        if stage >= 4:
            prep_conv_w()
            emit_attn_b(stage)
        if stage >= 7:
            emit_ln_slab()
            emit_ffn()

        W("gpsimd", "dve", kb.cnt["dve"])
        W("gpsimd", "pe", kb.cnt["pe"])
        W("gpsimd", "act", kb.cnt["act"])
        for c in range(8):
            kb.dma(out_fm.rearrange("(c p) t -> c p t", p=128)[c],
                   x_res[:, c * T:(c + 1) * T], sem="dmo", queue="gpsimd")
        W("gpsimd", "dmo", kb.cnt["dmo"])

        block = ctx.enter_context(nc.Block())
        block.sync(lambda e: [fn(e) for fn in kb.q["sync"]])
        block.tensor(lambda e: [fn(e) for fn in kb.q["tensor"]])
        block.vector(lambda e: [fn(e) for fn in kb.q["vector"]])
        block.scalar(lambda e: [fn(e) for fn in kb.q["scalar"]])
        block.gpsimd(lambda e: [fn(e) for fn in kb.q["gpsimd"]])

    return nc


def _layout_inputs(inputs, core):
    d = {}
    d["x_fm"] = np.ascontiguousarray(np.asarray(inputs["L0_attn_a_x"])[core].T)
    d["w_qkv"] = inputs["L0_attn_a_w_qkv"]
    d["w_out_a"] = inputs["L0_attn_a_w_out"]
    d["b_out_a"] = np.asarray(inputs["L0_attn_a_b_out"]).reshape(1, D)
    d["Wq"] = inputs["L0_attn_b_Wq"]; d["Wk"] = inputs["L0_attn_b_Wk"]; d["Wv"] = inputs["L0_attn_b_Wv"]
    d["Wb"] = inputs["L0_attn_b_Wb"]
    d["forget"] = np.asarray(inputs["L0_attn_b_forget_param"]).reshape(1, H_B)
    d["conv_q"] = inputs["L0_attn_b_conv_q"]; d["conv_k"] = inputs["L0_attn_b_conv_k"]
    d["conv_v"] = inputs["L0_attn_b_conv_v"]
    d["fir_s"] = np.asarray(inputs["L0_attn_b_fir_short_w"]).reshape(D, 3)
    d["fir_l"] = np.asarray(inputs["L0_attn_b_fir_long_w"]).reshape(D, KLONG)
    d["log_temp"] = np.asarray(inputs["L0_attn_b_log_temp"]).reshape(1, H_B)
    d["floor_p"] = np.asarray(inputs["L0_attn_b_floor_param"]).reshape(1, H_B * 4)
    d["W1"] = inputs["L0_attn_b_W1"]; d["b1"] = np.asarray(inputs["L0_attn_b_b1"]).reshape(1, DMID)
    d["W2"] = inputs["L0_attn_b_W2"]; d["b2"] = np.asarray(inputs["L0_attn_b_b2"]).reshape(1, H_B * 4)
    d["norm_w"] = np.asarray(inputs["L0_attn_b_norm_w"]).reshape(1, DV)
    d["Wo"] = inputs["L0_attn_b_Wo"]
    d["W_lin"] = inputs["L0_ffn_W_lin"]
    d["b_lin"] = np.asarray(inputs["L0_ffn_b_lin"]).reshape(1, D)
    d["gate_w"] = inputs["L0_ffn_gate_w"]
    d["lora_l"] = inputs["L0_ffn_left"]
    d["lora_r"] = np.asarray(inputs["L0_ffn_right"]).reshape(ER, D)
    for k, v in CONSTS.items():
        d["c_" + k] = v
    return {k: np.ascontiguousarray(np.asarray(v, dtype=np.float32)) for k, v in d.items()}


def kernel(**inputs):
    nc = build_program(stage=KSTAGE)
    in_maps = [_layout_inputs(inputs, core) for core in range(NCORE)]
    res = run_bass_kernel_spmd(nc, in_maps, core_ids=list(range(NCORE)))
    out = np.stack([np.asarray(r["out_fm"]).T for r in res.results], axis=0)
    return np.ascontiguousarray(out.astype(np.float32))


# version 04
# speedup vs baseline: 1.0427x; 1.0427x over previous
import contextlib
import math
import numpy as np

import concourse.bass as bass
import concourse.mybir as mybir
from concourse.bass_utils import run_bass_kernel_spmd

F32 = mybir.dt.float32
BF16 = mybir.dt.bfloat16
AF = mybir.ActivationFunctionType
ALU = mybir.AluOpType
AX = mybir.AxisListType.X

D = 1024
T = 2048
NCORE = 8
NH_A = 16
DH_A = 64
SCALE_A = DH_A ** -0.5
H_B = 4
DK = 256
DV = 256
CHUNK = 32
NCHUNK = T // CHUNK
GRP = 4
NGRP = NCHUNK // GRP
FORGET_MIN = 0.55
MAX_FLOOR = 0.075
E_FF = 8
R_FF = 4
ER = E_FF * R_FF
DMID = 2048
GATE_IN = D + 16 * H_B
LFT = 256
KLONG = 63
NTT = T // 128
KSTAGE = 99
SUBSTAGE = 99
DBG_SPILLS = False


def _host_constants():
    c = {}
    c["ident"] = np.eye(128, dtype=np.float32)
    i32 = np.arange(32)
    sl = (i32[:, None] > i32[None, :]).astype(np.float32)
    il = (i32[:, None] >= i32[None, :]).astype(np.float32)
    msl = np.zeros((128, 128), np.float32)
    mil = np.zeros((128, 128), np.float32)
    for b in range(4):
        msl[b * 32:(b + 1) * 32, b * 32:(b + 1) * 32] = sl
        mil[b * 32:(b + 1) * 32, b * 32:(b + 1) * 32] = il
    c["mask_sl"] = -msl
    c["mask_su"] = np.ascontiguousarray(-msl.T)
    c["mask_il"] = mil
    c["mask_iu"] = np.ascontiguousarray(mil.T)
    L = LFT
    s = np.arange(L)
    f = np.arange(128)
    ang = 2.0 * np.pi * np.outer(s, f) / L
    fwdA = np.cos(ang)
    fwdB = np.zeros((L, 128))
    fwdB[:, 0] = np.cos(np.pi * s)
    fwdB[:, 1:] = -np.sin(ang[:, 1:])
    c["dft_fwdA"] = fwdA.astype(np.float32)
    c["dft_fwdB"] = fwdB.astype(np.float32)
    tau = np.arange(128, 256)
    angi = 2.0 * np.pi * np.outer(f, tau) / L
    invA = np.zeros((128, 128)); invA[0] = 1.0 / L
    invA[1:] = 2.0 / L * np.cos(angi[1:])
    invB = np.zeros((128, 128)); invB[0] = np.cos(np.pi * tau) / L
    invB[1:] = -2.0 / L * np.sin(angi[1:])
    c["dft_invA"] = invA.astype(np.float32)
    c["dft_invB"] = invB.astype(np.float32)
    j = np.arange(KLONG)
    lag = (KLONG - 1) - j
    angh = 2.0 * np.pi * np.outer(lag, f) / L
    hA = np.cos(angh)
    hB = np.zeros((KLONG, 128))
    hB[:, 0] = np.cos(np.pi * lag)
    hB[:, 1:] = -np.sin(angh[:, 1:])
    c["dft_hA"] = hA.astype(np.float32)
    c["dft_hB"] = hB.astype(np.float32)
    return c


CONSTS = _host_constants()


class KB:

    def __init__(self, nc, ctx):
        self.nc = nc
        self.ctx = ctx
        self.q = {"sync": [], "tensor": [], "vector": [], "scalar": [], "gpsimd": []}
        self.sems = {}
        self.cnt = {}
        for s in ("dma", "dmb", "dmc", "dmo", "pe", "dve", "act", "pool"):
            self.sems[s] = ctx.enter_context(nc.semaphore(s))
            self.cnt[s] = 0
        self.lastw = {e: {} for e in self.q}

    def on(self, engine, fn):
        self.q[engine].append(fn)

    def wait(self, engine, sem_name, v=None):
        if v is None:
            v = self.cnt[sem_name]
        if v <= 0:
            return
        if self.lastw[engine].get(sem_name, -1) >= v:
            return
        self.lastw[engine][sem_name] = v
        sem = self.sems[sem_name]
        self.on(engine, lambda e, sm=sem, vv=v: e.wait_ge(sm, vv))

    def run(self, engine, fn, inc=None, n=1):
        if inc is None:
            self.on(engine, fn)
            return self.cnt.get(engine, 0)
        self.cnt[inc] += n
        v = self.cnt[inc]
        sem = self.sems[inc]
        def wrapped(e, fn=fn, sem=sem, n=n):
            fn(e).then_inc(sem, n)
        self.on(engine, wrapped)
        return v

    def dma(self, dst_ap, src_ap, sem="dma", queue="sync", slow=False):
        self.cnt[sem] += 16
        v = self.cnt[sem]
        s = self.sems[sem]
        if slow:
            self.on(queue, lambda e, d=dst_ap, sp=src_ap, sm=s:
                    e.dma_start(out=d, in_=sp, allow_slow_non_contiguous=True).then_inc(sm, 16))
        else:
            self.on(queue, lambda e, d=dst_ap, sp=src_ap, sm=s:
                    e.dma_start(out=d, in_=sp).then_inc(sm, 16))
        return v

    def fence(self, engine, sem="dve" if True else None):
        s = {"vector": "dve", "scalar": "act", "gpsimd": "pool"}[engine]
        self.cnt[s] += 1
        v = self.cnt[s]
        sem_h = self.sems[s]
        self.on(engine, lambda e, sm=sem_h, vv=v: e.nop().then_inc(sm, vv - vv + 1))
        self.on(engine, lambda e, sm=sem_h, vv=v: e.wait_ge(sm, vv))
        return v


def ceil_div(a, b):
    return (a + b - 1) // b


def build_program(stage=99):
    nc = bass.Bass("TRN2", target_bir_lowering=False)

    din = {}
    def dt_in(name, shape):
        din[name] = nc.dram_tensor(name, list(shape), F32, kind="ExternalInput").ap()
        return din[name]

    x_in = dt_in("x_fm", [D, T])
    w_qkv = dt_in("w_qkv", [3 * D, D])
    w_out_a = dt_in("w_out_a", [D, D])
    b_out_a = dt_in("b_out_a", [1, D])
    Wq_d = dt_in("Wq", [D, D]); Wk_d = dt_in("Wk", [D, D]); Wv_d = dt_in("Wv", [D, D])
    Wb_d = dt_in("Wb", [D, H_B])
    forget_d = dt_in("forget", [1, H_B])
    conv_qd = dt_in("conv_q", [D, 4]); conv_kd = dt_in("conv_k", [D, 4]); conv_vd = dt_in("conv_v", [D, 4])
    fir_sd = dt_in("fir_s", [D, 3])
    fir_ld = dt_in("fir_l", [D, KLONG])
    log_temp_d = dt_in("log_temp", [1, H_B])
    floor_d = dt_in("floor_p", [1, H_B * 4])
    W1_d = dt_in("W1", [GATE_IN, DMID])
    b1_d = dt_in("b1", [1, DMID])
    W2_d = dt_in("W2", [DMID, H_B * 4])
    b2_d = dt_in("b2", [1, H_B * 4])
    norm_wd = dt_in("norm_w", [1, DV])
    Wo_d = dt_in("Wo", [D, D])
    W_lin_d = dt_in("W_lin", [D, D])
    b_lin_d = dt_in("b_lin", [1, D])
    gate_wd = dt_in("gate_w", [E_FF, D])
    lora_ld = dt_in("lora_l", [E_FF, D, R_FF])
    lora_rd = dt_in("lora_r", [ER, D])
    for k, v in CONSTS.items():
        dt_in("c_" + k, v.shape)

    out_fm = nc.dram_tensor("out_fm", [D, T], F32, kind="ExternalOutput").ap()
    scr_small = nc.dram_tensor("scr_small", [64, 64], F32, kind="Internal").ap()
    scr_short = nc.dram_tensor("scr_short", [T, D], F32, kind="Internal").ap()
    _spill_kind = "ExternalOutput" if DBG_SPILLS else "Internal"
    scr_short_b = nc.dram_tensor("scr_short_b", [T, D], BF16, kind=_spill_kind).ap()
    scr_long_b = nc.dram_tensor("scr_long_b", [T, D], BF16, kind=_spill_kind).ap()
    scr_delta_b = nc.dram_tensor("scr_delta_b", [T, D], BF16, kind=_spill_kind).ap()
    scr_dbg = nc.dram_tensor("scr_dbg", [128, 256], F32, kind=_spill_kind).ap()
    scr_dbg2 = nc.dram_tensor("scr_dbg2", [128, 256], F32, kind=_spill_kind).ap()
    scr_dbg3 = nc.dram_tensor("scr_dbg3", [64, 2048], BF16, kind=_spill_kind).ap()

    with contextlib.ExitStack() as ctx:
        kb = KB(nc, ctx)
        SB = lambda name, shape, dt=BF16: ctx.enter_context(nc.sbuf_tensor(name, shape, dt))

        x_res = SB("x_res", [128, 8 * T], F32)
        x_ln = SB("x_ln", [128, 8 * T], BF16)
        bt1 = SB("bt1", [128, 16 * 1024], BF16)
        bt2 = SB("bt2", [128, 4 * 1024], BF16)
        bt3 = SB("bt3", [128, 16 * 1024], BF16)
        wstream = SB("wstream", [128, 6 * 1024], BF16)
        wsmall = SB("wsmall", [128, 1024], BF16)
        wstage = SB("wstage", [128, 1024], F32)
        ident_b = SB("ident_b", [128, 128], BF16)
        ident_f = SB("ident_f", [128, 128], F32)
        masks = SB("masks", [128, 4 * 128], BF16)
        stat = SB("stat", [128, 512], F32)
        statb = SB("statb", [128, 1024], BF16)
        svt = SB("svt", [128, 16 * 65], BF16)
        bcols = SB("bcols", [128, 64], F32)
        b1cols = SB("b1cols", [128, 16], F32)
        Sst = SB("Sst", [128, 2 * 256], F32)
        stats_tok = SB("stats_tok", [128, 16 * 64], BF16)
        ptile = SB("ptile", [128, 16 * 16], F32)
        wconv = SB("wconv", [128, 660], F32)

        PS = lambda name, shape, dt=F32: ctx.enter_context(nc.psum_tensor(name, shape, dt))
        p2a = PS("p2a", [128, 1024])
        p2b = PS("p2b", [128, 1024])
        p1a = PS("p1a", [128, 512])
        p1b = PS("p1b", [128, 512])
        p1c = PS("p1c", [128, 512])
        ptr = PS("ptr", [128, 256], BF16)

        sems = kb.sems
        W = kb.wait
        R = kb.run

        def mm(e, out, lhsT, rhs, first, last, tp=None):
            return e.matmul(out, lhsT, rhs, start=first, stop=last, tile_position=tp)

        def evac(engine, dst, src, inc):
            return R(engine, lambda e, d=dst, s=src: e.tensor_copy(d, s), inc=inc)

        def transpose_to(dst, src, slot, pe_inc=True):
            v_pe = R("tensor", lambda e, s=src, sl=slot: e.transpose(
                ptr[:, sl * 128: sl * 128 + src.shape[0]], s, ident_b[:]), inc="pe")
            W("vector", "pe", v_pe)
            v_d = evac("vector", dst, ptr[:, slot * 128: slot * 128 + src.shape[0]], "dve")
            return v_pe, v_d

        for c in range(8):
            kb.dma(x_res[:, c * T:(c + 1) * T],
                   x_in.rearrange("(c p) t -> c p t", p=128)[c])
        kb.dma(wstage[:, 0:128], din["c_ident"])
        for i, mname in enumerate(("mask_sl", "mask_su", "mask_il", "mask_iu")):
            kb.dma(wstage[:, 128 * (i + 1):128 * (i + 2)], din["c_" + mname])
        v_x = kb.cnt["dma"]
        W("vector", "dma", v_x)
        R("vector", lambda e: e.tensor_copy(ident_b[:], wstage[:, 0:128]))
        R("vector", lambda e: e.tensor_copy(ident_f[:], wstage[:, 0:128]))
        v_mcast = R("vector", lambda e: e.tensor_copy(masks[:, 0:512], wstage[:, 128:640]),
                    inc="dve")

        def emit_ln_slab():
            sqb = bt2
            W("vector", "pe", kb.cnt["pe"])
            for s0 in range(0, T, 512):
                v_cast = 0
                for c in range(8):
                    v_cast = R("vector", lambda e, c=c, s0=s0: e.tensor_copy(
                        x_ln[:, c * T + s0: c * T + s0 + 512],
                        x_res[:, c * T + s0: c * T + s0 + 512]), inc="dve")
                W("scalar", "dve", v_cast)
                v_sq = 0
                for c in range(8):
                    v_sq = R("scalar", lambda e, c=c, s0=s0: e.activation(
                        sqb[:, c * 512: c * 512 + 512],
                        x_ln[:, c * T + s0: c * T + s0 + 512], AF.Square), inc="act")
                W("tensor", "dve", v_cast)
                W("tensor", "act", v_sq)
                for c in range(8):
                    R("tensor", lambda e, c=c, s0=s0: mm(
                        e, p1a[0:1, 0:512], ones_col[:, 0:1],
                        x_ln[:, c * T + s0: c * T + s0 + 512], c == 0, c == 7))
                v_pe = 0
                for c in range(8):
                    v_pe = R("tensor", lambda e, c=c: mm(
                        e, p1b[0:1, 0:512], ones_col[:, 0:1],
                        sqb[:, c * 512: c * 512 + 512], c == 0, c == 7), inc="pe")
                W("vector", "pe", v_pe)
                R("vector", lambda e: e.tensor_scalar(
                    stat[0:1, 0:512], p1a[0:1, 0:512], 1.0 / D, None, ALU.mult))
                R("vector", lambda e: e.tensor_scalar(
                    wstage[0:1, 0:512], p1b[0:1, 0:512], 1.0 / D, None, ALU.mult))
                R("vector", lambda e: e.tensor_mul(Sst[0:1, 0:512], stat[0:1, 0:512],
                                                   stat[0:1, 0:512]))
                R("vector", lambda e: e.tensor_sub(Sst[0:1, 0:512], wstage[0:1, 0:512],
                                                   Sst[0:1, 0:512]))
                v_d = R("vector", lambda e: e.tensor_scalar(
                    Sst[0:1, 0:512], Sst[0:1, 0:512], 1e-5, None, ALU.add), inc="dve")
                W("scalar", "dve", v_d)
                v_a = R("scalar", lambda e: e.activation(Sst[0:1, 0:512], Sst[0:1, 0:512],
                                                         AF.Sqrt), inc="act")
                W("vector", "act", v_a)
                R("vector", lambda e: e.reciprocal(Sst[0:1, 0:512], Sst[0:1, 0:512]))
                R("vector", lambda e: e.tensor_mul(wstage[0:1, 0:512], stat[0:1, 0:512],
                                                   Sst[0:1, 0:512]))
                R("vector", lambda e: e.tensor_copy(statb[0:1, 0:512], Sst[0:1, 0:512]))
                v_d = R("vector", lambda e: e.tensor_copy(statb[0:1, 512:1024],
                                                          wstage[0:1, 0:512]), inc="dve")
                W("tensor", "dve", v_d)
                R("tensor", lambda e: mm(e, p1b[:, 0:512], ones_row[0:1, 0:128],
                                         statb[0:1, 0:512], True, True))
                v_pe = R("tensor", lambda e: mm(e, p1c[:, 0:512], ones_row[0:1, 0:128],
                                                statb[0:1, 512:1024], True, True), inc="pe")
                W("vector", "pe", v_pe)
                vlast = 0
                for c in range(8):
                    R("vector", lambda e, c=c, s0=s0: e.tensor_tensor(
                        x_ln[:, c * T + s0: c * T + s0 + 512],
                        x_ln[:, c * T + s0: c * T + s0 + 512], p1b[:, 0:512], ALU.mult))
                    vlast = R("vector", lambda e, c=c, s0=s0: e.tensor_tensor(
                        x_ln[:, c * T + s0: c * T + s0 + 512],
                        x_ln[:, c * T + s0: c * T + s0 + 512], p1c[:, 0:512], ALU.subtract),
                        inc="dve")
                W("tensor", "dve", vlast)
            return kb.cnt["dve"]

        ones_col = SB("ones_col", [128, 2], BF16)
        zrow = SB("zrow", [1, 512], BF16)
        ones_row = SB("ones_row", [1, 128], BF16)
        R("vector", lambda e: e.memset(ones_col[:], 1.0))
        R("vector", lambda e: e.memset(zrow[:], 0.0))
        R("vector", lambda e: e.memset(ones_row[:], 1.0))
        R("vector", lambda e: e.memset(svt[:], 1.0))

        W("vector", "dma", v_x)
        if stage >= 1:
            emit_ln_slab()

        def prep_qkvT(hp, slot):
            vs = []
            for j, base in enumerate((0, D, 2 * D)):
                W("sync", "dve", kb.cnt["dve"])
                v_l = kb.dma(wstage[:, 0:1024],
                             w_qkv[base + hp * 128: base + hp * 128 + 128, :], sem="dmb")
                W("vector", "dmb", v_l)
                v_c = R("vector", lambda e: e.tensor_copy(wsmall[:, 0:1024], wstage[:, 0:1024]),
                        inc="dve")
                for c in range(8):
                    W("tensor", "dve", v_c)
                    v_pe = R("tensor", lambda e, c=c: e.transpose(
                        ptr[:, 0:128], wsmall[:, c * 128:(c + 1) * 128], ident_b[:]), inc="pe")
                    W("vector", "pe", v_pe)
                    v_d = R("vector", lambda e, c=c, j=j, slot=slot: e.tensor_copy(
                        wstream[:, slot * 3072 + c * 384 + j * 128: slot * 3072 + c * 384 + j * 128 + 128],
                        ptr[:, 0:128]), inc="dve")
                    W("tensor", "dve", v_d)
                vs.append(v_d)
            return kb.cnt["dve"]

        def prep_bias_cols(dsrc, dst_off, n_chunk):
            W("sync", "dve", kb.cnt["dve"])
            v_l = kb.dma(wstage[0:1, 0:n_chunk * 128], dsrc, sem="dmb")
            W("vector", "dmb", v_l)
            v_c = R("vector", lambda e: e.tensor_copy(statb[0:1, 0:n_chunk * 128],
                                                      wstage[0:1, 0:n_chunk * 128]), inc="dve")
            for c in range(n_chunk):
                W("tensor", "dve", v_c)
                v_pe = R("tensor", lambda e, c=c: e.transpose(
                    ptr[:, 0:1], statb[0:1, c * 128:(c + 1) * 128], ident_b[0:1, 0:1]), inc="pe")
                W("vector", "pe", v_pe)
                v_d = R("vector", lambda e, c=c: e.tensor_copy(
                    bcols[:, dst_off + c: dst_off + c + 1], ptr[:, 0:1]), inc="dve")
                W("tensor", "dve", v_d)
            return kb.cnt["dve"]


        def emit_attn_a(stage=99):
            sub = SUBSTAGE
            prep_bias_cols(b_out_a, 0, 8)
            prep_qkvT(0, 0)
            for hp in range(8 if sub >= 5 else 1):
                slot = hp % 2
                if hp < 7 and sub >= 5:
                    prep_qkvT(hp + 1, (hp + 1) % 2)
                for j in range(3):
                    for s0 in range(0, T, 512):
                        v_pe = 0
                        for c in range(8):
                            v_pe = R("tensor", lambda e, j=j, s0=s0, c=c, slot=slot: mm(
                                e, p2a[:, 0:512],
                                wstream[:, slot * 3072 + c * 384 + j * 128: slot * 3072 + c * 384 + j * 128 + 128],
                                x_ln[:, c * T + s0: c * T + s0 + 512], c == 0, c == 7),
                                inc="pe")
                        W("vector", "pe", v_pe)
                        v_d = R("vector", lambda e, j=j, s0=s0: e.tensor_copy(
                            bt3[:, j * T + s0: j * T + s0 + 512], p2a[:, 0:512]), inc="dve")
                        W("tensor", "dve", v_d)
                if sub < 2:
                    continue
                v_d = kb.cnt["dve"]
                W("vector", "dve", v_d)
                for j in range(2):
                    v_sqd = R("vector", lambda e, j=j: e.tensor_mul(
                        bt2[:, j * T: j * T + T], bt3[:, j * T: j * T + T],
                        bt3[:, j * T: j * T + T]), inc="dve")
                W("tensor", "dve", v_sqd)
                for j in range(2):
                    v_pe2 = 0
                    for s0 in range(0, T, 512):
                        v_pe2 = R("tensor", lambda e, j=j, s0=s0: mm(
                            e, p1a[0:2, 0:512], hones[:, 0:2],
                            bt2[:, j * T + s0: j * T + s0 + 512], True, True), inc="pe")
                        W("vector", "pe", v_pe2)
                        if s0 == 0:
                            v_dm = R("vector", lambda e, j=j: e.tensor_reduce(
                                stat[0:2, 8 + j: 9 + j], p1a[0:2, 0:512], AX, ALU.max),
                                inc="dve")
                        else:
                            v_dm = R("vector", lambda e: e.tensor_reduce(
                                stat[0:2, 10:11], p1a[0:2, 0:512], AX, ALU.max), inc="dve")
                            W("vector", "dve", v_dm)
                            v_dm = R("vector", lambda e, j=j: e.tensor_max(
                                stat[0:2, 8 + j: 9 + j], stat[0:2, 8 + j: 9 + j],
                                stat[0:2, 10:11]), inc="dve")
                        W("tensor", "dve", v_dm)
                v_dm = R("vector", lambda e: e.tensor_mul(
                    stat[0:2, 12:13], stat[0:2, 8:9], stat[0:2, 9:10]), inc="dve")
                W("scalar", "dve", v_dm)
                v_a = R("scalar", lambda e: e.activation(stat[0:2, 13:14], stat[0:2, 12:13],
                                                         AF.Sqrt), inc="act")
                W("vector", "act", v_a)
                v_dm = R("vector", lambda e: e.tensor_scalar(
                    stat[0:2, 14:15], stat[0:2, 13:14], -SCALE_A, None, ALU.mult), inc="dve")
                W("gpsimd", "dve", v_dm)
                v_bo = kb.dma(scr_small[0:2, 0:1], stat[0:2, 14:15], sem="dmc", queue="gpsimd",
                              slow=True)
                W("gpsimd", "dmc", v_bo)
                v_bi = kb.dma(stat[0:1, 16:18], scr_small[0:2, 0:1].rearrange("a b -> b a"),
                              sem="dmc", queue="gpsimd", slow=True)
                if sub < 3:
                    continue
                for par in range(2):
                    h = hp * 2 + par
                    W("vector", "dmc", v_bi)
                    v_c = R("vector", lambda e, par=par: e.tensor_copy(
                        statb[0:1, 16 + par: 17 + par], stat[0:1, 16 + par: 17 + par]),
                        inc="dve")
                    W("tensor", "dve", v_c)
                    v_pe = R("tensor", lambda e, par=par: mm(
                        e, p1c[:, 0:1], ones_row[0:1, 0:128],
                        statb[0:1, 16 + par: 17 + par], True, True), inc="pe")
                    W("vector", "pe", v_pe)
                    v_bias = R("vector", lambda e: e.tensor_copy(stat[:, 20:21], p1c[:, 0:1]),
                               inc="dve")
                    for i in range(NTT):
                        vp = R("tensor", lambda e, i=i, par=par: e.transpose(
                            ptr[:, 0:64], bt3[64 * par:64 * par + 64, 2 * T + i * 128: 2 * T + (i + 1) * 128],
                            ident_b[64 * par:64 * par + 64, 64 * par:64 * par + 64]), inc="pe")
                        W("vector", "pe", vp)
                        vd = R("vector", lambda e, i=i: e.tensor_copy(
                            svt[:, i * 65: i * 65 + 64], ptr[:, 0:64]), inc="dve")
                        W("tensor", "dve", vd)
                    if sub == 31:
                        continue
                    for ppv in (p1a, p1b, p1c):
                        R("tensor", lambda e, ppv=ppv: mm(
                            e, ppv[:, 0:512], ones_row[0:1, 0:128], zrow[0:1, 0:512],
                            True, True))
                    W("scalar", "dve", v_bias)
                    exp_act = {}
                    sc_pe = {}
                    if sub == 31:
                        continue
                    for i in range(NTT):
                        for h2 in range(2):
                            ppx = p2a if h2 == 0 else p2b
                            if i > 0:
                                W("tensor", "act", exp_act[(i - 1, h2)])
                            v_pe = 0
                            for s4 in range(2):
                                v_pe = R("tensor", lambda e, i=i, par=par, h2=h2, s4=s4,
                                         ppx=ppx: mm(
                                    e, ppx[:, s4 * 512:(s4 + 1) * 512],
                                    bt3[64 * par:64 * par + 64, T + i * 128: T + (i + 1) * 128],
                                    bt3[64 * par:64 * par + 64,
                                        h2 * 1024 + s4 * 512: h2 * 1024 + s4 * 512 + 512],
                                    True, True), inc="pe" if s4 == 1 else None)
                            sc_pe[(i, h2)] = v_pe
                        for h2 in range(2):
                            W("scalar", "pe", sc_pe[(i, h2)])
                            ppx = p2a if h2 == 0 else p2b
                            exp_act[(i, h2)] = R("scalar", lambda e, i=i, h2=h2, ppx=ppx:
                                e.activation(
                                    bt2[:, (i % 2) * 2048 + h2 * 1024:
                                        (i % 2) * 2048 + h2 * 1024 + 1024],
                                    ppx[:, 0:1024], AF.Exp, bias=stat[:, 20:21],
                                    scale=SCALE_A), inc="act")
                        if i > 0:
                            W("tensor", "act", exp_act[(i - 1, 1)])
                            if sub != 32:
                                for tq in range(NTT):
                                    bank, sl = divmod(tq, 7)
                                    ppv = (p1a, p1b, p1c)[bank]
                                    R("tensor", lambda e, i=i, tq=tq, ppv=ppv, sl=sl:
                                      e.matmul(
                                        ppv[:, sl * 68: sl * 68 + 65],
                                        bt2[:, ((i - 1) % 2) * 2048 + tq * 128:
                                            ((i - 1) % 2) * 2048 + tq * 128 + 128],
                                        svt[:, (i - 1) * 65: (i - 1) * 65 + 65],
                                        start=False, stop=False, skip_group_check=True))
                    W("tensor", "act", exp_act[(NTT - 1, 1)])
                    v_pv = 0
                    for tq in range(NTT):
                        bank, sl = divmod(tq, 7)
                        ppv = (p1a, p1b, p1c)[bank]
                        inc = "pe" if tq == NTT - 1 else None
                        v = R("tensor", lambda e, tq=tq, ppv=ppv, sl=sl: e.matmul(
                            ppv[:, sl * 68: sl * 68 + 65],
                            bt2[:, ((NTT - 1) % 2) * 2048 + tq * 128:
                                ((NTT - 1) % 2) * 2048 + tq * 128 + 128],
                            svt[:, (NTT - 1) * 65: (NTT - 1) * 65 + 65],
                            start=False, stop=False, skip_group_check=True),
                            inc=inc)
                        if inc and sub == 32:
                            v_pv = v
                        elif inc:
                            v_pv = v
                    W("vector", "pe", v_pv)
                    for tq in range(0 if sub in (32, 33) else NTT):
                        bank, sl = divmod(tq, 7)
                        ppv = (p1a, p1b, p1c)[bank]
                        if sub == 36:
                            v_d = R("vector", lambda e, ppv=ppv: e.tensor_copy(
                                stat[:, 256:320], ppv[:, 0:64]), inc="dve")
                            W("vector", "dve", v_d)
                            continue
                        if sub in (34, 35):
                            v_d = R("vector", lambda e, ppv=ppv, sl=sl: e.tensor_copy(
                                stat[:, 22:23], ppv[:, sl * 68 + 64: sl * 68 + 65]),
                                inc="dve")
                            W("vector", "dve", v_d)
                            v_d = R("vector", lambda e, ppv=ppv, sl=sl: e.tensor_copy(
                                stat[:, 256:320], ppv[:, sl * 68: sl * 68 + 64]), inc="dve")
                            W("vector", "dve", v_d)
                            if sub == 35:
                                v_d = R("vector", lambda e: e.reciprocal(
                                    stat[:, 21:22], stat[:, 22:23]), inc="dve")
                                W("vector", "dve", v_d)
                            continue
                        v_d = R("vector", lambda e, ppv=ppv, sl=sl: e.tensor_copy(
                            stat[:, 22:23], ppv[:, sl * 68 + 64: sl * 68 + 65]), inc="dve")
                        W("vector", "dve", v_d)
                        v_d = R("vector", lambda e: e.reciprocal(
                            stat[:, 21:22], stat[:, 22:23]), inc="dve")
                        W("vector", "dve", v_d)
                        v_d = R("vector", lambda e, ppv=ppv, sl=sl: e.tensor_copy(
                            stat[:, 256:320], ppv[:, sl * 68: sl * 68 + 64]), inc="dve")
                        W("vector", "dve", v_d)
                        v_d = R("vector", lambda e, tq=tq, h=h: e.tensor_scalar(
                            bt1[:, tq * 1024 + h * 64: tq * 1024 + h * 64 + 64],
                            stat[:, 256:320], stat[:, 21:22], None, ALU.mult),
                            inc="dve")
                        W("vector", "dve", v_d)
                    W("tensor", "dve", kb.cnt["dve"])
            if sub < 6:
                return
            v_last = kb.cnt["dve"]
            W("tensor", "dve", v_last)
            for i in range(NTT):
                for c in range(8):
                    vp = R("tensor", lambda e, i=i, c=c: e.transpose(
                        ptr[:, (c % 2) * 128:(c % 2) * 128 + 128],
                        bt1[:, i * 1024 + c * 128: i * 1024 + (c + 1) * 128], ident_b[:]),
                        inc="pe")
                    W("vector", "pe", vp)
                    vd = R("vector", lambda e, i=i, c=c: e.tensor_copy(
                        x_ln[:, c * T + i * 128: c * T + i * 128 + 128],
                        ptr[:, (c % 2) * 128:(c % 2) * 128 + 128]), inc="dve")
                    W("tensor", "dve", vd)
            for o in range(8):
                W("sync", "dve", kb.cnt["dve"])
                v_l = kb.dma(wstage[:, 0:1024], w_out_a[o * 128:(o + 1) * 128, :], sem="dmb")
                W("vector", "dmb", v_l)
                v_c = R("vector", lambda e: e.tensor_copy(wsmall[:, 0:1024], wstage[:, 0:1024]),
                        inc="dve")
                for c in range(8):
                    W("tensor", "dve", v_c)
                    vp = R("tensor", lambda e, c=c: e.transpose(
                        ptr[:, 0:128], wsmall[:, c * 128:(c + 1) * 128], ident_b[:]), inc="pe")
                    W("vector", "pe", vp)
                    vd = R("vector", lambda e, c=c: e.tensor_copy(
                        wstream[:, c * 128:(c + 1) * 128], ptr[:, 0:128]), inc="dve")
                    W("tensor", "dve", vd)
                for s0 in range(0, T, 512):
                    v_pe = 0
                    for c in range(8):
                        v_pe = R("tensor", lambda e, c=c, s0=s0: mm(
                            e, p2a[:, 0:512], wstream[:, c * 128:(c + 1) * 128],
                            x_ln[:, c * T + s0: c * T + s0 + 512], c == 0, c == 7), inc="pe")
                    W("vector", "pe", v_pe)
                    W("vector", "dve", kb.cnt["dve"])
                    v_d = R("vector", lambda e, o=o, s0=s0: e.scalar_tensor_tensor(
                        x_res[:, o * T + s0: o * T + s0 + 512], p2a[:, 0:512],
                        bcols[:, 0 + o: 1 + o], x_res[:, o * T + s0: o * T + s0 + 512],
                        ALU.add, ALU.add), inc="dve")
                    W("vector", "dve", v_d)
                    W("tensor", "dve", v_d)

        hones = SB("hones", [128, 2], BF16)
        R("vector", lambda e: e.memset(hones[:], 0.0))
        R("vector", lambda e: e.memset(hones[0:64, 0:1], 1.0))
        R("vector", lambda e: e.memset(hones[64:128, 1:2], 1.0))

        if stage >= 2:
            emit_attn_a(stage)

        def prep_ffn_weights():
            prep_bias_cols(b_lin_d, 8, 8)
            W("sync", "dve", kb.cnt["dve"])
            v_l = kb.dma(wstage[0:8, 0:1024], gate_wd, sem="dmb")
            W("vector", "dmb", v_l)
            v_c = R("vector", lambda e: e.tensor_copy(statb[0:8, 0:1024],
                                                      wstage[0:8, 0:1024]), inc="dve")
            for c in range(8):
                W("tensor", "dve", v_c)
                vp = R("tensor", lambda e, c=c: e.transpose(
                    ptr[:, 0:8], statb[0:8, c * 128:(c + 1) * 128],
                    ident_b[0:8, 0:8]), inc="pe")
                W("vector", "pe", vp)
                v_c2 = R("vector", lambda e, c=c: e.tensor_copy(
                    wsmall[:, c * 8:(c + 1) * 8], ptr[:, 0:8]), inc="dve")
                W("tensor", "dve", v_c2)
            W("sync", "dve", kb.cnt["dve"])
            for e_i in range(E_FF):
                for c in range(8):
                    kb.dma(wstage[:, c * 32 + 4 * e_i: c * 32 + 4 * e_i + 4],
                           lora_ld[e_i, c * 128:(c + 1) * 128, :], sem="dmb")
            v_l = kb.cnt["dmb"]
            W("vector", "dmb", v_l)
            v_c = R("vector", lambda e: e.tensor_copy(wsmall[:, 64:320], wstage[:, 0:256]),
                    inc="dve")
            W("sync", "dve", kb.cnt["dve"])
            v_l = kb.dma(wstage[0:32, 0:1024], lora_rd, sem="dmb")
            W("vector", "dmb", v_l)
            v_c = R("vector", lambda e: e.tensor_copy(bt3[0:32, 2048:3072],
                                                      wstage[0:32, 0:1024]), inc="dve")
            return v_c

        def emit_ffn():
            prep_ffn_weights()
            for sl4 in range(4):
                v_pe = 0
                for c in range(8):
                    v_pe = R("tensor", lambda e, c=c, sl4=sl4: mm(
                        e, p1a[0:8, 0:512], wsmall[:, c * 8:(c + 1) * 8],
                        x_ln[:, c * T + sl4 * 512: c * T + sl4 * 512 + 512],
                        c == 0, c == 7), inc="pe")
                W("vector", "pe", v_pe)
                v_d = R("vector", lambda e: e.tensor_copy(stat[0:8, 0:512], p1a[0:8, 0:512]),
                        inc="dve")
                W("tensor", "dve", v_d)
                for st in range(4):
                    i = sl4 * 4 + st
                    vp = R("tensor", lambda e, st=st: e.transpose(
                        p1b[:, 0:8], stat[0:8, st * 128:(st + 1) * 128],
                        ident_f[0:8, 0:8]), inc="pe")
                    W("vector", "pe", vp)
                    v_d = R("vector", lambda e, i=i: e.tensor_copy(
                        ptile[:, i * 16: i * 16 + 8], p1b[:, 0:8]), inc="dve")
                    W("tensor", "dve", v_d)
            for i in range(NTT):
                o0 = i * 16
                v_d = R("vector", lambda e, o0=o0: e.tensor_reduce(
                    ptile[:, o0 + 8: o0 + 9], ptile[:, o0: o0 + 8], AX, ALU.max,
                    negate=True), inc="dve")
                W("scalar", "dve", v_d)
                v_a = R("scalar", lambda e, o0=o0: e.activation(
                    ptile[:, o0: o0 + 8], ptile[:, o0: o0 + 8], AF.Exp,
                    bias=ptile[:, o0 + 8: o0 + 9], scale=1.0,
                    accum_out=ptile[:, o0 + 9: o0 + 10]), inc="act")
                W("vector", "act", v_a)
                v_d = R("vector", lambda e, o0=o0: e.reciprocal(
                    ptile[:, o0 + 10: o0 + 11], ptile[:, o0 + 9: o0 + 10]), inc="dve")
                W("vector", "dve", v_d)
                v_d = R("vector", lambda e, o0=o0: e.tensor_scalar(
                    ptile[:, o0: o0 + 8], ptile[:, o0: o0 + 8],
                    ptile[:, o0 + 10: o0 + 11], None, ALU.mult), inc="dve")
                W("vector", "dve", v_d)
                v_d = R("vector", lambda e, o0=o0: e.tensor_reduce(
                    stat[:, 8:9], ptile[:, o0: o0 + 8], AX, ALU.max), inc="dve")
                W("vector", "dve", v_d)
                v_d = R("vector", lambda e, o0=o0: e.tensor_scalar(
                    stat[:, 0:8], ptile[:, o0: o0 + 8], stat[:, 8:9], None,
                    ALU.is_equal), inc="dve")
                W("vector", "dve", v_d)
                v_d = R("vector", lambda e, o0=o0: e.scalar_tensor_tensor(
                    stat[:, 0:8], stat[:, 0:8], -1e30, ptile[:, o0: o0 + 8],
                    ALU.mult, ALU.add), inc="dve")
                W("vector", "dve", v_d)
                v_d = R("vector", lambda e: e.tensor_reduce(
                    stat[:, 9:10], stat[:, 0:8], AX, ALU.max), inc="dve")
                W("vector", "dve", v_d)
                v_d = R("vector", lambda e, o0=o0: e.tensor_scalar(
                    stat[:, 0:8], ptile[:, o0: o0 + 8], stat[:, 9:10], None,
                    ALU.is_ge), inc="dve")
                W("vector", "dve", v_d)
                v_d = R("vector", lambda e, o0=o0: e.tensor_mul(
                    stat[:, 0:8], stat[:, 0:8], ptile[:, o0: o0 + 8]), inc="dve")
                W("vector", "dve", v_d)
                for r in range(4):
                    v_d = R("vector", lambda e, r=r: e.tensor_copy(
                        statb[:, 32:64].rearrange("p (x r) -> p x r", r=4)[:, :, r],
                        stat[:, 0:8]), inc="dve")
                W("tensor", "dve", v_d)
                vp = R("tensor", lambda e: e.transpose(
                    ptr[0:32, 0:128], statb[:, 32:64], ident_b[:]), inc="pe")
                W("vector", "pe", vp)
                v_d = R("vector", lambda e, i=i: e.tensor_copy(
                    bt2[0:32, 2048 + i * 128: 2048 + (i + 1) * 128], ptr[0:32, 0:128]),
                    inc="dve")
                W("tensor", "dve", v_d)
            for sl4 in range(4):
                v_pe = 0
                for c in range(8):
                    v_pe = R("tensor", lambda e, c=c, sl4=sl4: mm(
                        e, p1b[0:32, 0:512], wsmall[:, 64 + c * 32: 64 + (c + 1) * 32],
                        x_ln[:, c * T + sl4 * 512: c * T + sl4 * 512 + 512],
                        c == 0, c == 7), inc="pe")
                W("vector", "pe", v_pe)
                v_d = R("vector", lambda e, sl4=sl4: e.tensor_copy(
                    bt3[0:32, sl4 * 512:(sl4 + 1) * 512], p1b[0:32, 0:512]), inc="dve")
                W("tensor", "dve", v_d)
            v_th = R("vector", lambda e: e.tensor_mul(
                bt3[0:32, 0:2048], bt3[0:32, 0:2048], bt2[0:32, 2048:4096]), inc="dve")
            W("tensor", "dve", v_th)
            for o in range(8):
                W("sync", "dve", kb.cnt["dve"])
                v_l = kb.dma(wstage[:, 0:1024], W_lin_d[o * 128:(o + 1) * 128, :], sem="dmb")
                W("vector", "dmb", v_l)
                v_c = R("vector", lambda e: e.tensor_copy(wsmall[:, 320:320 + 1024] if False
                                                          else statb[:, 0:1024],
                                                          wstage[:, 0:1024]), inc="dve")
                for c in range(8):
                    W("tensor", "dve", v_c)
                    vp = R("tensor", lambda e, c=c: e.transpose(
                        ptr[:, 0:128], statb[:, c * 128:(c + 1) * 128], ident_b[:]),
                        inc="pe")
                    W("vector", "pe", vp)
                    vd = R("vector", lambda e, c=c: e.tensor_copy(
                        wstream[:, c * 128:(c + 1) * 128], ptr[:, 0:128]), inc="dve")
                    W("tensor", "dve", vd)
                for s0 in range(0, T, 512):
                    v_pe = 0
                    for c in range(8):
                        R("tensor", lambda e, c=c, s0=s0: mm(
                            e, p2a[:, 0:512], wstream[:, c * 128:(c + 1) * 128],
                            x_ln[:, c * T + s0: c * T + s0 + 512], c == 0, False))
                    v_pe = R("tensor", lambda e, o=o, s0=s0: mm(
                        e, p2a[:, 0:512], bt3[0:32, 2048 + o * 128: 2048 + (o + 1) * 128],
                        bt3[0:32, s0: s0 + 512], False, True), inc="pe")
                    W("vector", "pe", v_pe)
                    W("vector", "dve", kb.cnt["dve"])
                    v_d = R("vector", lambda e, o=o, s0=s0: e.scalar_tensor_tensor(
                        x_res[:, o * T + s0: o * T + s0 + 512], p2a[:, 0:512],
                        bcols[:, 8 + o: 9 + o], x_res[:, o * T + s0: o * T + s0 + 512],
                        ALU.add, ALU.add), inc="dve")
                    W("vector", "dve", v_d)
                    W("tensor", "dve", v_d)

        if stage >= 3:
            emit_ln_slab()
            emit_ffn()
        if stage >= 4:
            emit_ln_slab()


        def emit_attn_b(stage=99):
            W("sync", "dve", kb.cnt["dve"])
            v_l = kb.dma(wstage[:, 0:32].rearrange("p (c k) -> p c k", k=4),
                         Wb_d.rearrange("(c p) k -> p c k", p=128), sem="dmb")
            W("vector", "dmb", v_l)
            v_c = R("vector", lambda e: e.tensor_copy(wsmall[:, 0:32], wstage[:, 0:32]),
                    inc="dve")
            W("sync", "dve", kb.cnt["dve"])
            v_l = kb.dma(wstage[:, 32:56].rearrange("p (c k) -> p c k", k=3),
                         fir_sd.rearrange("(c p) k -> p c k", p=128), sem="dmb")
            v_l = kb.dma(wstage[:, 56:560].rearrange("p (c k) -> p c k", k=63),
                         fir_ld.rearrange("(c p) k -> p c k", p=128), sem="dmb")
            W("vector", "dmb", v_l)
            v_c = R("vector", lambda e: e.tensor_copy(wconv[:, 32:560], wstage[:, 32:560]),
                    inc="dve")
            v_c = R("vector", lambda e: e.tensor_copy(wsmall[:, 56:560], wstage[:, 56:560]),
                    inc="dve")
            W("sync", "dve", kb.cnt["dve"])
            v_l = kb.dma(wstage[0:1, 40:44], forget_d, sem="dmb")
            W("vector", "dmb", v_l)
            v_c = R("vector", lambda e: e.tensor_copy(stat[0:1, 40:44], wstage[0:1, 40:44]),
                    inc="dve")
            W("scalar", "dve", v_c)
            v_a = R("scalar", lambda e: e.activation(stat[0:1, 336:340], stat[0:1, 40:44],
                                                     AF.Sigmoid), inc="act")
            W("vector", "act", v_a)
            v_c = R("vector", lambda e: e.tensor_scalar(
                stat[0:1, 336:340], stat[0:1, 336:340], 0.45, 0.55, ALU.mult, ALU.add),
                inc="dve")
            W("vector", "dve", v_c)
            v_c = R("vector", lambda e: e.tensor_copy(statb[0:1, 0:4], stat[0:1, 336:340]),
                    inc="dve")
            W("vector", "dve", v_c)
            v_c = R("vector", lambda e: e.tensor_copy(stat[0:1, 340:344], statb[0:1, 0:4]),
                    inc="dve")
            W("vector", "dve", v_c)
            v_c = R("vector", lambda e: e.tensor_sub(stat[0:1, 344:348], stat[0:1, 336:340],
                                                     stat[0:1, 340:344]), inc="dve")
            W("vector", "dve", v_c)
            v_c = R("vector", lambda e: e.tensor_copy(statb[0:1, 4:8], stat[0:1, 344:348]),
                    inc="dve")
            W("tensor", "dve", v_c)
            for h in range(H_B):
                for which, col in ((0, h), (1, 4 + h)):
                    v_pe = R("tensor", lambda e, col=col: mm(
                        e, p1a[:, 0:1], ones_row[0:1, 0:128], statb[0:1, col:col + 1],
                        True, True), inc="pe")
                    W("vector", "pe", v_pe)
                    v_c = R("vector", lambda e: e.tensor_copy(stat[:, 350:351], p1a[:, 0:1]),
                            inc="dve")
                    W("vector", "dve", v_c)
                    v_c = R("vector", lambda e, h=h, which=which: e.tensor_tensor(
                        wstream[:, 2048 + (h * 2 + which) * 128: 2048 + (h * 2 + which + 1) * 128],
                        ident_b[:], stat[:, 350:351].broadcast_to((128, 128)), ALU.mult),
                        inc="dve")
                    W("tensor", "dve", v_c)
            for s0 in range(0, T, 512):
                v_pe = 0
                for c in range(8):
                    v_pe = R("tensor", lambda e, c=c, s0=s0: mm(
                        e, p1a[0:4, 0:512], wsmall[:, c * 4: (c + 1) * 4],
                        x_ln[:, c * T + s0: c * T + s0 + 512], c == 0, c == 7), inc="pe")
                W("scalar", "pe", v_pe)
                v_a = R("scalar", lambda e, s0=s0: e.activation(
                    statb[0:4, 0:512], p1a[0:4, 0:512], AF.Sigmoid), inc="act")
                W("tensor", "act", v_a)
                for st in range(4):
                    i = s0 // 128 + st
                    vp = R("tensor", lambda e, st=st: e.transpose(
                        ptr[:, 0:4], statb[0:4, st * 128:(st + 1) * 128],
                        ident_b[0:4, 0:4]), inc="pe")
                    W("vector", "pe", vp)
                    v_d = R("vector", lambda e, i=i: e.tensor_copy(
                        svt[:, i * 4:(i + 1) * 4], ptr[0:4, 0:128].rearrange("p f -> f p")
                        if False else ptr[0:128, 0:4] if False else ptr[:, 0:4]),
                        inc="dve")
                    W("tensor", "dve", v_d)

            for h in range(H_B):
                emit_attn_b_head(h, stage)

            if stage >= 6:
                emit_attn_b_gate()

        def load_w_chunks(wd, h, jslot):
            vs = 0
            for c in range(8):
                W("sync", "dve", kb.cnt["dve"])
                v_l = kb.dma(wstage[:, 0:256], wd[c * 128:(c + 1) * 128,
                                                  h * 256:(h + 1) * 256], sem="dmb")
                W("vector", "dmb", v_l)
                vs = R("vector", lambda e, c=c: e.tensor_copy(
                    wstream[:, c * 256:(c + 1) * 256], wstage[:, 0:256]), inc="dve")
            return vs

        def emit_attn_b_head(h, stage=99):
            for j, wd in ((0, Wq_d), (1, Wk_d), (2, Wv_d)):
                v_w = load_w_chunks(wd, h, 0)
                W("tensor", "dve", v_w)
                for mt in range(2):
                    for s0 in range(0, T, 512):
                        v_pe = 0
                        for c in range(8):
                            v_pe = R("tensor", lambda e, c=c, mt=mt, s0=s0: mm(
                                e, p2a[:, 0:512],
                                wstream[:, c * 256 + mt * 128: c * 256 + mt * 128 + 128],
                                x_ln[:, c * T + s0: c * T + s0 + 512], c == 0, c == 7),
                                inc="pe")
                        W("vector", "pe", v_pe)
                        v_d = R("vector", lambda e, mt=mt, s0=s0: e.tensor_copy(
                            bt2[:, mt * 2048 + s0: mt * 2048 + s0 + 512], p2a[:, 0:512]),
                            inc="dve")
                        W("tensor", "dve", v_d)
                for mt in range(2):
                    cc = 2 * h + mt
                    dst0 = j * 4096 + mt * 2048
                    v_d = R("vector", lambda e, mt=mt, cc=cc, dst0=dst0, j=j: e.tensor_scalar(
                        bt3[:, dst0: dst0 + T], bt2[:, mt * 2048: mt * 2048 + T],
                        wsmall[:, 32 + 24 * 0 + 0: 0] if False else
                        conv_w_col(j, cc, 3), None, ALU.mult), inc="dve")
                    W("vector", "dve", v_d)
                    for jj in range(1, 4):
                        v_d = R("vector", lambda e, mt=mt, cc=cc, dst0=dst0, j=j, jj=jj:
                                e.scalar_tensor_tensor(
                                    bt3[:, dst0 + jj: dst0 + T],
                                    bt2[:, mt * 2048: mt * 2048 + T - jj],
                                    conv_w_col(j, cc, 3 - jj),
                                    bt3[:, dst0 + jj: dst0 + T], ALU.mult, ALU.add),
                                inc="dve")
                        W("vector", "dve", v_d)
                    W("scalar", "dve", v_d)
                    v_a = R("scalar", lambda e, dst0=dst0: e.activation(
                        bt3[:, dst0: dst0 + T], bt3[:, dst0: dst0 + T], AF.Silu),
                        inc="act")
                    W("vector", "act", v_a)
                    W("tensor", "act", v_a)
            for i in range(NTT):
                for mt in range(2):
                    vp = R("tensor", lambda e, i=i, mt=mt: e.transpose(
                        ptr[:, 0:128], bt3[:, 8192 + mt * 2048 + i * 128: 8192 + mt * 2048 + (i + 1) * 128],
                        ident_b[:]), inc="pe")
                    W("vector", "pe", vp)
                    v_d = R("vector", lambda e, i=i, mt=mt, h=h: e.tensor_copy(
                        bt1[:, i * 1024 + h * 256 + mt * 128: i * 1024 + h * 256 + (mt + 1) * 128],
                        ptr[:, 0:128]), inc="dve")
                    W("tensor", "dve", v_d)
            for mt in range(2):
                cc = 2 * h + mt
                v_d = R("vector", lambda e, mt=mt, cc=cc: e.tensor_scalar(
                    bt2[:, mt * 2048: mt * 2048 + T], bt3[:, 8192 + mt * 2048: 8192 + mt * 2048 + T],
                    fir_w_col(0, cc, 2), None, ALU.mult), inc="dve")
                W("vector", "dve", v_d)
                for jj in range(1, 3):
                    v_d = R("vector", lambda e, mt=mt, cc=cc, jj=jj: e.scalar_tensor_tensor(
                        bt2[:, mt * 2048 + jj: mt * 2048 + T],
                        bt3[:, 8192 + mt * 2048: 8192 + mt * 2048 + T - jj],
                        fir_w_col(0, cc, 2 - jj),
                        bt2[:, mt * 2048 + jj: mt * 2048 + T], ALU.mult, ALU.add),
                        inc="dve")
                    W("vector", "dve", v_d)
            v_short = kb.cnt["dve"]
            for i in range(NTT):
                for mt in range(2):
                    W("tensor", "dve", v_short)
                    vp = R("tensor", lambda e, i=i, mt=mt: e.transpose(
                        ptr[:, 0:128], bt2[:, mt * 2048 + i * 128: mt * 2048 + (i + 1) * 128],
                        ident_b[:]), inc="pe")
                    W("vector", "pe", vp)
                    v_d = R("vector", lambda e, mt=mt: e.tensor_copy(
                        svt[:, 64 + mt * 128: 64 + (mt + 1) * 128], ptr[:, 0:128]),
                        inc="dve")
                    W("tensor", "dve", v_d)
                emit_stats_tile(0, h, i, svt[:, 64:320])
                W("gpsimd", "dve", kb.cnt["dve"])
                v_sp = kb.dma(scr_short_b[i * 128:(i + 1) * 128, h * 256:(h + 1) * 256],
                              svt[:, 64:320], sem="dmc", queue="gpsimd")
                W("vector", "dmc", v_sp)
            emit_fir_long(h)
            if stage >= 5:
                emit_delta_head(h)
            for i in range(NTT):
                emit_stats_tile(3, h, i, bt1[:, i * 1024 + h * 256: i * 1024 + (h + 1) * 256])

        def conv_w_col(j, cc, k):
            return wconv[:, 560 + j * 32 + cc * 4 + k: 560 + j * 32 + cc * 4 + k + 1]

        def fir_w_col(which, cc, k):
            if which == 0:
                return wconv[:, 32 + cc * 3 + k: 32 + cc * 3 + k + 1]
            return wconv[:, 56 + cc * 63 + k: 56 + cc * 63 + k + 1]

        def emit_stats_tile(branch, h, i, src_ap):
            base = i * 64 + branch * 16 + h * 4
            W("vector", "dve", kb.cnt["dve"])
            v_sum = R("vector", lambda e: e.tensor_reduce(stat[:, 352:353], src_ap, AX,
                                                          ALU.add), inc="dve")
            v_sq = R("vector", lambda e: e.tensor_tensor(stat[:, 0:256], src_ap, src_ap,
                                                         ALU.mult), inc="dve")
            W("vector", "dve", v_sq)
            v_ssq = R("vector", lambda e: e.tensor_reduce(stat[:, 353:354], stat[:, 0:256],
                                                          AX, ALU.add), inc="dve")
            R("vector", lambda e: e.tensor_reduce(stat[:, 354:355], src_ap, AX, ALU.max))
            W("vector", "dve", v_sum)
            R("vector", lambda e, base=base: e.tensor_scalar(
                stats_tok[:, base: base + 1], stat[:, 352:353], 1.0 / DV, None, ALU.mult))
            W("vector", "dve", v_ssq)
            v_d = R("vector", lambda e: e.tensor_scalar(stat[:, 356:357], stat[:, 352:353],
                                                        1.0 / DV, None, ALU.mult), inc="dve")
            W("vector", "dve", v_d)
            v_d = R("vector", lambda e: e.tensor_mul(stat[:, 356:357], stat[:, 356:357],
                                                     stat[:, 356:357]), inc="dve")
            W("vector", "dve", v_d)
            v_d = R("vector", lambda e: e.scalar_tensor_tensor(
                stat[:, 357:358], stat[:, 353:354], 1.0 / DV, stat[:, 356:357],
                ALU.mult, ALU.subtract), inc="dve")
            W("vector", "dve", v_d)
            R("vector", lambda e, base=base: e.tensor_copy(stats_tok[:, base + 1: base + 2],
                                                           stat[:, 357:358]))
            R("vector", lambda e, base=base: e.tensor_copy(stats_tok[:, base + 2: base + 3],
                                                           stat[:, 354:355]))
            W("scalar", "dve", v_ssq)
            v_a = R("scalar", lambda e: e.activation(stat[:, 355:356], stat[:, 353:354],
                                                     AF.Sqrt), inc="act")
            W("vector", "act", v_a)
            v_d = R("vector", lambda e, base=base: e.tensor_copy(
                stats_tok[:, base + 3: base + 4], stat[:, 355:356]), inc="dve")
            return v_d

        def emit_fir_long(h):
            for mt in range(2):
                cc = 2 * h + mt
                vp = R("tensor", lambda e, cc=cc: e.transpose(
                    ptr[0:63, 0:128], wsmall[:, 56 + cc * 63: 56 + (cc + 1) * 63],
                    ident_b[:]), inc="pe")
                W("vector", "pe", vp)
                v_d = R("vector", lambda e, mt=mt: e.tensor_copy(
                    statb[0:63, mt * 128:(mt + 1) * 128], ptr[0:63, 0:128]), inc="dve")
                W("tensor", "dve", v_d)
            W("sync", "dve", kb.cnt["dve"])
            kb.dma(wstage[0:63, 0:128], din["c_dft_hA"], sem="dmb")
            v_l = kb.dma(wstage[0:63, 128:256], din["c_dft_hB"], sem="dmb")
            W("vector", "dmb", v_l)
            v_c = R("vector", lambda e: e.tensor_copy(statb[0:63, 256:512],
                                                      wstage[0:63, 0:256]), inc="dve")
            W("tensor", "dve", v_c)
            v_pe = R("tensor", lambda e: mm(e, p1a[:, 0:256], statb[0:63, 256:384],
                                            statb[0:63, 0:256], True, True), inc="pe")
            v_pe = R("tensor", lambda e: mm(e, p1b[:, 0:256], statb[0:63, 384:512],
                                            statb[0:63, 0:256], True, True), inc="pe")
            W("vector", "pe", v_pe)
            v_d = R("vector", lambda e: e.tensor_copy(bt2[:, 0:256], p1a[:, 0:256]), inc="dve")
            v_d = R("vector", lambda e: e.tensor_scalar(bt2[:, 256:512], p1b[:, 0:256],
                                                        -1.0, None, ALU.mult), inc="dve")
            v_d = R("vector", lambda e: e.tensor_copy(bt2[:, 512:768], p1b[:, 0:256]), inc="dve")
            v_d = R("vector", lambda e: e.tensor_copy(bt2[:, 768:1024], p1a[:, 0:256]), inc="dve")
            W("vector", "dve", v_d)
            v_d = R("vector", lambda e: e.memset(bt2[0:1, 256:512], 0.0), inc="dve")
            v_d = R("vector", lambda e: e.memset(bt2[0:1, 512:768], 0.0), inc="dve")
            v_d = R("vector", lambda e: e.tensor_copy(bt2[0:1, 768:1024], p1b[0:1, 0:256]),
                    inc="dve")
            W("tensor", "dve", v_d)
            W("sync", "dve", kb.cnt["dve"])
            kb.dma(wstage[:, 0:128], din["c_dft_fwdA"][0:128, :], sem="dmb")
            kb.dma(wstage[:, 128:256], din["c_dft_fwdA"][128:256, :], sem="dmb")
            kb.dma(wstage[:, 256:384], din["c_dft_fwdB"][0:128, :], sem="dmb")
            v_l = kb.dma(wstage[:, 384:512], din["c_dft_fwdB"][128:256, :], sem="dmb")
            W("vector", "dmb", v_l)
            v_c = R("vector", lambda e: e.tensor_copy(bt2[:, 1024:1536], wstage[:, 0:512]),
                    inc="dve")
            W("sync", "dve", kb.cnt["dve"])
            kb.dma(wstage[:, 0:128], din["c_dft_invA"], sem="dmb")
            v_l = kb.dma(wstage[:, 128:256], din["c_dft_invB"], sem="dmb")
            W("vector", "dmb", v_l)
            v_c = R("vector", lambda e: e.tensor_copy(bt2[:, 1536:1792], wstage[:, 0:256]),
                    inc="dve")
            W("tensor", "dve", v_c)
            for i in range(NTT):
                v_pe = 0
                for (dst, c0) in ((p1a, 1024), (p1b, 1280)):
                    for kk in range(2):
                        src_tile = i - 1 + kk
                        if src_tile < 0:
                            continue
                        first = (kk == 0) if i > 0 else True
                        last = (kk == 1)
                        v_pe = R("tensor", lambda e, dst=dst, c0=c0, kk=kk, src_tile=src_tile,
                                 first=first, last=last, h=h: mm(
                                     e, dst[:, 0:256], bt2[:, c0 + kk * 128: c0 + (kk + 1) * 128],
                                     bt1[:, src_tile * 1024 + h * 256: src_tile * 1024 + (h + 1) * 256],
                                     first, last), inc="pe" if last else None)
                W("vector", "pe", v_pe)
                v_d = R("vector", lambda e: e.tensor_copy(bt2[:, 1792:2048], p1a[:, 0:256]),
                        inc="dve")
                v_d = R("vector", lambda e: e.tensor_copy(svt[:, 320:576], p1b[:, 0:256]),
                        inc="dve")
                W("vector", "dve", v_d)
                v_d = R("vector", lambda e: e.tensor_tensor(
                    stat[:, 0:256], bt2[:, 1792:2048], bt2[:, 0:256], ALU.mult), inc="dve")
                v_d = R("vector", lambda e: e.tensor_tensor(
                    stat[:, 256:512], svt[:, 320:576], bt2[:, 256:512], ALU.mult), inc="dve")
                W("vector", "dve", v_d)
                v_d = R("vector", lambda e: e.tensor_add(stat[:, 0:256], stat[:, 0:256],
                                                         stat[:, 256:512]), inc="dve")
                W("vector", "dve", v_d)
                v_d = R("vector", lambda e: e.tensor_copy(svt[:, 576:832], stat[:, 0:256]),
                        inc="dve")
                v_d = R("vector", lambda e: e.tensor_tensor(
                    stat[:, 0:256], bt2[:, 1792:2048], bt2[:, 512:768], ALU.mult), inc="dve")
                v_d = R("vector", lambda e: e.tensor_tensor(
                    stat[:, 256:512], svt[:, 320:576], bt2[:, 768:1024], ALU.mult), inc="dve")
                W("vector", "dve", v_d)
                v_d = R("vector", lambda e: e.tensor_add(stat[:, 0:256], stat[:, 0:256],
                                                         stat[:, 256:512]), inc="dve")
                W("vector", "dve", v_d)
                v_d = R("vector", lambda e: e.tensor_copy(svt[:, 832:1040] if False else
                                                          svt[:, 832:1040],
                                                          stat[:, 0:208]) if False else
                        e.tensor_copy(bt2[:, 1792:2048], stat[:, 0:256]), inc="dve")
                W("tensor", "dve", v_d)
                R("tensor", lambda e: mm(e, p1c[:, 0:256], bt2[:, 1536:1664],
                                         svt[:, 576:832], True, False))
                v_pe = R("tensor", lambda e: mm(e, p1c[:, 0:256], bt2[:, 1664:1792],
                                                bt2[:, 1792:2048], False, True), inc="pe")
                W("vector", "pe", v_pe)
                v_d = R("vector", lambda e: e.tensor_copy(svt[:, 320:576], p1c[:, 0:256]),
                        inc="dve")
                W("tensor", "dve", v_d)
                emit_stats_tile(1, h, i, svt[:, 320:576])
                W("gpsimd", "dve", kb.cnt["dve"])
                v_sp = kb.dma(scr_long_b[i * 128:(i + 1) * 128, h * 256:(h + 1) * 256],
                              svt[:, 320:576], sem="dmc", queue="gpsimd")
                W("vector", "dmc", v_sp)

        def emit_delta_head(h):
            QT, KT, QN, KN, KBF, KBT, VB, TT0, XP, XL, QB0, ABD, ACM, WFM, QPF, UCMP = (
                0, 256, 512, 768, 1024, 1280, 1536, 1792, 1920, 2048, 2176, 2304,
                2432, 2560, 2816, 3072)
            v_z = R("vector", lambda e: e.memset(Sst[:], 0.0), inc="dve")
            v_z = R("vector", lambda e: e.memset(svt[:, 64:576], 0.0), inc="dve")
            W("gpsimd", "dve", v_z)
            W("tensor", "dve", v_z)
            for g in range(NGRP):
                t0 = g * 128
                for (src0, dst) in ((0, QT), (4096, KT)):
                    for mt in range(2):
                        W("tensor", "dve", kb.cnt["dve"])
                        vp = R("tensor", lambda e, src0=src0, mt=mt, t0=t0: e.transpose(
                            ptr[:, 0:128], bt3[:, src0 + mt * 2048 + t0: src0 + mt * 2048 + t0 + 128],
                            ident_b[:]), inc="pe")
                        W("vector", "pe", vp)
                        v_d = R("vector", lambda e, dst=dst, mt=mt: e.tensor_copy(
                            bt2[:, dst + mt * 128: dst + (mt + 1) * 128], ptr[:, 0:128]),
                            inc="dve")
                        W("tensor", "dve", v_d)
                for (src, col) in ((QT, 360), (KT, 361)):
                    v_d = R("vector", lambda e, src=src: e.tensor_tensor(
                        stat[:, 0:256], bt2[:, src: src + 256], bt2[:, src: src + 256],
                        ALU.mult), inc="dve")
                    W("vector", "dve", v_d)
                    v_d = R("vector", lambda e: e.tensor_reduce(
                        stat[:, 364:365], stat[:, 0:256], AX, ALU.add), inc="dve")
                    W("vector", "dve", v_d)
                    v_d = R("vector", lambda e: e.tensor_scalar(
                        stat[:, 364:365], stat[:, 364:365], 1e-6, None, ALU.add), inc="dve")
                    W("scalar", "dve", v_d)
                    v_a = R("scalar", lambda e: e.activation(
                        stat[:, 365:366], stat[:, 364:365], AF.Sqrt), inc="act")
                    W("vector", "act", v_a)
                    v_d = R("vector", lambda e, col=col: e.reciprocal(
                        stat[:, col: col + 1], stat[:, 365:366]), inc="dve")
                    W("vector", "dve", v_d)
                v_d = R("vector", lambda e, g=g, h=h: e.tensor_tensor(
                    stat[:, 362:363], stat[:, 361:362], svt[:, g * 4 + h: g * 4 + h + 1],
                    ALU.mult), inc="dve")
                W("vector", "dve", v_d)
                for (srctok, col, dst) in ((QT, 360, QN), (KT, 361, KN), (KT, 362, KBF)):
                    v_d = R("vector", lambda e, col=col: e.tensor_tensor(
                        statb[:, 0:128], ident_b[:],
                        stat[:, col: col + 1].broadcast_to((128, 128)), ALU.mult),
                        inc="dve")
                    W("tensor", "dve", v_d)
                    for mt in range(2):
                        v_pe = R("tensor", lambda e, srctok=srctok, mt=mt: mm(
                            e, p1a[:, 0:128], bt2[:, srctok + mt * 128: srctok + (mt + 1) * 128],
                            statb[:, 0:128], True, True), inc="pe")
                        W("vector", "pe", v_pe)
                        v_d = R("vector", lambda e, dst=dst, mt=mt: e.tensor_copy(
                            bt2[:, dst + mt * 128: dst + (mt + 1) * 128], p1a[:, 0:128]),
                            inc="dve")
                        W("tensor", "dve", v_d)
                v_d = R("vector", lambda e: e.tensor_tensor(
                    bt2[:, KBT: KBT + 256], bt2[:, KT: KT + 256],
                    stat[:, 362:363].broadcast_to((128, 256)), ALU.mult), inc="dve")
                v_d = R("vector", lambda e, g=g, h=h: e.tensor_tensor(
                    bt2[:, VB: VB + 256], bt1[:, g * 1024 + h * 256: g * 1024 + (h + 1) * 256],
                    svt[:, g * 4 + h: g * 4 + h + 1].broadcast_to((128, 256)), ALU.mult),
                    inc="dve")
                W("tensor", "dve", v_d)
                for mt in range(2):
                    R("tensor", lambda e, mt=mt: mm(
                        e, p1a[:, 0:128], bt2[:, KN + mt * 128: KN + (mt + 1) * 128],
                        bt2[:, KBF + mt * 128: KBF + (mt + 1) * 128], mt == 0, mt == 1))
                v_pe = 0
                for mt in range(2):
                    v_pe = R("tensor", lambda e, mt=mt: mm(
                        e, p1b[:, 0:128], bt2[:, KBF + mt * 128: KBF + (mt + 1) * 128],
                        bt2[:, KN + mt * 128: KN + (mt + 1) * 128], mt == 0, mt == 1),
                        inc="pe")
                W("vector", "pe", v_pe)
                v_d = R("vector", lambda e: e.tensor_tensor(
                    bt2[:, XP: XP + 128], p1a[:, 0:128], masks[:, 128:256], ALU.mult),
                    inc="dve")
                v_d = R("vector", lambda e: e.tensor_tensor(
                    bt2[:, XL: XL + 128], p1b[:, 0:128], masks[:, 0:128], ALU.mult),
                    inc="dve")
                v_d = R("vector", lambda e: e.tensor_add(
                    bt2[:, TT0: TT0 + 128], bt2[:, XP: XP + 128], ident_b[:]),
                    inc="dve")
                W("tensor", "dve", v_d)
                for lvl in range(4):
                    last = (lvl == 3)
                    v_pe = R("tensor", lambda e: mm(e, p1a[:, 0:128], bt2[:, XL: XL + 128],
                                                    bt2[:, XP: XP + 128], True, True),
                             inc="pe")
                    v_pe = R("tensor", lambda e: mm(e, p1b[:, 0:128], bt2[:, XP: XP + 128],
                                                    bt2[:, XL: XL + 128], True, True),
                             inc="pe")
                    W("vector", "pe", v_pe)
                    v_d = R("vector", lambda e: e.tensor_copy(bt2[:, XP: XP + 128],
                                                              p1a[:, 0:128]), inc="dve")
                    v_d = R("vector", lambda e: e.tensor_copy(bt2[:, XL: XL + 128],
                                                              p1b[:, 0:128]), inc="dve")
                    W("vector", "dve", v_d)
                    v_d = R("vector", lambda e: e.tensor_add(
                        bt2[:, QB0: QB0 + 128], bt2[:, XL: XL + 128], ident_b[:]),
                        inc="dve")
                    W("tensor", "dve", v_d)
                    v_pe = R("tensor", lambda e: mm(e, p1a[:, 0:128], bt2[:, QB0: QB0 + 128],
                                                    bt2[:, TT0: TT0 + 128], True, True),
                             inc="pe")
                    W("vector", "pe", v_pe)
                    v_d = R("vector", lambda e: e.tensor_copy(bt2[:, TT0: TT0 + 128],
                                                              p1a[:, 0:128]), inc="dve")
                    W("tensor", "dve", v_d)
                for ci in range(4):
                    v_pe = R("tensor", lambda e, ci=ci: mm(
                        e, p1a[0:32, 0:256], bt2[:, TT0 + ci * 32: TT0 + (ci + 1) * 32],
                        bt2[:, VB: VB + 256], True, True), inc="pe")
                    W("vector", "pe", v_pe)
                    v_d = R("vector", lambda e, ci=ci: e.tensor_copy(
                        bt2[0:32, UCMP + ci * 256: UCMP + (ci + 1) * 256], p1a[0:32, 0:256]),
                        inc="dve")
                    W("tensor", "dve", v_d)
                for mt in range(2):
                    v_pe = R("tensor", lambda e, mt=mt: mm(
                        e, p1b[:, 0:128], bt2[:, KBT + mt * 128: KBT + (mt + 1) * 128],
                        bt2[:, TT0: TT0 + 128], True, True), inc="pe")
                    W("vector", "pe", v_pe)
                    v_d = R("vector", lambda e, mt=mt: e.tensor_copy(
                        bt2[:, WFM + mt * 128: WFM + (mt + 1) * 128], p1b[:, 0:128]),
                        inc="dve")
                    W("tensor", "dve", v_d)
                for ci in range(4):
                    v_pe = 0
                    for mt in range(2):
                        v_pe = R("tensor", lambda e, ci=ci, mt=mt: mm(
                            e, p1b[0:32, 0:32],
                            bt2[:, KN + mt * 128 + ci * 32: KN + mt * 128 + ci * 32 + 32],
                            bt2[:, QN + mt * 128 + ci * 32: QN + mt * 128 + ci * 32 + 32],
                            mt == 0, mt == 1), inc="pe" if mt == 1 else None)
                    W("vector", "pe", v_pe)
                    v_d = R("vector", lambda e, ci=ci: e.tensor_tensor(
                        bt2[0:32, ACM + ci * 32: ACM + (ci + 1) * 32], p1b[0:32, 0:32],
                        masks[0:32, 384:416], ALU.mult), inc="dve")
                    W("tensor", "dve", v_d)
                for ci in range(4):
                    oi = ci * 32
                    for mt in range(2):
                        vp = R("tensor", lambda e, mt=mt, oi=oi: e.transpose(
                            ptr[0:32, 0:128], bt2[:, KN + mt * 128 + oi: KN + mt * 128 + oi + 32],
                            ident_b[:]), inc="pe")
                        W("vector", "pe", vp)
                        v_d = R("vector", lambda e, mt=mt: e.tensor_copy(
                            statb[0:32, 384 + mt * 128: 384 + (mt + 1) * 128],
                            ptr[0:32, 0:128]), inc="dve")
                        W("tensor", "dve", v_d)
                    v_pe = 0
                    for mt in range(2):
                        v_pe = R("tensor", lambda e, mt=mt, oi=oi: mm(
                            e, p1a[0:32, 0:256],
                            bt2[:, WFM + mt * 128 + oi: WFM + mt * 128 + oi + 32],
                            svt[:, 64 + mt * 256: 64 + (mt + 1) * 256], mt == 0, mt == 1),
                            inc="pe" if mt == 1 else None)
                    W("vector", "pe", v_pe)
                    v_d = R("vector", lambda e, ci=ci: e.tensor_sub(
                        statb[0:32, 640:896],
                        bt2[0:32, UCMP + ci * 256: UCMP + (ci + 1) * 256],
                        p1a[0:32, 0:256]), inc="dve")
                    W("tensor", "dve", v_d)
                    for mt in range(2):
                        R("tensor", lambda e, mt=mt, oi=oi: mm(
                            e, p1b[0:32, 0:256],
                            bt2[:, QN + mt * 128 + oi: QN + mt * 128 + oi + 32],
                            svt[:, 64 + mt * 256: 64 + (mt + 1) * 256], mt == 0, False))
                    v_pe = R("tensor", lambda e, ci=ci: mm(
                        e, p1b[0:32, 0:256], bt2[0:32, ACM + ci * 32: ACM + (ci + 1) * 32],
                        statb[0:32, 640:896], False, True), inc="pe")
                    W("vector", "pe", v_pe)
                    v_d = R("vector", lambda e: e.tensor_copy(
                        bt2[0:32, XP: XP + 256], p1b[0:32, 0:256]), inc="dve")
                    W("tensor", "dve", v_d)
                    for mt in range(2):
                        vp = R("tensor", lambda e, mt=mt: e.transpose(
                            ptr[:, 0:32], bt2[0:32, XP + mt * 128: XP + (mt + 1) * 128],
                            ident_b[0:32, 0:32]), inc="pe")
                        W("vector", "pe", vp)
                        v_d = R("vector", lambda e, mt=mt, t0=t0, oi=oi: e.tensor_copy(
                            bt3[:, 12288 + mt * 2048 + t0 + oi: 12288 + mt * 2048 + t0 + oi + 32],
                            ptr[:, 0:32]), inc="dve")
                        W("tensor", "dve", v_d)
                    v_pe = 0
                    for mt, pp in ((0, p2a), (1, p2b)):
                        R("tensor", lambda e, mt=mt, pp=pp, h=h: mm(
                            e, pp[:, 0:256], wstream[:, 2048 + (h * 2) * 128: 2048 + (h * 2 + 1) * 128],
                            svt[:, 64 + mt * 256: 64 + (mt + 1) * 256], True, False))
                        R("tensor", lambda e, mt=mt, pp=pp, h=h: mm(
                            e, pp[:, 0:256], wstream[:, 2048 + (h * 2 + 1) * 128: 2048 + (h * 2 + 2) * 128],
                            svt[:, 64 + mt * 256: 64 + (mt + 1) * 256], False, False))
                        v_pe = R("tensor", lambda e, mt=mt, pp=pp: mm(
                            e, pp[:, 0:256], statb[0:32, 384 + mt * 128: 384 + (mt + 1) * 128],
                            statb[0:32, 640:896], False, True), inc="pe" if mt == 1 else None)
                    W("scalar", "pe", v_pe)
                    v_a = 0
                    for mt, pp in ((0, p2a), (1, p2b)):
                        v_a = R("scalar", lambda e, mt=mt, pp=pp: e.activation(
                            Sst[:, mt * 256:(mt + 1) * 256], pp[:, 0:256], AF.Copy),
                            inc="act")
                    W("gpsimd", "act", v_a)
                    v_p = 0
                    for mt in range(2):
                        v_p = R("gpsimd", lambda e, mt=mt: e.tensor_copy(
                            svt[:, 64 + mt * 256: 64 + (mt + 1) * 256],
                            Sst[:, mt * 256:(mt + 1) * 256]), inc="pool")
                    W("tensor", "pool", v_p)
                    W("tensor", "act", v_a)
            for i in range(NTT):
                for mt in range(2):
                    W("tensor", "dve", kb.cnt["dve"])
                    vp = R("tensor", lambda e, i=i, mt=mt: e.transpose(
                        ptr[:, 0:128], bt3[:, 12288 + mt * 2048 + i * 128: 12288 + mt * 2048 + (i + 1) * 128],
                        ident_b[:]), inc="pe")
                    W("vector", "pe", vp)
                    v_d = R("vector", lambda e, mt=mt: e.tensor_copy(
                        svt[:, 576 + mt * 128: 576 + (mt + 1) * 128], ptr[:, 0:128]),
                        inc="dve")
                    W("tensor", "dve", v_d)
                emit_stats_tile(2, h, i, svt[:, 576:832])
                W("gpsimd", "dve", kb.cnt["dve"])
                v_sp = kb.dma(scr_delta_b[i * 128:(i + 1) * 128, h * 256:(h + 1) * 256],
                              svt[:, 576:832], sem="dmc", queue="gpsimd")
                W("vector", "dmc", v_sp)

        def emit_attn_b_gate():
            W("tensor", "dve", kb.cnt["dve"])
            W("vector", "pe", kb.cnt["pe"])
            W("vector", "act", kb.cnt["act"])
            W("vector", "pool", kb.cnt["pool"])
            for i in range(NTT):
                vp = R("tensor", lambda e, i=i: e.transpose(
                    ptr[0:64, 0:128], stats_tok[:, i * 64:(i + 1) * 64], ident_b[:]),
                    inc="pe")
                W("vector", "pe", vp)
                v_d = R("vector", lambda e, i=i: e.tensor_copy(
                    bt2[0:64, i * 128:(i + 1) * 128], ptr[0:64, 0:128]), inc="dve")
                W("tensor", "dve", v_d)
            W("gpsimd", "dve", kb.cnt["dve"])
            v_sp = kb.dma(scr_dbg3, bt2[0:64, 0:2048], sem="dmc", queue="gpsimd")
            W("vector", "dmc", v_sp)
            W("sync", "dve", kb.cnt["dve"])
            kb.dma(wstage[0:1, 0:4], log_temp_d, sem="dmb")
            kb.dma(wstage[0:1, 16:32], floor_d, sem="dmb")
            v_l = kb.dma(wstage[0:1, 32:48], b2_d, sem="dmb")
            W("vector", "dmb", v_l)
            W("scalar", "dmb", v_l)
            v_a = R("scalar", lambda e: e.activation(stat[0:1, 368:372], wstage[0:1, 0:4],
                                                     AF.Exp, scale=-1.0), inc="act")
            v_a = R("scalar", lambda e: e.activation(stat[0:1, 372:388], wstage[0:1, 16:32],
                                                     AF.Sigmoid), inc="act")
            W("vector", "act", v_a)
            v_d = R("vector", lambda e: e.tensor_scalar(
                stat[0:1, 372:388], stat[0:1, 372:388], MAX_FLOOR, None, ALU.mult),
                inc="dve")
            for j4 in range(4):
                v_d = R("vector", lambda e, j4=j4: e.tensor_copy(
                    stat[0:1, 388:404].rearrange("p (x r) -> p x r", r=4)[:, :, j4],
                    stat[0:1, 368:372]), inc="dve")
            W("vector", "dve", v_d)
            v_d = R("vector", lambda e: e.tensor_copy(statb[0:1, 0:16], stat[0:1, 388:404]),
                    inc="dve")
            v_d = R("vector", lambda e: e.tensor_copy(statb[0:1, 16:32], stat[0:1, 372:388]),
                    inc="dve")
            W("tensor", "dve", v_d)
            v_pe = R("tensor", lambda e: mm(e, p1c[:, 0:16], ones_row[0:1, 0:128],
                                            statb[0:1, 0:16], True, True), inc="pe")
            W("vector", "pe", v_pe)
            v_d = R("vector", lambda e: e.tensor_copy(wconv[:, 0:16], p1c[:, 0:16]),
                    inc="dve")
            W("tensor", "dve", v_d)
            v_pe = R("tensor", lambda e: mm(e, p1c[:, 0:16], ones_row[0:1, 0:128],
                                            statb[0:1, 16:32], True, True), inc="pe")
            W("vector", "pe", v_pe)
            v_d = R("vector", lambda e: e.tensor_copy(wconv[:, 16:32], p1c[:, 0:16]),
                    inc="dve")
            W("tensor", "dmb", kb.cnt["dmb"])
            vp = R("tensor", lambda e: e.transpose(p1c[0:16, 0:1], wstage[0:1, 32:48],
                                                   ident_f[0:1, 0:1]), inc="pe")
            W("vector", "pe", vp)
            v_d = R("vector", lambda e: e.tensor_copy(wconv[0:16, 656:657], p1c[0:16, 0:1]),
                    inc="dve")
            W("tensor", "dve", v_d)
            for hb in range(2):
                W("sync", "dve", kb.cnt["dve"])
                v_l = kb.dma(wstage[0:1, 0:1024], b1_d[:, hb * 1024:(hb + 1) * 1024],
                             sem="dmb")
                W("vector", "dmb", v_l)
                v_c = R("vector", lambda e: e.tensor_copy(statb[0:1, 0:1024],
                                                          wstage[0:1, 0:1024]), inc="dve")
                for c8 in range(8):
                    W("tensor", "dve", v_c)
                    vp = R("tensor", lambda e, c8=c8: e.transpose(
                        ptr[:, 0:1], statb[0:1, c8 * 128:(c8 + 1) * 128],
                        ident_b[0:1, 0:1]), inc="pe")
                    W("vector", "pe", vp)
                    v_c = R("vector", lambda e, hb=hb, c8=c8: e.tensor_copy(
                        b1cols[:, hb * 8 + c8: hb * 8 + c8 + 1], ptr[:, 0:1]), inc="dve")
                    W("tensor", "dve", v_c)
            for half in range(2):
                w2_pe = [0, 0]
                for m in range(16):
                    W("vector", "pe", w2_pe[1])
                    W("sync", "dve", kb.cnt["dve"])
                    for kc in range(9):
                        rows = 128 if kc < 8 else 64
                        kb.dma(wstage[0:rows, 0:128] if kc == 0 else wstage[0:rows, 0:128],
                               W1_d[kc * 128: kc * 128 + rows, m * 128:(m + 1) * 128],
                               sem="dmb")
                        W("vector", "dmb", kb.cnt["dmb"])
                        v_c = R("vector", lambda e, kc=kc, rows=rows: e.tensor_copy(
                            wstream[0:rows, 3072 + kc * 128: 3072 + (kc + 1) * 128],
                            wstage[0:rows, 0:128]), inc="dve")
                        W("sync", "dve", kb.cnt["dve"])
                    kb.dma(wstage[:, 128:144], W2_d[m * 128:(m + 1) * 128, :], sem="dmb")
                    W("vector", "dmb", kb.cnt["dmb"])
                    v_c = R("vector", lambda e: e.tensor_copy(
                        wstream[:, 4224:4240], wstage[:, 128:144]), inc="dve")
                    W("tensor", "dve", v_c)
                    for sl2 in range(2):
                        sl = half * 2 + sl2
                        s0 = sl * 512
                        pp = p2a if sl2 == 0 else p2b
                        for kc in range(8):
                            R("tensor", lambda e, kc=kc, m=m, s0=s0, pp=pp: mm(
                                e, pp[:, 0:512], wstream[:, 3072 + kc * 128: 3072 + (kc + 1) * 128],
                                x_ln[:, kc * T + s0: kc * T + s0 + 512], kc == 0, False))
                        v_pe = R("tensor", lambda e, m=m, s0=s0, pp=pp: mm(
                            e, pp[:, 0:512], wstream[0:64, 4096:4224],
                            bt2[0:64, s0: s0 + 512], False, True), inc="pe")
                        W("scalar", "pe", v_pe)
                        W("scalar", "pe", w2_pe[sl2])
                        v_a = R("scalar", lambda e, m=m, sl2=sl2, pp=pp: e.activation(
                            wsmall[:, sl2 * 512:(sl2 + 1) * 512],
                            pp[:, 0:512], AF.Gelu, bias=b1cols[:, m: m + 1]), inc="act")
                        W("tensor", "act", v_a)
                        ppl = p1a if sl2 == 0 else p1b
                        v_pe = R("tensor", lambda e, m=m, sl2=sl2, ppl=ppl: mm(
                            e, ppl[0:16, 0:512], wstream[:, 4224:4240],
                            wsmall[:, sl2 * 512:(sl2 + 1) * 512], m == 0, m == 15),
                            inc="pe")
                        w2_pe[sl2] = v_pe
                        W("vector", "pe", v_pe) if m == 15 else None
                for sl2 in range(2):
                    sl = half * 2 + sl2
                    ppl = p1a if sl2 == 0 else p1b
                    W("vector", "dve", kb.cnt["dve"])
                    v_d = R("vector", lambda e, ppl=ppl: e.scalar_tensor_tensor(
                        stat[0:16, 0:512], ppl[0:16, 0:512], 1.0,
                        wconv[0:16, 656:657].broadcast_to((16, 512)), ALU.mult, ALU.add),
                        inc="dve")
                    W("tensor", "dve", v_d)
                    for st in range(4):
                        i = sl * 4 + st
                        vp = R("tensor", lambda e, st=st, ppl=ppl: e.transpose(
                            p1c[:, 0:16], stat[0:16, st * 128:(st + 1) * 128],
                            ident_f[0:16, 0:16]), inc="pe")
                        W("vector", "pe", vp)
                        v_d = R("vector", lambda e, i=i: e.tensor_copy(
                            ptile[:, i * 16:(i + 1) * 16], p1c[:, 0:16]), inc="dve")
                        W("tensor", "dve", v_d)
            W("gpsimd", "dve", kb.cnt["dve"])
            v_sp = kb.dma(scr_dbg, ptile[:], sem="dmc", queue="gpsimd")
            W("vector", "dmc", v_sp)
            for i in range(NTT):
                o0 = i * 16
                v_d = R("vector", lambda e, o0=o0: e.tensor_mul(
                    ptile[:, o0: o0 + 16], ptile[:, o0: o0 + 16], wconv[:, 0:16]),
                    inc="dve")
                W("vector", "dve", v_d)
                for hh in range(4):
                    c0 = o0 + hh * 4
                    v_d = R("vector", lambda e, c0=c0: e.tensor_reduce(
                        stat[:, 400:401], ptile[:, c0: c0 + 4], AX, ALU.max, negate=True),
                        inc="dve")
                    W("scalar", "dve", v_d)
                    v_a = R("scalar", lambda e, c0=c0: e.activation(
                        ptile[:, c0: c0 + 4], ptile[:, c0: c0 + 4], AF.Exp,
                        bias=stat[:, 400:401], accum_out=stat[:, 401:402]), inc="act")
                    W("vector", "act", v_a)
                    v_d = R("vector", lambda e: e.reciprocal(stat[:, 402:403],
                                                             stat[:, 401:402]), inc="dve")
                    W("vector", "dve", v_d)
                    v_d = R("vector", lambda e, c0=c0: e.tensor_scalar(
                        ptile[:, c0: c0 + 4], ptile[:, c0: c0 + 4], stat[:, 402:403],
                        None, ALU.mult), inc="dve")
                    W("vector", "dve", v_d)
                v_d = R("vector", lambda e, o0=o0: e.tensor_max(
                    ptile[:, o0: o0 + 16], ptile[:, o0: o0 + 16], wconv[:, 16:32]),
                    inc="dve")
                W("vector", "dve", v_d)
                for hh in range(4):
                    c0 = o0 + hh * 4
                    v_d = R("vector", lambda e, c0=c0: e.tensor_reduce(
                        stat[:, 401:402], ptile[:, c0: c0 + 4], AX, ALU.add), inc="dve")
                    W("vector", "dve", v_d)
                    v_d = R("vector", lambda e: e.reciprocal(stat[:, 402:403],
                                                             stat[:, 401:402]), inc="dve")
                    W("vector", "dve", v_d)
                    v_d = R("vector", lambda e, c0=c0: e.tensor_scalar(
                        ptile[:, c0: c0 + 4], ptile[:, c0: c0 + 4], stat[:, 402:403],
                        None, ALU.mult), inc="dve")
                    W("vector", "dve", v_d)
            W("gpsimd", "dve", kb.cnt["dve"])
            v_sp = kb.dma(scr_dbg2, ptile[:], sem="dmc", queue="gpsimd")
            W("vector", "dmc", v_sp)
            W("sync", "dve", kb.cnt["dve"])
            v_l = kb.dma(wstage[0:1, 0:256], norm_wd, sem="dmb")
            W("vector", "dmb", v_l)
            v_c = R("vector", lambda e: e.tensor_copy(statb[0:1, 576:832], wstage[0:1, 0:256]),
                    inc="dve")
            W("tensor", "dve", v_c)
            v_pe = R("tensor", lambda e: mm(e, p2a[:, 0:256], ones_row[0:1, 0:128],
                                            statb[0:1, 576:832], True, True), inc="pe")
            W("vector", "pe", v_pe)
            v_d = R("vector", lambda e: e.tensor_copy(statb[:, 0:256], p2a[:, 0:256]),
                    inc="dve")
            W("tensor", "dve", v_d)
            for i in range(NTT):
                W("sync", "dve", kb.cnt["dve"])
                W("sync", "pe", kb.cnt["pe"])
                kb.dma(bt3[:, 0:1024], scr_short_b[i * 128:(i + 1) * 128, :], sem="dmb")
                kb.dma(bt3[:, 1024:2048], scr_long_b[i * 128:(i + 1) * 128, :], sem="dmb")
                v_l = kb.dma(bt3[:, 2048:3072], scr_delta_b[i * 128:(i + 1) * 128, :],
                             sem="dmb")
                W("vector", "dmb", v_l)
                for hh in range(4):
                    d0 = 3072 + hh * 256
                    srcs = (bt3[:, hh * 256:(hh + 1) * 256],
                            bt3[:, 1024 + hh * 256: 1024 + (hh + 1) * 256],
                            bt3[:, 2048 + hh * 256: 2048 + (hh + 1) * 256],
                            bt1[:, i * 1024 + hh * 256: i * 1024 + (hh + 1) * 256])
                    for j4 in range(4):
                        pcol = ptile[:, i * 16 + hh * 4 + j4: i * 16 + hh * 4 + j4 + 1]
                        if j4 == 0:
                            v_d = R("vector", lambda e, d0=d0, src=srcs[0], pcol=pcol:
                                    e.tensor_tensor(bt3[:, d0: d0 + 256], src,
                                                    pcol.broadcast_to((128, 256)), ALU.mult),
                                    inc="dve")
                        else:
                            v_d = R("vector", lambda e, src=srcs[j4], pcol=pcol:
                                    e.tensor_tensor(stat[:, 0:256], src,
                                                    pcol.broadcast_to((128, 256)), ALU.mult),
                                    inc="dve")
                            W("vector", "dve", v_d)
                            v_d = R("vector", lambda e, d0=d0: e.tensor_tensor(
                                bt3[:, d0: d0 + 256], bt3[:, d0: d0 + 256], stat[:, 0:256],
                                ALU.add), inc="dve")
                        W("vector", "dve", v_d)
                    v_d = R("vector", lambda e, d0=d0: e.tensor_tensor(
                        stat[:, 0:256], bt3[:, d0: d0 + 256], bt3[:, d0: d0 + 256],
                        ALU.mult), inc="dve")
                    W("vector", "dve", v_d)
                    v_d = R("vector", lambda e: e.tensor_reduce(
                        stat[:, 400:401], stat[:, 0:256], AX, ALU.add), inc="dve")
                    W("vector", "dve", v_d)
                    v_d = R("vector", lambda e: e.tensor_scalar(
                        stat[:, 400:401], stat[:, 400:401], 1.0 / DV, 1e-5, ALU.mult,
                        ALU.add), inc="dve")
                    W("scalar", "dve", v_d)
                    v_a = R("scalar", lambda e: e.activation(
                        stat[:, 401:402], stat[:, 400:401], AF.Sqrt), inc="act")
                    W("vector", "act", v_a)
                    v_d = R("vector", lambda e: e.reciprocal(stat[:, 402:403],
                                                             stat[:, 401:402]), inc="dve")
                    W("vector", "dve", v_d)
                    v_d = R("vector", lambda e, d0=d0: e.tensor_tensor(
                        bt3[:, d0: d0 + 256], bt3[:, d0: d0 + 256],
                        stat[:, 402:403].broadcast_to((128, 256)), ALU.mult), inc="dve")
                    W("vector", "dve", v_d)
                    v_d = R("vector", lambda e, d0=d0: e.tensor_tensor(
                        bt3[:, d0: d0 + 256], bt3[:, d0: d0 + 256], statb[:, 0:256],
                        ALU.mult), inc="dve")
                    W("vector", "dve", v_d)
                for c in range(8):
                    W("tensor", "dve", kb.cnt["dve"])
                    vp = R("tensor", lambda e, c=c: e.transpose(
                        ptr[:, 0:128], bt3[:, 3072 + c * 128: 3072 + (c + 1) * 128],
                        ident_b[:]), inc="pe")
                    W("vector", "pe", vp)
                    v_d = R("vector", lambda e, c=c, i=i: e.tensor_copy(
                        x_ln[:, c * T + i * 128: c * T + i * 128 + 128], ptr[:, 0:128]),
                        inc="dve")
                    W("tensor", "dve", v_d)
            for o in range(8):
                W("sync", "dve", kb.cnt["dve"])
                for kc in range(8):
                    kb.dma(wstage[:, 0:128] if kc == 0 else wstage[:, 0:128],
                           Wo_d[kc * 128:(kc + 1) * 128, o * 128:(o + 1) * 128], sem="dmb")
                    W("vector", "dmb", kb.cnt["dmb"])
                    v_c = R("vector", lambda e, kc=kc: e.tensor_copy(
                        wstream[:, 3072 + kc * 128: 3072 + (kc + 1) * 128],
                        wstage[:, 0:128]), inc="dve")
                    W("sync", "dve", kb.cnt["dve"])
                W("tensor", "dve", kb.cnt["dve"])
                for s0 in range(0, T, 512):
                    v_pe = 0
                    for kc in range(8):
                        v_pe = R("tensor", lambda e, kc=kc, s0=s0: mm(
                            e, p2a[:, 0:512], wstream[:, 3072 + kc * 128: 3072 + (kc + 1) * 128],
                            x_ln[:, kc * T + s0: kc * T + s0 + 512], kc == 0, kc == 7),
                            inc="pe")
                    W("vector", "pe", v_pe)
                    W("vector", "dve", kb.cnt["dve"])
                    v_d = R("vector", lambda e, o=o, s0=s0: e.tensor_tensor(
                        x_res[:, o * T + s0: o * T + s0 + 512],
                        x_res[:, o * T + s0: o * T + s0 + 512], p2a[:, 0:512], ALU.add),
                        inc="dve")
                    W("vector", "dve", v_d)
                    W("tensor", "dve", v_d)

        def prep_conv_w():
            W("sync", "dve", kb.cnt["dve"])
            kb.dma(wstage[:, 560:592].rearrange("p (c k) -> p c k", k=4),
                   conv_qd.rearrange("(c p) k -> p c k", p=128), sem="dmb")
            kb.dma(wstage[:, 592:624].rearrange("p (c k) -> p c k", k=4),
                   conv_kd.rearrange("(c p) k -> p c k", p=128), sem="dmb")
            v_l = kb.dma(wstage[:, 624:656].rearrange("p (c k) -> p c k", k=4),
                         conv_vd.rearrange("(c p) k -> p c k", p=128), sem="dmb")
            W("vector", "dmb", v_l)
            v_c = R("vector", lambda e: e.tensor_copy(wconv[:, 560:656], wstage[:, 560:656]),
                    inc="dve")
            return v_c

        if stage >= 4:
            prep_conv_w()
            emit_attn_b(stage)
        if stage >= 7:
            emit_ln_slab()
            emit_ffn()

        W("gpsimd", "dve", kb.cnt["dve"])
        W("gpsimd", "pe", kb.cnt["pe"])
        W("gpsimd", "act", kb.cnt["act"])
        for c in range(8):
            kb.dma(out_fm.rearrange("(c p) t -> c p t", p=128)[c],
                   x_res[:, c * T:(c + 1) * T], sem="dmo", queue="gpsimd")
        W("gpsimd", "dmo", kb.cnt["dmo"])

        block = ctx.enter_context(nc.Block())
        block.sync(lambda e: [fn(e) for fn in kb.q["sync"]])
        block.tensor(lambda e: [fn(e) for fn in kb.q["tensor"]])
        block.vector(lambda e: [fn(e) for fn in kb.q["vector"]])
        block.scalar(lambda e: [fn(e) for fn in kb.q["scalar"]])
        block.gpsimd(lambda e: [fn(e) for fn in kb.q["gpsimd"]])

    return nc


def _layout_inputs(inputs, core):
    d = {}
    d["x_fm"] = np.ascontiguousarray(np.asarray(inputs["L0_attn_a_x"])[core].T)
    d["w_qkv"] = inputs["L0_attn_a_w_qkv"]
    d["w_out_a"] = inputs["L0_attn_a_w_out"]
    d["b_out_a"] = np.asarray(inputs["L0_attn_a_b_out"]).reshape(1, D)
    d["Wq"] = inputs["L0_attn_b_Wq"]; d["Wk"] = inputs["L0_attn_b_Wk"]; d["Wv"] = inputs["L0_attn_b_Wv"]
    d["Wb"] = inputs["L0_attn_b_Wb"]
    d["forget"] = np.asarray(inputs["L0_attn_b_forget_param"]).reshape(1, H_B)
    d["conv_q"] = inputs["L0_attn_b_conv_q"]; d["conv_k"] = inputs["L0_attn_b_conv_k"]
    d["conv_v"] = inputs["L0_attn_b_conv_v"]
    d["fir_s"] = np.asarray(inputs["L0_attn_b_fir_short_w"]).reshape(D, 3)
    d["fir_l"] = np.asarray(inputs["L0_attn_b_fir_long_w"]).reshape(D, KLONG)
    d["log_temp"] = np.asarray(inputs["L0_attn_b_log_temp"]).reshape(1, H_B)
    d["floor_p"] = np.asarray(inputs["L0_attn_b_floor_param"]).reshape(1, H_B * 4)
    d["W1"] = inputs["L0_attn_b_W1"]; d["b1"] = np.asarray(inputs["L0_attn_b_b1"]).reshape(1, DMID)
    d["W2"] = inputs["L0_attn_b_W2"]; d["b2"] = np.asarray(inputs["L0_attn_b_b2"]).reshape(1, H_B * 4)
    d["norm_w"] = np.asarray(inputs["L0_attn_b_norm_w"]).reshape(1, DV)
    d["Wo"] = inputs["L0_attn_b_Wo"]
    d["W_lin"] = inputs["L0_ffn_W_lin"]
    d["b_lin"] = np.asarray(inputs["L0_ffn_b_lin"]).reshape(1, D)
    d["gate_w"] = inputs["L0_ffn_gate_w"]
    d["lora_l"] = inputs["L0_ffn_left"]
    d["lora_r"] = np.asarray(inputs["L0_ffn_right"]).reshape(ER, D)
    for k, v in CONSTS.items():
        d["c_" + k] = v
    return {k: np.ascontiguousarray(np.asarray(v, dtype=np.float32)) for k, v in d.items()}


def kernel(**inputs):
    nc = build_program(stage=KSTAGE)
    in_maps = [_layout_inputs(inputs, core) for core in range(NCORE)]
    res = run_bass_kernel_spmd(nc, in_maps, core_ids=list(range(NCORE)))
    out = np.stack([np.asarray(r["out_fm"]).T for r in res.results], axis=0)
    return np.ascontiguousarray(out.astype(np.float32))


# version 05
# speedup vs baseline: 1.0607x; 1.0173x over previous
import contextlib
import math
import numpy as np

import concourse.bass as bass
import concourse.mybir as mybir
from concourse.bass_utils import run_bass_kernel_spmd

F32 = mybir.dt.float32
BF16 = mybir.dt.bfloat16
AF = mybir.ActivationFunctionType
ALU = mybir.AluOpType
AX = mybir.AxisListType.X

D = 1024
T = 2048
NCORE = 8
NH_A = 16
DH_A = 64
SCALE_A = DH_A ** -0.5
H_B = 4
DK = 256
DV = 256
CHUNK = 32
NCHUNK = T // CHUNK
GRP = 4
NGRP = NCHUNK // GRP
FORGET_MIN = 0.55
MAX_FLOOR = 0.075
E_FF = 8
R_FF = 4
ER = E_FF * R_FF
DMID = 2048
GATE_IN = D + 16 * H_B
LFT = 256
KLONG = 63
NTT = T // 128
KSTAGE = 99
SUBSTAGE = 99
DBG_SPILLS = False


def _host_constants():
    c = {}
    c["ident"] = np.eye(128, dtype=np.float32)
    i32 = np.arange(32)
    sl = (i32[:, None] > i32[None, :]).astype(np.float32)
    il = (i32[:, None] >= i32[None, :]).astype(np.float32)
    msl = np.zeros((128, 128), np.float32)
    mil = np.zeros((128, 128), np.float32)
    for b in range(4):
        msl[b * 32:(b + 1) * 32, b * 32:(b + 1) * 32] = sl
        mil[b * 32:(b + 1) * 32, b * 32:(b + 1) * 32] = il
    c["mask_sl"] = -msl
    c["mask_su"] = np.ascontiguousarray(-msl.T)
    c["mask_il"] = mil
    c["mask_iu"] = np.ascontiguousarray(mil.T)
    L = LFT
    s = np.arange(L)
    f = np.arange(128)
    ang = 2.0 * np.pi * np.outer(s, f) / L
    fwdA = np.cos(ang)
    fwdB = np.zeros((L, 128))
    fwdB[:, 0] = np.cos(np.pi * s)
    fwdB[:, 1:] = -np.sin(ang[:, 1:])
    c["dft_fwdA"] = fwdA.astype(np.float32)
    c["dft_fwdB"] = fwdB.astype(np.float32)
    tau = np.arange(128, 256)
    angi = 2.0 * np.pi * np.outer(f, tau) / L
    invA = np.zeros((128, 128)); invA[0] = 1.0 / L
    invA[1:] = 2.0 / L * np.cos(angi[1:])
    invB = np.zeros((128, 128)); invB[0] = np.cos(np.pi * tau) / L
    invB[1:] = -2.0 / L * np.sin(angi[1:])
    c["dft_invA"] = invA.astype(np.float32)
    c["dft_invB"] = invB.astype(np.float32)
    j = np.arange(KLONG)
    lag = (KLONG - 1) - j
    angh = 2.0 * np.pi * np.outer(lag, f) / L
    hA = np.cos(angh)
    hB = np.zeros((KLONG, 128))
    hB[:, 0] = np.cos(np.pi * lag)
    hB[:, 1:] = -np.sin(angh[:, 1:])
    c["dft_hA"] = hA.astype(np.float32)
    c["dft_hB"] = hB.astype(np.float32)
    return c


CONSTS = _host_constants()


class KB:

    def __init__(self, nc, ctx):
        self.nc = nc
        self.ctx = ctx
        self.q = {"sync": [], "tensor": [], "vector": [], "scalar": [], "gpsimd": []}
        self.sems = {}
        self.cnt = {}
        for s in ("dma", "dmb", "dmc", "dmo", "pe", "dve", "act", "pool"):
            self.sems[s] = ctx.enter_context(nc.semaphore(s))
            self.cnt[s] = 0
        self.lastw = {e: {} for e in self.q}

    def on(self, engine, fn):
        self.q[engine].append(fn)

    def wait(self, engine, sem_name, v=None):
        if v is None:
            v = self.cnt[sem_name]
        if v <= 0:
            return
        if self.lastw[engine].get(sem_name, -1) >= v:
            return
        self.lastw[engine][sem_name] = v
        sem = self.sems[sem_name]
        self.on(engine, lambda e, sm=sem, vv=v: e.wait_ge(sm, vv))

    def run(self, engine, fn, inc=None, n=1):
        if inc is None:
            self.on(engine, fn)
            return self.cnt.get(engine, 0)
        self.cnt[inc] += n
        v = self.cnt[inc]
        sem = self.sems[inc]
        def wrapped(e, fn=fn, sem=sem, n=n):
            fn(e).then_inc(sem, n)
        self.on(engine, wrapped)
        return v

    def dma(self, dst_ap, src_ap, sem="dma", queue="sync", slow=False):
        self.cnt[sem] += 16
        v = self.cnt[sem]
        s = self.sems[sem]
        if slow:
            self.on(queue, lambda e, d=dst_ap, sp=src_ap, sm=s:
                    e.dma_start(out=d, in_=sp, allow_slow_non_contiguous=True).then_inc(sm, 16))
        else:
            self.on(queue, lambda e, d=dst_ap, sp=src_ap, sm=s:
                    e.dma_start(out=d, in_=sp).then_inc(sm, 16))
        return v

    def fence(self, engine, sem="dve" if True else None):
        s = {"vector": "dve", "scalar": "act", "gpsimd": "pool"}[engine]
        self.cnt[s] += 1
        v = self.cnt[s]
        sem_h = self.sems[s]
        self.on(engine, lambda e, sm=sem_h, vv=v: e.nop().then_inc(sm, vv - vv + 1))
        self.on(engine, lambda e, sm=sem_h, vv=v: e.wait_ge(sm, vv))
        return v


def ceil_div(a, b):
    return (a + b - 1) // b


def build_program(stage=99):
    nc = bass.Bass("TRN2", target_bir_lowering=False)

    din = {}
    def dt_in(name, shape):
        din[name] = nc.dram_tensor(name, list(shape), F32, kind="ExternalInput").ap()
        return din[name]

    x_in = dt_in("x_fm", [D, T])
    w_qkv = dt_in("w_qkv", [3 * D, D])
    w_out_a = dt_in("w_out_a", [D, D])
    b_out_a = dt_in("b_out_a", [1, D])
    Wq_d = dt_in("Wq", [D, D]); Wk_d = dt_in("Wk", [D, D]); Wv_d = dt_in("Wv", [D, D])
    Wb_d = dt_in("Wb", [D, H_B])
    forget_d = dt_in("forget", [1, H_B])
    conv_qd = dt_in("conv_q", [D, 4]); conv_kd = dt_in("conv_k", [D, 4]); conv_vd = dt_in("conv_v", [D, 4])
    fir_sd = dt_in("fir_s", [D, 3])
    fir_ld = dt_in("fir_l", [D, KLONG])
    log_temp_d = dt_in("log_temp", [1, H_B])
    floor_d = dt_in("floor_p", [1, H_B * 4])
    W1_d = dt_in("W1", [GATE_IN, DMID])
    b1_d = dt_in("b1", [1, DMID])
    W2_d = dt_in("W2", [DMID, H_B * 4])
    b2_d = dt_in("b2", [1, H_B * 4])
    norm_wd = dt_in("norm_w", [1, DV])
    Wo_d = dt_in("Wo", [D, D])
    W_lin_d = dt_in("W_lin", [D, D])
    b_lin_d = dt_in("b_lin", [1, D])
    gate_wd = dt_in("gate_w", [E_FF, D])
    lora_ld = dt_in("lora_l", [E_FF, D, R_FF])
    lora_rd = dt_in("lora_r", [ER, D])
    for k, v in CONSTS.items():
        dt_in("c_" + k, v.shape)

    out_fm = nc.dram_tensor("out_fm", [D, T], F32, kind="ExternalOutput").ap()
    scr_small = nc.dram_tensor("scr_small", [64, 64], F32, kind="Internal").ap()
    scr_short = nc.dram_tensor("scr_short", [T, D], F32, kind="Internal").ap()
    _spill_kind = "ExternalOutput" if DBG_SPILLS else "Internal"
    scr_short_b = nc.dram_tensor("scr_short_b", [T, D], BF16, kind=_spill_kind).ap()
    scr_long_b = nc.dram_tensor("scr_long_b", [T, D], BF16, kind=_spill_kind).ap()
    scr_delta_b = nc.dram_tensor("scr_delta_b", [T, D], BF16, kind=_spill_kind).ap()
    scr_dbg = nc.dram_tensor("scr_dbg", [128, 256], F32, kind=_spill_kind).ap()
    scr_dbg2 = nc.dram_tensor("scr_dbg2", [128, 256], F32, kind=_spill_kind).ap()
    scr_dbg3 = nc.dram_tensor("scr_dbg3", [64, 2048], BF16, kind=_spill_kind).ap()

    with contextlib.ExitStack() as ctx:
        kb = KB(nc, ctx)
        SB = lambda name, shape, dt=BF16: ctx.enter_context(nc.sbuf_tensor(name, shape, dt))

        x_res = SB("x_res", [128, 8 * T], F32)
        x_ln = SB("x_ln", [128, 8 * T], BF16)
        bt1 = SB("bt1", [128, 16 * 1024], BF16)
        bt2 = SB("bt2", [128, 4 * 1024], BF16)
        bt3 = SB("bt3", [128, 16 * 1024], BF16)
        wstream = SB("wstream", [128, 6 * 1024], BF16)
        wsmall = SB("wsmall", [128, 1024], BF16)
        wstage = SB("wstage", [128, 1024], F32)
        ident_b = SB("ident_b", [128, 128], BF16)
        ident_f = SB("ident_f", [128, 128], F32)
        masks = SB("masks", [128, 4 * 128], BF16)
        stat = SB("stat", [128, 512], F32)
        statb = SB("statb", [128, 1024], BF16)
        svt = SB("svt", [128, 16 * 65], BF16)
        bcols = SB("bcols", [128, 64], F32)
        b1cols = SB("b1cols", [128, 16], F32)
        Sst = SB("Sst", [128, 2 * 256], F32)
        stats_tok = SB("stats_tok", [128, 16 * 64], BF16)
        ptile = SB("ptile", [128, 16 * 16], F32)
        wconv = SB("wconv", [128, 660], F32)

        PS = lambda name, shape, dt=F32: ctx.enter_context(nc.psum_tensor(name, shape, dt))
        p2a = PS("p2a", [128, 1024])
        p2b = PS("p2b", [128, 1024])
        p1a = PS("p1a", [128, 512])
        p1b = PS("p1b", [128, 512])
        p1c = PS("p1c", [128, 512])
        ptr = PS("ptr", [128, 256], BF16)

        sems = kb.sems
        W = kb.wait
        R = kb.run

        def mm(e, out, lhsT, rhs, first, last, tp=None):
            return e.matmul(out, lhsT, rhs, start=first, stop=last, tile_position=tp)

        def evac(engine, dst, src, inc):
            return R(engine, lambda e, d=dst, s=src: e.tensor_copy(d, s), inc=inc)

        def transpose_to(dst, src, slot, pe_inc=True):
            v_pe = R("tensor", lambda e, s=src, sl=slot: e.transpose(
                ptr[:, sl * 128: sl * 128 + src.shape[0]], s, ident_b[:]), inc="pe")
            W("vector", "pe", v_pe)
            v_d = evac("vector", dst, ptr[:, slot * 128: slot * 128 + src.shape[0]], "dve")
            return v_pe, v_d

        for c in range(8):
            kb.dma(x_res[:, c * T:(c + 1) * T],
                   x_in.rearrange("(c p) t -> c p t", p=128)[c])
        kb.dma(wstage[:, 0:128], din["c_ident"])
        for i, mname in enumerate(("mask_sl", "mask_su", "mask_il", "mask_iu")):
            kb.dma(wstage[:, 128 * (i + 1):128 * (i + 2)], din["c_" + mname])
        v_x = kb.cnt["dma"]
        W("vector", "dma", v_x)
        R("vector", lambda e: e.tensor_copy(ident_b[:], wstage[:, 0:128]))
        R("vector", lambda e: e.tensor_copy(ident_f[:], wstage[:, 0:128]))
        v_mcast = R("vector", lambda e: e.tensor_copy(masks[:, 0:512], wstage[:, 128:640]),
                    inc="dve")

        def emit_ln_slab():
            sqb = bt2
            W("vector", "pe", kb.cnt["pe"])
            for s0 in range(0, T, 512):
                v_cast = 0
                for c in range(8):
                    v_cast = R("vector", lambda e, c=c, s0=s0: e.tensor_copy(
                        x_ln[:, c * T + s0: c * T + s0 + 512],
                        x_res[:, c * T + s0: c * T + s0 + 512]), inc="dve")
                W("scalar", "dve", v_cast)
                v_sq = 0
                for c in range(8):
                    v_sq = R("scalar", lambda e, c=c, s0=s0: e.activation(
                        sqb[:, c * 512: c * 512 + 512],
                        x_ln[:, c * T + s0: c * T + s0 + 512], AF.Square), inc="act")
                W("tensor", "dve", v_cast)
                W("tensor", "act", v_sq)
                for c in range(8):
                    R("tensor", lambda e, c=c, s0=s0: mm(
                        e, p1a[0:1, 0:512], ones_col[:, 0:1],
                        x_ln[:, c * T + s0: c * T + s0 + 512], c == 0, c == 7))
                v_pe = 0
                for c in range(8):
                    v_pe = R("tensor", lambda e, c=c: mm(
                        e, p1b[0:1, 0:512], ones_col[:, 0:1],
                        sqb[:, c * 512: c * 512 + 512], c == 0, c == 7), inc="pe")
                W("vector", "pe", v_pe)
                R("vector", lambda e: e.tensor_scalar(
                    stat[0:1, 0:512], p1a[0:1, 0:512], 1.0 / D, None, ALU.mult))
                R("vector", lambda e: e.tensor_scalar(
                    wstage[0:1, 0:512], p1b[0:1, 0:512], 1.0 / D, None, ALU.mult))
                R("vector", lambda e: e.tensor_mul(Sst[0:1, 0:512], stat[0:1, 0:512],
                                                   stat[0:1, 0:512]))
                R("vector", lambda e: e.tensor_sub(Sst[0:1, 0:512], wstage[0:1, 0:512],
                                                   Sst[0:1, 0:512]))
                v_d = R("vector", lambda e: e.tensor_scalar(
                    Sst[0:1, 0:512], Sst[0:1, 0:512], 1e-5, None, ALU.add), inc="dve")
                W("scalar", "dve", v_d)
                v_a = R("scalar", lambda e: e.activation(Sst[0:1, 0:512], Sst[0:1, 0:512],
                                                         AF.Sqrt), inc="act")
                W("vector", "act", v_a)
                R("vector", lambda e: e.reciprocal(Sst[0:1, 0:512], Sst[0:1, 0:512]))
                R("vector", lambda e: e.tensor_mul(wstage[0:1, 0:512], stat[0:1, 0:512],
                                                   Sst[0:1, 0:512]))
                R("vector", lambda e: e.tensor_copy(statb[0:1, 0:512], Sst[0:1, 0:512]))
                v_d = R("vector", lambda e: e.tensor_copy(statb[0:1, 512:1024],
                                                          wstage[0:1, 0:512]), inc="dve")
                W("tensor", "dve", v_d)
                R("tensor", lambda e: mm(e, p1b[:, 0:512], ones_row[0:1, 0:128],
                                         statb[0:1, 0:512], True, True))
                v_pe = R("tensor", lambda e: mm(e, p1c[:, 0:512], ones_row[0:1, 0:128],
                                                statb[0:1, 512:1024], True, True), inc="pe")
                W("vector", "pe", v_pe)
                vlast = 0
                for c in range(8):
                    R("vector", lambda e, c=c, s0=s0: e.tensor_tensor(
                        x_ln[:, c * T + s0: c * T + s0 + 512],
                        x_ln[:, c * T + s0: c * T + s0 + 512], p1b[:, 0:512], ALU.mult))
                    vlast = R("vector", lambda e, c=c, s0=s0: e.tensor_tensor(
                        x_ln[:, c * T + s0: c * T + s0 + 512],
                        x_ln[:, c * T + s0: c * T + s0 + 512], p1c[:, 0:512], ALU.subtract),
                        inc="dve")
                W("tensor", "dve", vlast)
            return kb.cnt["dve"]

        ones_col = SB("ones_col", [128, 2], BF16)
        zrow = SB("zrow", [1, 512], BF16)
        ones_row = SB("ones_row", [1, 128], BF16)
        R("vector", lambda e: e.memset(ones_col[:], 1.0))
        R("vector", lambda e: e.memset(zrow[:], 0.0))
        R("vector", lambda e: e.memset(ones_row[:], 1.0))
        R("vector", lambda e: e.memset(svt[:], 1.0))

        W("vector", "dma", v_x)
        if stage >= 1:
            emit_ln_slab()

        def prep_qkvT(hp, slot):
            vs = []
            for j, base in enumerate((0, D, 2 * D)):
                W("sync", "dve", kb.cnt["dve"])
                v_l = kb.dma(wstage[:, 0:1024],
                             w_qkv[base + hp * 128: base + hp * 128 + 128, :], sem="dmb")
                W("vector", "dmb", v_l)
                v_c = R("vector", lambda e: e.tensor_copy(wsmall[:, 0:1024], wstage[:, 0:1024]),
                        inc="dve")
                for c in range(8):
                    W("tensor", "dve", v_c)
                    v_pe = R("tensor", lambda e, c=c: e.transpose(
                        ptr[:, 0:128], wsmall[:, c * 128:(c + 1) * 128], ident_b[:]), inc="pe")
                    W("vector", "pe", v_pe)
                    v_d = R("vector", lambda e, c=c, j=j, slot=slot: e.tensor_copy(
                        wstream[:, slot * 3072 + c * 384 + j * 128: slot * 3072 + c * 384 + j * 128 + 128],
                        ptr[:, 0:128]), inc="dve")
                    W("tensor", "dve", v_d)
                vs.append(v_d)
            return kb.cnt["dve"]

        def prep_bias_cols(dsrc, dst_off, n_chunk):
            W("sync", "dve", kb.cnt["dve"])
            v_l = kb.dma(wstage[0:1, 0:n_chunk * 128], dsrc, sem="dmb")
            W("vector", "dmb", v_l)
            v_c = R("vector", lambda e: e.tensor_copy(statb[0:1, 0:n_chunk * 128],
                                                      wstage[0:1, 0:n_chunk * 128]), inc="dve")
            for c in range(n_chunk):
                W("tensor", "dve", v_c)
                v_pe = R("tensor", lambda e, c=c: e.transpose(
                    ptr[:, 0:1], statb[0:1, c * 128:(c + 1) * 128], ident_b[0:1, 0:1]), inc="pe")
                W("vector", "pe", v_pe)
                v_d = R("vector", lambda e, c=c: e.tensor_copy(
                    bcols[:, dst_off + c: dst_off + c + 1], ptr[:, 0:1]), inc="dve")
                W("tensor", "dve", v_d)
            return kb.cnt["dve"]


        def emit_attn_a(stage=99):
            sub = SUBSTAGE
            prep_bias_cols(b_out_a, 0, 8)
            prep_qkvT(0, 0)
            for hp in range(8 if sub >= 5 else 1):
                slot = hp % 2
                if hp < 7 and sub >= 5:
                    prep_qkvT(hp + 1, (hp + 1) % 2)
                for j in range(3):
                    for s0 in range(0, T, 512):
                        v_pe = 0
                        for c in range(8):
                            v_pe = R("tensor", lambda e, j=j, s0=s0, c=c, slot=slot: mm(
                                e, p2a[:, 0:512],
                                wstream[:, slot * 3072 + c * 384 + j * 128: slot * 3072 + c * 384 + j * 128 + 128],
                                x_ln[:, c * T + s0: c * T + s0 + 512], c == 0, c == 7),
                                inc="pe")
                        W("vector", "pe", v_pe)
                        v_d = R("vector", lambda e, j=j, s0=s0: e.tensor_copy(
                            bt3[:, j * T + s0: j * T + s0 + 512], p2a[:, 0:512]), inc="dve")
                        W("tensor", "dve", v_d)
                if sub < 2:
                    continue
                v_d = kb.cnt["dve"]
                W("vector", "dve", v_d)
                for j in range(2):
                    v_sqd = R("vector", lambda e, j=j: e.tensor_mul(
                        bt2[:, j * T: j * T + T], bt3[:, j * T: j * T + T],
                        bt3[:, j * T: j * T + T]), inc="dve")
                W("tensor", "dve", v_sqd)
                for j in range(2):
                    v_pe2 = 0
                    for s0 in range(0, T, 512):
                        v_pe2 = R("tensor", lambda e, j=j, s0=s0: mm(
                            e, p1a[0:2, 0:512], hones[:, 0:2],
                            bt2[:, j * T + s0: j * T + s0 + 512], True, True), inc="pe")
                        W("vector", "pe", v_pe2)
                        if s0 == 0:
                            v_dm = R("vector", lambda e, j=j: e.tensor_reduce(
                                stat[0:2, 8 + j: 9 + j], p1a[0:2, 0:512], AX, ALU.max),
                                inc="dve")
                        else:
                            v_dm = R("vector", lambda e: e.tensor_reduce(
                                stat[0:2, 10:11], p1a[0:2, 0:512], AX, ALU.max), inc="dve")
                            W("vector", "dve", v_dm)
                            v_dm = R("vector", lambda e, j=j: e.tensor_max(
                                stat[0:2, 8 + j: 9 + j], stat[0:2, 8 + j: 9 + j],
                                stat[0:2, 10:11]), inc="dve")
                        W("tensor", "dve", v_dm)
                v_dm = R("vector", lambda e: e.tensor_mul(
                    stat[0:2, 12:13], stat[0:2, 8:9], stat[0:2, 9:10]), inc="dve")
                W("scalar", "dve", v_dm)
                v_a = R("scalar", lambda e: e.activation(stat[0:2, 13:14], stat[0:2, 12:13],
                                                         AF.Sqrt), inc="act")
                W("vector", "act", v_a)
                v_dm = R("vector", lambda e: e.tensor_scalar(
                    stat[0:2, 14:15], stat[0:2, 13:14], -SCALE_A, None, ALU.mult), inc="dve")
                W("gpsimd", "dve", v_dm)
                v_bo = kb.dma(scr_small[0:2, 0:1], stat[0:2, 14:15], sem="dmc", queue="gpsimd",
                              slow=True)
                W("gpsimd", "dmc", v_bo)
                v_bi = kb.dma(stat[0:1, 16:18], scr_small[0:2, 0:1].rearrange("a b -> b a"),
                              sem="dmc", queue="gpsimd", slow=True)
                if sub < 3:
                    continue
                for par in range(2):
                    h = hp * 2 + par
                    W("vector", "dmc", v_bi)
                    v_c = R("vector", lambda e, par=par: e.tensor_copy(
                        statb[0:1, 16 + par: 17 + par], stat[0:1, 16 + par: 17 + par]),
                        inc="dve")
                    W("tensor", "dve", v_c)
                    v_pe = R("tensor", lambda e, par=par: mm(
                        e, p1c[:, 0:1], ones_row[0:1, 0:128],
                        statb[0:1, 16 + par: 17 + par], True, True), inc="pe")
                    W("vector", "pe", v_pe)
                    v_bias = R("vector", lambda e: e.tensor_copy(stat[:, 20:21], p1c[:, 0:1]),
                               inc="dve")
                    for i in range(NTT):
                        vp = R("tensor", lambda e, i=i, par=par: e.transpose(
                            ptr[:, 0:64], bt3[64 * par:64 * par + 64, 2 * T + i * 128: 2 * T + (i + 1) * 128],
                            ident_b[64 * par:64 * par + 64, 64 * par:64 * par + 64]), inc="pe")
                        W("vector", "pe", vp)
                        vd = R("vector", lambda e, i=i: e.tensor_copy(
                            svt[:, i * 65: i * 65 + 64], ptr[:, 0:64]), inc="dve")
                        W("tensor", "dve", vd)
                    if sub == 31:
                        continue
                    for ppv in (p1a, p1b, p1c):
                        R("tensor", lambda e, ppv=ppv: mm(
                            e, ppv[:, 0:512], ones_row[0:1, 0:128], zrow[0:1, 0:512],
                            True, True))
                    W("scalar", "dve", v_bias)
                    exp_act = {}
                    sc_pe = {}
                    if sub == 31:
                        continue
                    for i in range(NTT):
                        for h2 in range(2):
                            ppx = p2a if h2 == 0 else p2b
                            if i > 0:
                                W("tensor", "act", exp_act[(i - 1, h2)])
                            v_pe = 0
                            for s4 in range(2):
                                v_pe = R("tensor", lambda e, i=i, par=par, h2=h2, s4=s4,
                                         ppx=ppx: mm(
                                    e, ppx[:, s4 * 512:(s4 + 1) * 512],
                                    bt3[64 * par:64 * par + 64, T + i * 128: T + (i + 1) * 128],
                                    bt3[64 * par:64 * par + 64,
                                        h2 * 1024 + s4 * 512: h2 * 1024 + s4 * 512 + 512],
                                    True, True), inc="pe" if s4 == 1 else None)
                            sc_pe[(i, h2)] = v_pe
                        for h2 in range(2):
                            W("scalar", "pe", sc_pe[(i, h2)])
                            ppx = p2a if h2 == 0 else p2b
                            exp_act[(i, h2)] = R("scalar", lambda e, i=i, h2=h2, ppx=ppx:
                                e.activation(
                                    bt2[:, (i % 2) * 2048 + h2 * 1024:
                                        (i % 2) * 2048 + h2 * 1024 + 1024],
                                    ppx[:, 0:1024], AF.Exp, bias=stat[:, 20:21],
                                    scale=SCALE_A), inc="act")
                        if i > 0:
                            W("tensor", "act", exp_act[(i - 1, 1)])
                            if sub != 32:
                                for tq in range(NTT):
                                    bank, sl = divmod(tq, 7)
                                    ppv = (p1a, p1b, p1c)[bank]
                                    R("tensor", lambda e, i=i, tq=tq, ppv=ppv, sl=sl:
                                      e.matmul(
                                        ppv[:, sl * 68: sl * 68 + 65],
                                        bt2[:, ((i - 1) % 2) * 2048 + tq * 128:
                                            ((i - 1) % 2) * 2048 + tq * 128 + 128],
                                        svt[:, (i - 1) * 65: (i - 1) * 65 + 65],
                                        start=False, stop=False, skip_group_check=True))
                    W("tensor", "act", exp_act[(NTT - 1, 1)])
                    v_pv = 0
                    for tq in range(NTT):
                        bank, sl = divmod(tq, 7)
                        ppv = (p1a, p1b, p1c)[bank]
                        inc = "pe" if tq == NTT - 1 else None
                        v = R("tensor", lambda e, tq=tq, ppv=ppv, sl=sl: e.matmul(
                            ppv[:, sl * 68: sl * 68 + 65],
                            bt2[:, ((NTT - 1) % 2) * 2048 + tq * 128:
                                ((NTT - 1) % 2) * 2048 + tq * 128 + 128],
                            svt[:, (NTT - 1) * 65: (NTT - 1) * 65 + 65],
                            start=False, stop=False, skip_group_check=True),
                            inc=inc)
                        if inc and sub == 32:
                            v_pv = v
                        elif inc:
                            v_pv = v
                    W("vector", "pe", v_pv)
                    for tq in range(0 if sub in (32, 33) else NTT):
                        bank, sl = divmod(tq, 7)
                        ppv = (p1a, p1b, p1c)[bank]
                        if sub == 36:
                            v_d = R("vector", lambda e, ppv=ppv: e.tensor_copy(
                                stat[:, 256:320], ppv[:, 0:64]), inc="dve")
                            W("vector", "dve", v_d)
                            continue
                        if sub in (34, 35):
                            v_d = R("vector", lambda e, ppv=ppv, sl=sl: e.tensor_copy(
                                stat[:, 22:23], ppv[:, sl * 68 + 64: sl * 68 + 65]),
                                inc="dve")
                            W("vector", "dve", v_d)
                            v_d = R("vector", lambda e, ppv=ppv, sl=sl: e.tensor_copy(
                                stat[:, 256:320], ppv[:, sl * 68: sl * 68 + 64]), inc="dve")
                            W("vector", "dve", v_d)
                            if sub == 35:
                                v_d = R("vector", lambda e: e.reciprocal(
                                    stat[:, 21:22], stat[:, 22:23]), inc="dve")
                                W("vector", "dve", v_d)
                            continue
                        v_d = R("vector", lambda e, ppv=ppv, sl=sl: e.tensor_copy(
                            stat[:, 22:23], ppv[:, sl * 68 + 64: sl * 68 + 65]), inc="dve")
                        W("vector", "dve", v_d)
                        v_d = R("vector", lambda e: e.reciprocal(
                            stat[:, 21:22], stat[:, 22:23]), inc="dve")
                        W("vector", "dve", v_d)
                        v_d = R("vector", lambda e, ppv=ppv, sl=sl: e.tensor_copy(
                            stat[:, 256:320], ppv[:, sl * 68: sl * 68 + 64]), inc="dve")
                        W("vector", "dve", v_d)
                        v_d = R("vector", lambda e, tq=tq, h=h: e.tensor_scalar(
                            bt1[:, tq * 1024 + h * 64: tq * 1024 + h * 64 + 64],
                            stat[:, 256:320], stat[:, 21:22], None, ALU.mult),
                            inc="dve")
                        W("vector", "dve", v_d)
                    W("tensor", "dve", kb.cnt["dve"])
            if sub < 6:
                return
            v_last = kb.cnt["dve"]
            W("tensor", "dve", v_last)
            for i in range(NTT):
                for c in range(8):
                    vp = R("tensor", lambda e, i=i, c=c: e.transpose(
                        ptr[:, (c % 2) * 128:(c % 2) * 128 + 128],
                        bt1[:, i * 1024 + c * 128: i * 1024 + (c + 1) * 128], ident_b[:]),
                        inc="pe")
                    W("vector", "pe", vp)
                    vd = R("vector", lambda e, i=i, c=c: e.tensor_copy(
                        x_ln[:, c * T + i * 128: c * T + i * 128 + 128],
                        ptr[:, (c % 2) * 128:(c % 2) * 128 + 128]), inc="dve")
                    W("tensor", "dve", vd)
            for o in range(8):
                W("sync", "dve", kb.cnt["dve"])
                v_l = kb.dma(wstage[:, 0:1024], w_out_a[o * 128:(o + 1) * 128, :], sem="dmb")
                W("vector", "dmb", v_l)
                v_c = R("vector", lambda e: e.tensor_copy(wsmall[:, 0:1024], wstage[:, 0:1024]),
                        inc="dve")
                for c in range(8):
                    W("tensor", "dve", v_c)
                    vp = R("tensor", lambda e, c=c: e.transpose(
                        ptr[:, 0:128], wsmall[:, c * 128:(c + 1) * 128], ident_b[:]), inc="pe")
                    W("vector", "pe", vp)
                    vd = R("vector", lambda e, c=c: e.tensor_copy(
                        wstream[:, c * 128:(c + 1) * 128], ptr[:, 0:128]), inc="dve")
                    W("tensor", "dve", vd)
                for s0 in range(0, T, 512):
                    v_pe = 0
                    for c in range(8):
                        v_pe = R("tensor", lambda e, c=c, s0=s0: mm(
                            e, p2a[:, 0:512], wstream[:, c * 128:(c + 1) * 128],
                            x_ln[:, c * T + s0: c * T + s0 + 512], c == 0, c == 7), inc="pe")
                    W("vector", "pe", v_pe)
                    W("vector", "dve", kb.cnt["dve"])
                    v_d = R("vector", lambda e, o=o, s0=s0: e.scalar_tensor_tensor(
                        x_res[:, o * T + s0: o * T + s0 + 512], p2a[:, 0:512],
                        bcols[:, 0 + o: 1 + o], x_res[:, o * T + s0: o * T + s0 + 512],
                        ALU.add, ALU.add), inc="dve")
                    W("vector", "dve", v_d)
                    W("tensor", "dve", v_d)

        hones = SB("hones", [128, 2], BF16)
        R("vector", lambda e: e.memset(hones[:], 0.0))
        R("vector", lambda e: e.memset(hones[0:64, 0:1], 1.0))
        R("vector", lambda e: e.memset(hones[64:128, 1:2], 1.0))

        if stage >= 2:
            emit_attn_a(stage)

        def prep_ffn_weights():
            prep_bias_cols(b_lin_d, 8, 8)
            W("sync", "dve", kb.cnt["dve"])
            v_l = kb.dma(wstage[0:8, 0:1024], gate_wd, sem="dmb")
            W("vector", "dmb", v_l)
            v_c = R("vector", lambda e: e.tensor_copy(statb[0:8, 0:1024],
                                                      wstage[0:8, 0:1024]), inc="dve")
            for c in range(8):
                W("tensor", "dve", v_c)
                vp = R("tensor", lambda e, c=c: e.transpose(
                    ptr[:, 0:8], statb[0:8, c * 128:(c + 1) * 128],
                    ident_b[0:8, 0:8]), inc="pe")
                W("vector", "pe", vp)
                v_c2 = R("vector", lambda e, c=c: e.tensor_copy(
                    wsmall[:, c * 8:(c + 1) * 8], ptr[:, 0:8]), inc="dve")
                W("tensor", "dve", v_c2)
            W("sync", "dve", kb.cnt["dve"])
            for e_i in range(E_FF):
                for c in range(8):
                    kb.dma(wstage[:, c * 32 + 4 * e_i: c * 32 + 4 * e_i + 4],
                           lora_ld[e_i, c * 128:(c + 1) * 128, :], sem="dmb")
            v_l = kb.cnt["dmb"]
            W("vector", "dmb", v_l)
            v_c = R("vector", lambda e: e.tensor_copy(wsmall[:, 64:320], wstage[:, 0:256]),
                    inc="dve")
            W("sync", "dve", kb.cnt["dve"])
            v_l = kb.dma(wstage[0:32, 0:1024], lora_rd, sem="dmb")
            W("vector", "dmb", v_l)
            v_c = R("vector", lambda e: e.tensor_copy(bt3[0:32, 2048:3072],
                                                      wstage[0:32, 0:1024]), inc="dve")
            return v_c

        def emit_ffn():
            prep_ffn_weights()
            for sl4 in range(4):
                v_pe = 0
                for c in range(8):
                    v_pe = R("tensor", lambda e, c=c, sl4=sl4: mm(
                        e, p1a[0:8, 0:512], wsmall[:, c * 8:(c + 1) * 8],
                        x_ln[:, c * T + sl4 * 512: c * T + sl4 * 512 + 512],
                        c == 0, c == 7), inc="pe")
                W("vector", "pe", v_pe)
                v_d = R("vector", lambda e: e.tensor_copy(stat[0:8, 0:512], p1a[0:8, 0:512]),
                        inc="dve")
                W("tensor", "dve", v_d)
                for st in range(4):
                    i = sl4 * 4 + st
                    vp = R("tensor", lambda e, st=st: e.transpose(
                        p1b[:, 0:8], stat[0:8, st * 128:(st + 1) * 128],
                        ident_f[0:8, 0:8]), inc="pe")
                    W("vector", "pe", vp)
                    v_d = R("vector", lambda e, i=i: e.tensor_copy(
                        ptile[:, i * 16: i * 16 + 8], p1b[:, 0:8]), inc="dve")
                    W("tensor", "dve", v_d)
            for i in range(NTT):
                o0 = i * 16
                v_d = R("vector", lambda e, o0=o0: e.tensor_reduce(
                    ptile[:, o0 + 8: o0 + 9], ptile[:, o0: o0 + 8], AX, ALU.max,
                    negate=True), inc="dve")
                W("scalar", "dve", v_d)
                v_a = R("scalar", lambda e, o0=o0: e.activation(
                    ptile[:, o0: o0 + 8], ptile[:, o0: o0 + 8], AF.Exp,
                    bias=ptile[:, o0 + 8: o0 + 9], scale=1.0,
                    accum_out=ptile[:, o0 + 9: o0 + 10]), inc="act")
                W("vector", "act", v_a)
                v_d = R("vector", lambda e, o0=o0: e.reciprocal(
                    ptile[:, o0 + 10: o0 + 11], ptile[:, o0 + 9: o0 + 10]), inc="dve")
                W("vector", "dve", v_d)
                v_d = R("vector", lambda e, o0=o0: e.tensor_scalar(
                    ptile[:, o0: o0 + 8], ptile[:, o0: o0 + 8],
                    ptile[:, o0 + 10: o0 + 11], None, ALU.mult), inc="dve")
                W("vector", "dve", v_d)
                v_d = R("vector", lambda e, o0=o0: e.tensor_reduce(
                    stat[:, 8:9], ptile[:, o0: o0 + 8], AX, ALU.max), inc="dve")
                W("vector", "dve", v_d)
                v_d = R("vector", lambda e, o0=o0: e.tensor_scalar(
                    stat[:, 0:8], ptile[:, o0: o0 + 8], stat[:, 8:9], None,
                    ALU.is_equal), inc="dve")
                W("vector", "dve", v_d)
                v_d = R("vector", lambda e, o0=o0: e.scalar_tensor_tensor(
                    stat[:, 0:8], stat[:, 0:8], -1e30, ptile[:, o0: o0 + 8],
                    ALU.mult, ALU.add), inc="dve")
                W("vector", "dve", v_d)
                v_d = R("vector", lambda e: e.tensor_reduce(
                    stat[:, 9:10], stat[:, 0:8], AX, ALU.max), inc="dve")
                W("vector", "dve", v_d)
                v_d = R("vector", lambda e, o0=o0: e.tensor_scalar(
                    stat[:, 0:8], ptile[:, o0: o0 + 8], stat[:, 9:10], None,
                    ALU.is_ge), inc="dve")
                W("vector", "dve", v_d)
                v_d = R("vector", lambda e, o0=o0: e.tensor_mul(
                    stat[:, 0:8], stat[:, 0:8], ptile[:, o0: o0 + 8]), inc="dve")
                W("vector", "dve", v_d)
                for r in range(4):
                    v_d = R("vector", lambda e, r=r: e.tensor_copy(
                        statb[:, 32:64].rearrange("p (x r) -> p x r", r=4)[:, :, r],
                        stat[:, 0:8]), inc="dve")
                W("tensor", "dve", v_d)
                vp = R("tensor", lambda e: e.transpose(
                    ptr[0:32, 0:128], statb[:, 32:64], ident_b[:]), inc="pe")
                W("vector", "pe", vp)
                v_d = R("vector", lambda e, i=i: e.tensor_copy(
                    bt2[0:32, 2048 + i * 128: 2048 + (i + 1) * 128], ptr[0:32, 0:128]),
                    inc="dve")
                W("tensor", "dve", v_d)
            for sl4 in range(4):
                v_pe = 0
                for c in range(8):
                    v_pe = R("tensor", lambda e, c=c, sl4=sl4: mm(
                        e, p1b[0:32, 0:512], wsmall[:, 64 + c * 32: 64 + (c + 1) * 32],
                        x_ln[:, c * T + sl4 * 512: c * T + sl4 * 512 + 512],
                        c == 0, c == 7), inc="pe")
                W("vector", "pe", v_pe)
                v_d = R("vector", lambda e, sl4=sl4: e.tensor_copy(
                    bt3[0:32, sl4 * 512:(sl4 + 1) * 512], p1b[0:32, 0:512]), inc="dve")
                W("tensor", "dve", v_d)
            v_th = R("vector", lambda e: e.tensor_mul(
                bt3[0:32, 0:2048], bt3[0:32, 0:2048], bt2[0:32, 2048:4096]), inc="dve")
            W("tensor", "dve", v_th)
            for o in range(8):
                W("sync", "dve", kb.cnt["dve"])
                v_l = kb.dma(wstage[:, 0:1024], W_lin_d[o * 128:(o + 1) * 128, :], sem="dmb")
                W("vector", "dmb", v_l)
                v_c = R("vector", lambda e: e.tensor_copy(wsmall[:, 320:320 + 1024] if False
                                                          else statb[:, 0:1024],
                                                          wstage[:, 0:1024]), inc="dve")
                for c in range(8):
                    W("tensor", "dve", v_c)
                    vp = R("tensor", lambda e, c=c: e.transpose(
                        ptr[:, 0:128], statb[:, c * 128:(c + 1) * 128], ident_b[:]),
                        inc="pe")
                    W("vector", "pe", vp)
                    vd = R("vector", lambda e, c=c: e.tensor_copy(
                        wstream[:, c * 128:(c + 1) * 128], ptr[:, 0:128]), inc="dve")
                    W("tensor", "dve", vd)
                for s0 in range(0, T, 512):
                    v_pe = 0
                    for c in range(8):
                        R("tensor", lambda e, c=c, s0=s0: mm(
                            e, p2a[:, 0:512], wstream[:, c * 128:(c + 1) * 128],
                            x_ln[:, c * T + s0: c * T + s0 + 512], c == 0, False))
                    v_pe = R("tensor", lambda e, o=o, s0=s0: mm(
                        e, p2a[:, 0:512], bt3[0:32, 2048 + o * 128: 2048 + (o + 1) * 128],
                        bt3[0:32, s0: s0 + 512], False, True), inc="pe")
                    W("vector", "pe", v_pe)
                    W("vector", "dve", kb.cnt["dve"])
                    v_d = R("vector", lambda e, o=o, s0=s0: e.scalar_tensor_tensor(
                        x_res[:, o * T + s0: o * T + s0 + 512], p2a[:, 0:512],
                        bcols[:, 8 + o: 9 + o], x_res[:, o * T + s0: o * T + s0 + 512],
                        ALU.add, ALU.add), inc="dve")
                    W("vector", "dve", v_d)
                    W("tensor", "dve", v_d)

        if stage >= 3:
            emit_ln_slab()
            emit_ffn()
        if stage >= 4:
            emit_ln_slab()


        def emit_attn_b(stage=99):
            W("sync", "dve", kb.cnt["dve"])
            v_l = kb.dma(wstage[:, 0:32].rearrange("p (c k) -> p c k", k=4),
                         Wb_d.rearrange("(c p) k -> p c k", p=128), sem="dmb")
            W("vector", "dmb", v_l)
            v_c = R("vector", lambda e: e.tensor_copy(wsmall[:, 0:32], wstage[:, 0:32]),
                    inc="dve")
            W("sync", "dve", kb.cnt["dve"])
            v_l = kb.dma(wstage[:, 32:56].rearrange("p (c k) -> p c k", k=3),
                         fir_sd.rearrange("(c p) k -> p c k", p=128), sem="dmb")
            v_l = kb.dma(wstage[:, 56:560].rearrange("p (c k) -> p c k", k=63),
                         fir_ld.rearrange("(c p) k -> p c k", p=128), sem="dmb")
            W("vector", "dmb", v_l)
            v_c = R("vector", lambda e: e.tensor_copy(wconv[:, 32:560], wstage[:, 32:560]),
                    inc="dve")
            v_c = R("vector", lambda e: e.tensor_copy(wsmall[:, 56:560], wstage[:, 56:560]),
                    inc="dve")
            W("sync", "dve", kb.cnt["dve"])
            v_l = kb.dma(wstage[0:1, 40:44], forget_d, sem="dmb")
            W("vector", "dmb", v_l)
            v_c = R("vector", lambda e: e.tensor_copy(stat[0:1, 40:44], wstage[0:1, 40:44]),
                    inc="dve")
            W("scalar", "dve", v_c)
            v_a = R("scalar", lambda e: e.activation(stat[0:1, 336:340], stat[0:1, 40:44],
                                                     AF.Sigmoid), inc="act")
            W("vector", "act", v_a)
            v_c = R("vector", lambda e: e.tensor_scalar(
                stat[0:1, 336:340], stat[0:1, 336:340], 0.45, 0.55, ALU.mult, ALU.add),
                inc="dve")
            W("vector", "dve", v_c)
            v_c = R("vector", lambda e: e.tensor_copy(statb[0:1, 0:4], stat[0:1, 336:340]),
                    inc="dve")
            W("vector", "dve", v_c)
            v_c = R("vector", lambda e: e.tensor_copy(stat[0:1, 340:344], statb[0:1, 0:4]),
                    inc="dve")
            W("vector", "dve", v_c)
            v_c = R("vector", lambda e: e.tensor_sub(stat[0:1, 344:348], stat[0:1, 336:340],
                                                     stat[0:1, 340:344]), inc="dve")
            W("vector", "dve", v_c)
            v_c = R("vector", lambda e: e.tensor_copy(statb[0:1, 4:8], stat[0:1, 344:348]),
                    inc="dve")
            W("tensor", "dve", v_c)
            for h in range(H_B):
                for which, col in ((0, h), (1, 4 + h)):
                    v_pe = R("tensor", lambda e, col=col: mm(
                        e, p1a[:, 0:1], ones_row[0:1, 0:128], statb[0:1, col:col + 1],
                        True, True), inc="pe")
                    W("vector", "pe", v_pe)
                    v_c = R("vector", lambda e: e.tensor_copy(stat[:, 350:351], p1a[:, 0:1]),
                            inc="dve")
                    W("vector", "dve", v_c)
                    v_c = R("vector", lambda e, h=h, which=which: e.tensor_tensor(
                        wstream[:, 2048 + (h * 2 + which) * 128: 2048 + (h * 2 + which + 1) * 128],
                        ident_b[:], stat[:, 350:351].broadcast_to((128, 128)), ALU.mult),
                        inc="dve")
                    W("tensor", "dve", v_c)
            for s0 in range(0, T, 512):
                v_pe = 0
                for c in range(8):
                    v_pe = R("tensor", lambda e, c=c, s0=s0: mm(
                        e, p1a[0:4, 0:512], wsmall[:, c * 4: (c + 1) * 4],
                        x_ln[:, c * T + s0: c * T + s0 + 512], c == 0, c == 7), inc="pe")
                W("scalar", "pe", v_pe)
                v_a = R("scalar", lambda e, s0=s0: e.activation(
                    statb[0:4, 0:512], p1a[0:4, 0:512], AF.Sigmoid), inc="act")
                W("tensor", "act", v_a)
                for st in range(4):
                    i = s0 // 128 + st
                    vp = R("tensor", lambda e, st=st: e.transpose(
                        ptr[:, 0:4], statb[0:4, st * 128:(st + 1) * 128],
                        ident_b[0:4, 0:4]), inc="pe")
                    W("vector", "pe", vp)
                    v_d = R("vector", lambda e, i=i: e.tensor_copy(
                        svt[:, i * 4:(i + 1) * 4], ptr[0:4, 0:128].rearrange("p f -> f p")
                        if False else ptr[0:128, 0:4] if False else ptr[:, 0:4]),
                        inc="dve")
                    W("tensor", "dve", v_d)

            for h in range(H_B):
                emit_attn_b_head(h, stage)

            if stage >= 6:
                emit_attn_b_gate()

        def load_w_chunks(wd, h, jslot):
            vs = 0
            for c in range(8):
                W("sync", "dve", kb.cnt["dve"])
                v_l = kb.dma(wstage[:, 0:256], wd[c * 128:(c + 1) * 128,
                                                  h * 256:(h + 1) * 256], sem="dmb")
                W("vector", "dmb", v_l)
                vs = R("vector", lambda e, c=c: e.tensor_copy(
                    wstream[:, c * 256:(c + 1) * 256], wstage[:, 0:256]), inc="dve")
            return vs

        def emit_attn_b_head(h, stage=99):
            for j, wd in ((0, Wq_d), (1, Wk_d), (2, Wv_d)):
                v_w = load_w_chunks(wd, h, 0)
                W("tensor", "dve", v_w)
                for mt in range(2):
                    for s0 in range(0, T, 512):
                        v_pe = 0
                        for c in range(8):
                            v_pe = R("tensor", lambda e, c=c, mt=mt, s0=s0: mm(
                                e, p2a[:, 0:512],
                                wstream[:, c * 256 + mt * 128: c * 256 + mt * 128 + 128],
                                x_ln[:, c * T + s0: c * T + s0 + 512], c == 0, c == 7),
                                inc="pe")
                        W("vector", "pe", v_pe)
                        v_d = R("vector", lambda e, mt=mt, s0=s0: e.tensor_copy(
                            bt2[:, mt * 2048 + s0: mt * 2048 + s0 + 512], p2a[:, 0:512]),
                            inc="dve")
                        W("tensor", "dve", v_d)
                for mt in range(2):
                    cc = 2 * h + mt
                    dst0 = j * 4096 + mt * 2048
                    v_d = R("vector", lambda e, mt=mt, cc=cc, dst0=dst0, j=j: e.tensor_scalar(
                        bt3[:, dst0: dst0 + T], bt2[:, mt * 2048: mt * 2048 + T],
                        wsmall[:, 32 + 24 * 0 + 0: 0] if False else
                        conv_w_col(j, cc, 3), None, ALU.mult), inc="dve")
                    W("vector", "dve", v_d)
                    for jj in range(1, 4):
                        v_d = R("vector", lambda e, mt=mt, cc=cc, dst0=dst0, j=j, jj=jj:
                                e.scalar_tensor_tensor(
                                    bt3[:, dst0 + jj: dst0 + T],
                                    bt2[:, mt * 2048: mt * 2048 + T - jj],
                                    conv_w_col(j, cc, 3 - jj),
                                    bt3[:, dst0 + jj: dst0 + T], ALU.mult, ALU.add),
                                inc="dve")
                        W("vector", "dve", v_d)
                    W("scalar", "dve", v_d)
                    v_a = R("scalar", lambda e, dst0=dst0: e.activation(
                        bt3[:, dst0: dst0 + T], bt3[:, dst0: dst0 + T], AF.Silu),
                        inc="act")
                    W("vector", "act", v_a)
                    W("tensor", "act", v_a)
            for i in range(NTT):
                for mt in range(2):
                    vp = R("tensor", lambda e, i=i, mt=mt: e.transpose(
                        ptr[:, 0:128], bt3[:, 8192 + mt * 2048 + i * 128: 8192 + mt * 2048 + (i + 1) * 128],
                        ident_b[:]), inc="pe")
                    W("vector", "pe", vp)
                    v_d = R("vector", lambda e, i=i, mt=mt, h=h: e.tensor_copy(
                        bt1[:, i * 1024 + h * 256 + mt * 128: i * 1024 + h * 256 + (mt + 1) * 128],
                        ptr[:, 0:128]), inc="dve")
                    W("tensor", "dve", v_d)
            for mt in range(2):
                cc = 2 * h + mt
                v_d = R("vector", lambda e, mt=mt, cc=cc: e.tensor_scalar(
                    bt2[:, mt * 2048: mt * 2048 + T], bt3[:, 8192 + mt * 2048: 8192 + mt * 2048 + T],
                    fir_w_col(0, cc, 2), None, ALU.mult), inc="dve")
                W("vector", "dve", v_d)
                for jj in range(1, 3):
                    v_d = R("vector", lambda e, mt=mt, cc=cc, jj=jj: e.scalar_tensor_tensor(
                        bt2[:, mt * 2048 + jj: mt * 2048 + T],
                        bt3[:, 8192 + mt * 2048: 8192 + mt * 2048 + T - jj],
                        fir_w_col(0, cc, 2 - jj),
                        bt2[:, mt * 2048 + jj: mt * 2048 + T], ALU.mult, ALU.add),
                        inc="dve")
                    W("vector", "dve", v_d)
            v_short = kb.cnt["dve"]
            for i in range(NTT):
                for mt in range(2):
                    W("tensor", "dve", v_short)
                    vp = R("tensor", lambda e, i=i, mt=mt: e.transpose(
                        ptr[:, 0:128], bt2[:, mt * 2048 + i * 128: mt * 2048 + (i + 1) * 128],
                        ident_b[:]), inc="pe")
                    W("vector", "pe", vp)
                    v_d = R("vector", lambda e, mt=mt: e.tensor_copy(
                        svt[:, 64 + mt * 128: 64 + (mt + 1) * 128], ptr[:, 0:128]),
                        inc="dve")
                    W("tensor", "dve", v_d)
                emit_stats_tile(0, h, i, svt[:, 64:320])
                W("gpsimd", "dve", kb.cnt["dve"])
                v_sp = kb.dma(scr_short_b[i * 128:(i + 1) * 128, h * 256:(h + 1) * 256],
                              svt[:, 64:320], sem="dmc", queue="gpsimd")
                W("vector", "dmc", v_sp)
            emit_fir_long(h)
            if stage >= 5:
                emit_delta_head(h)
            for i in range(NTT):
                emit_stats_tile(3, h, i, bt1[:, i * 1024 + h * 256: i * 1024 + (h + 1) * 256])

        def conv_w_col(j, cc, k):
            return wconv[:, 560 + j * 32 + cc * 4 + k: 560 + j * 32 + cc * 4 + k + 1]

        def fir_w_col(which, cc, k):
            if which == 0:
                return wconv[:, 32 + cc * 3 + k: 32 + cc * 3 + k + 1]
            return wconv[:, 56 + cc * 63 + k: 56 + cc * 63 + k + 1]

        def emit_stats_tile(branch, h, i, src_ap):
            base = i * 64 + branch * 16 + h * 4
            W("vector", "dve", kb.cnt["dve"])
            v_sum = R("vector", lambda e: e.tensor_reduce(stat[:, 352:353], src_ap, AX,
                                                          ALU.add), inc="dve")
            v_sq = R("vector", lambda e: e.tensor_tensor(stat[:, 0:256], src_ap, src_ap,
                                                         ALU.mult), inc="dve")
            W("vector", "dve", v_sq)
            v_ssq = R("vector", lambda e: e.tensor_reduce(stat[:, 353:354], stat[:, 0:256],
                                                          AX, ALU.add), inc="dve")
            R("vector", lambda e: e.tensor_reduce(stat[:, 354:355], src_ap, AX, ALU.max))
            W("vector", "dve", v_sum)
            R("vector", lambda e, base=base: e.tensor_scalar(
                stats_tok[:, base: base + 1], stat[:, 352:353], 1.0 / DV, None, ALU.mult))
            W("vector", "dve", v_ssq)
            v_d = R("vector", lambda e: e.tensor_scalar(stat[:, 356:357], stat[:, 352:353],
                                                        1.0 / DV, None, ALU.mult), inc="dve")
            W("vector", "dve", v_d)
            v_d = R("vector", lambda e: e.tensor_mul(stat[:, 356:357], stat[:, 356:357],
                                                     stat[:, 356:357]), inc="dve")
            W("vector", "dve", v_d)
            v_d = R("vector", lambda e: e.scalar_tensor_tensor(
                stat[:, 357:358], stat[:, 353:354], 1.0 / DV, stat[:, 356:357],
                ALU.mult, ALU.subtract), inc="dve")
            W("vector", "dve", v_d)
            R("vector", lambda e, base=base: e.tensor_copy(stats_tok[:, base + 1: base + 2],
                                                           stat[:, 357:358]))
            R("vector", lambda e, base=base: e.tensor_copy(stats_tok[:, base + 2: base + 3],
                                                           stat[:, 354:355]))
            W("scalar", "dve", v_ssq)
            v_a = R("scalar", lambda e: e.activation(stat[:, 355:356], stat[:, 353:354],
                                                     AF.Sqrt), inc="act")
            W("vector", "act", v_a)
            v_d = R("vector", lambda e, base=base: e.tensor_copy(
                stats_tok[:, base + 3: base + 4], stat[:, 355:356]), inc="dve")
            return v_d

        def emit_fir_long(h):
            for mt in range(2):
                cc = 2 * h + mt
                vp = R("tensor", lambda e, cc=cc: e.transpose(
                    ptr[0:63, 0:128], wsmall[:, 56 + cc * 63: 56 + (cc + 1) * 63],
                    ident_b[:]), inc="pe")
                W("vector", "pe", vp)
                v_d = R("vector", lambda e, mt=mt: e.tensor_copy(
                    statb[0:63, mt * 128:(mt + 1) * 128], ptr[0:63, 0:128]), inc="dve")
                W("tensor", "dve", v_d)
            W("sync", "dve", kb.cnt["dve"])
            kb.dma(wstage[0:63, 0:128], din["c_dft_hA"], sem="dmb")
            v_l = kb.dma(wstage[0:63, 128:256], din["c_dft_hB"], sem="dmb")
            W("vector", "dmb", v_l)
            v_c = R("vector", lambda e: e.tensor_copy(statb[0:63, 256:512],
                                                      wstage[0:63, 0:256]), inc="dve")
            W("tensor", "dve", v_c)
            v_pe = R("tensor", lambda e: mm(e, p1a[:, 0:256], statb[0:63, 256:384],
                                            statb[0:63, 0:256], True, True), inc="pe")
            v_pe = R("tensor", lambda e: mm(e, p1b[:, 0:256], statb[0:63, 384:512],
                                            statb[0:63, 0:256], True, True), inc="pe")
            W("vector", "pe", v_pe)
            v_d = R("vector", lambda e: e.tensor_copy(bt2[:, 0:256], p1a[:, 0:256]), inc="dve")
            v_d = R("vector", lambda e: e.tensor_scalar(bt2[:, 256:512], p1b[:, 0:256],
                                                        -1.0, None, ALU.mult), inc="dve")
            v_d = R("vector", lambda e: e.tensor_copy(bt2[:, 512:768], p1b[:, 0:256]), inc="dve")
            v_d = R("vector", lambda e: e.tensor_copy(bt2[:, 768:1024], p1a[:, 0:256]), inc="dve")
            W("vector", "dve", v_d)
            v_d = R("vector", lambda e: e.memset(bt2[0:1, 256:512], 0.0), inc="dve")
            v_d = R("vector", lambda e: e.memset(bt2[0:1, 512:768], 0.0), inc="dve")
            v_d = R("vector", lambda e: e.tensor_copy(bt2[0:1, 768:1024], p1b[0:1, 0:256]),
                    inc="dve")
            W("tensor", "dve", v_d)
            W("sync", "dve", kb.cnt["dve"])
            kb.dma(wstage[:, 0:128], din["c_dft_fwdA"][0:128, :], sem="dmb")
            kb.dma(wstage[:, 128:256], din["c_dft_fwdA"][128:256, :], sem="dmb")
            kb.dma(wstage[:, 256:384], din["c_dft_fwdB"][0:128, :], sem="dmb")
            v_l = kb.dma(wstage[:, 384:512], din["c_dft_fwdB"][128:256, :], sem="dmb")
            W("vector", "dmb", v_l)
            v_c = R("vector", lambda e: e.tensor_copy(bt2[:, 1024:1536], wstage[:, 0:512]),
                    inc="dve")
            W("sync", "dve", kb.cnt["dve"])
            kb.dma(wstage[:, 0:128], din["c_dft_invA"], sem="dmb")
            v_l = kb.dma(wstage[:, 128:256], din["c_dft_invB"], sem="dmb")
            W("vector", "dmb", v_l)
            v_c = R("vector", lambda e: e.tensor_copy(bt2[:, 1536:1792], wstage[:, 0:256]),
                    inc="dve")
            W("tensor", "dve", v_c)
            for i in range(NTT):
                v_pe = 0
                for (dst, c0) in ((p1a, 1024), (p1b, 1280)):
                    for kk in range(2):
                        src_tile = i - 1 + kk
                        if src_tile < 0:
                            continue
                        first = (kk == 0) if i > 0 else True
                        last = (kk == 1)
                        v_pe = R("tensor", lambda e, dst=dst, c0=c0, kk=kk, src_tile=src_tile,
                                 first=first, last=last, h=h: mm(
                                     e, dst[:, 0:256], bt2[:, c0 + kk * 128: c0 + (kk + 1) * 128],
                                     bt1[:, src_tile * 1024 + h * 256: src_tile * 1024 + (h + 1) * 256],
                                     first, last), inc="pe" if last else None)
                W("vector", "pe", v_pe)
                v_d = R("vector", lambda e: e.tensor_copy(bt2[:, 1792:2048], p1a[:, 0:256]),
                        inc="dve")
                v_d = R("vector", lambda e: e.tensor_copy(svt[:, 320:576], p1b[:, 0:256]),
                        inc="dve")
                W("vector", "dve", v_d)
                v_d = R("vector", lambda e: e.tensor_tensor(
                    stat[:, 0:256], bt2[:, 1792:2048], bt2[:, 0:256], ALU.mult), inc="dve")
                v_d = R("vector", lambda e: e.tensor_tensor(
                    stat[:, 256:512], svt[:, 320:576], bt2[:, 256:512], ALU.mult), inc="dve")
                W("vector", "dve", v_d)
                v_d = R("vector", lambda e: e.tensor_add(stat[:, 0:256], stat[:, 0:256],
                                                         stat[:, 256:512]), inc="dve")
                W("vector", "dve", v_d)
                v_d = R("vector", lambda e: e.tensor_copy(svt[:, 576:832], stat[:, 0:256]),
                        inc="dve")
                v_d = R("vector", lambda e: e.tensor_tensor(
                    stat[:, 0:256], bt2[:, 1792:2048], bt2[:, 512:768], ALU.mult), inc="dve")
                v_d = R("vector", lambda e: e.tensor_tensor(
                    stat[:, 256:512], svt[:, 320:576], bt2[:, 768:1024], ALU.mult), inc="dve")
                W("vector", "dve", v_d)
                v_d = R("vector", lambda e: e.tensor_add(stat[:, 0:256], stat[:, 0:256],
                                                         stat[:, 256:512]), inc="dve")
                W("vector", "dve", v_d)
                v_d = R("vector", lambda e: e.tensor_copy(svt[:, 832:1040] if False else
                                                          svt[:, 832:1040],
                                                          stat[:, 0:208]) if False else
                        e.tensor_copy(bt2[:, 1792:2048], stat[:, 0:256]), inc="dve")
                W("tensor", "dve", v_d)
                R("tensor", lambda e: mm(e, p1c[:, 0:256], bt2[:, 1536:1664],
                                         svt[:, 576:832], True, False))
                v_pe = R("tensor", lambda e: mm(e, p1c[:, 0:256], bt2[:, 1664:1792],
                                                bt2[:, 1792:2048], False, True), inc="pe")
                W("vector", "pe", v_pe)
                v_d = R("vector", lambda e: e.tensor_copy(svt[:, 320:576], p1c[:, 0:256]),
                        inc="dve")
                W("tensor", "dve", v_d)
                emit_stats_tile(1, h, i, svt[:, 320:576])
                W("gpsimd", "dve", kb.cnt["dve"])
                v_sp = kb.dma(scr_long_b[i * 128:(i + 1) * 128, h * 256:(h + 1) * 256],
                              svt[:, 320:576], sem="dmc", queue="gpsimd")
                W("vector", "dmc", v_sp)

        def emit_delta_head(h):
            QT, KT, QN, KN, KBF, KBT, VB, TT0, XP, XL, QB0, ABD, ACM, WFM, QPF, UCMP = (
                0, 256, 512, 768, 1024, 1280, 1536, 1792, 1920, 2048, 2176, 2304,
                2432, 2560, 2816, 3072)
            v_z = R("vector", lambda e: e.memset(Sst[:], 0.0), inc="dve")
            v_z = R("vector", lambda e: e.memset(svt[:, 64:576], 0.0), inc="dve")
            W("gpsimd", "dve", v_z)
            W("tensor", "dve", v_z)
            for g in range(NGRP):
                t0 = g * 128
                for (src0, dst) in ((0, QT), (4096, KT)):
                    for mt in range(2):
                        W("tensor", "dve", kb.cnt["dve"])
                        vp = R("tensor", lambda e, src0=src0, mt=mt, t0=t0: e.transpose(
                            ptr[:, 0:128], bt3[:, src0 + mt * 2048 + t0: src0 + mt * 2048 + t0 + 128],
                            ident_b[:]), inc="pe")
                        W("vector", "pe", vp)
                        v_d = R("vector", lambda e, dst=dst, mt=mt: e.tensor_copy(
                            bt2[:, dst + mt * 128: dst + (mt + 1) * 128], ptr[:, 0:128]),
                            inc="dve")
                        W("tensor", "dve", v_d)
                for (src, col) in ((QT, 360), (KT, 361)):
                    v_d = R("vector", lambda e, src=src: e.tensor_tensor(
                        stat[:, 0:256], bt2[:, src: src + 256], bt2[:, src: src + 256],
                        ALU.mult), inc="dve")
                    W("vector", "dve", v_d)
                    v_d = R("vector", lambda e: e.tensor_reduce(
                        stat[:, 364:365], stat[:, 0:256], AX, ALU.add), inc="dve")
                    W("vector", "dve", v_d)
                    v_d = R("vector", lambda e: e.tensor_scalar(
                        stat[:, 364:365], stat[:, 364:365], 1e-6, None, ALU.add), inc="dve")
                    W("scalar", "dve", v_d)
                    v_a = R("scalar", lambda e: e.activation(
                        stat[:, 365:366], stat[:, 364:365], AF.Sqrt), inc="act")
                    W("vector", "act", v_a)
                    v_d = R("vector", lambda e, col=col: e.reciprocal(
                        stat[:, col: col + 1], stat[:, 365:366]), inc="dve")
                    W("vector", "dve", v_d)
                v_d = R("vector", lambda e, g=g, h=h: e.tensor_tensor(
                    stat[:, 362:363], stat[:, 361:362], svt[:, g * 4 + h: g * 4 + h + 1],
                    ALU.mult), inc="dve")
                W("vector", "dve", v_d)
                for (srctok, col, dst) in ((QT, 360, QN), (KT, 361, KN), (KT, 362, KBF)):
                    v_d = R("vector", lambda e, col=col: e.tensor_tensor(
                        statb[:, 0:128], ident_b[:],
                        stat[:, col: col + 1].broadcast_to((128, 128)), ALU.mult),
                        inc="dve")
                    W("tensor", "dve", v_d)
                    for mt in range(2):
                        v_pe = R("tensor", lambda e, srctok=srctok, mt=mt: mm(
                            e, p1a[:, 0:128], bt2[:, srctok + mt * 128: srctok + (mt + 1) * 128],
                            statb[:, 0:128], True, True), inc="pe")
                        W("vector", "pe", v_pe)
                        v_d = R("vector", lambda e, dst=dst, mt=mt: e.tensor_copy(
                            bt2[:, dst + mt * 128: dst + (mt + 1) * 128], p1a[:, 0:128]),
                            inc="dve")
                        W("tensor", "dve", v_d)
                v_d = R("vector", lambda e: e.tensor_tensor(
                    bt2[:, KBT: KBT + 256], bt2[:, KT: KT + 256],
                    stat[:, 362:363].broadcast_to((128, 256)), ALU.mult), inc="dve")
                v_d = R("vector", lambda e, g=g, h=h: e.tensor_tensor(
                    bt2[:, VB: VB + 256], bt1[:, g * 1024 + h * 256: g * 1024 + (h + 1) * 256],
                    svt[:, g * 4 + h: g * 4 + h + 1].broadcast_to((128, 256)), ALU.mult),
                    inc="dve")
                W("tensor", "dve", v_d)
                for mt in range(2):
                    R("tensor", lambda e, mt=mt: mm(
                        e, p1a[:, 0:128], bt2[:, KN + mt * 128: KN + (mt + 1) * 128],
                        bt2[:, KBF + mt * 128: KBF + (mt + 1) * 128], mt == 0, mt == 1))
                v_pe = 0
                for mt in range(2):
                    v_pe = R("tensor", lambda e, mt=mt: mm(
                        e, p1b[:, 0:128], bt2[:, KBF + mt * 128: KBF + (mt + 1) * 128],
                        bt2[:, KN + mt * 128: KN + (mt + 1) * 128], mt == 0, mt == 1),
                        inc="pe")
                W("vector", "pe", v_pe)
                v_d = R("vector", lambda e: e.tensor_tensor(
                    bt2[:, XP: XP + 128], p1a[:, 0:128], masks[:, 128:256], ALU.mult),
                    inc="dve")
                v_d = R("vector", lambda e: e.tensor_tensor(
                    bt2[:, XL: XL + 128], p1b[:, 0:128], masks[:, 0:128], ALU.mult),
                    inc="dve")
                v_d = R("vector", lambda e: e.tensor_add(
                    bt2[:, TT0: TT0 + 128], bt2[:, XP: XP + 128], ident_b[:]),
                    inc="dve")
                W("tensor", "dve", v_d)
                for lvl in range(4):
                    last = (lvl == 3)
                    v_pe = R("tensor", lambda e: mm(e, p1a[:, 0:128], bt2[:, XL: XL + 128],
                                                    bt2[:, XP: XP + 128], True, True),
                             inc="pe")
                    v_pe = R("tensor", lambda e: mm(e, p1b[:, 0:128], bt2[:, XP: XP + 128],
                                                    bt2[:, XL: XL + 128], True, True),
                             inc="pe")
                    W("vector", "pe", v_pe)
                    v_d = R("vector", lambda e: e.tensor_copy(bt2[:, XP: XP + 128],
                                                              p1a[:, 0:128]), inc="dve")
                    v_d = R("vector", lambda e: e.tensor_copy(bt2[:, XL: XL + 128],
                                                              p1b[:, 0:128]), inc="dve")
                    W("vector", "dve", v_d)
                    v_d = R("vector", lambda e: e.tensor_add(
                        bt2[:, QB0: QB0 + 128], bt2[:, XL: XL + 128], ident_b[:]),
                        inc="dve")
                    W("tensor", "dve", v_d)
                    v_pe = R("tensor", lambda e: mm(e, p1a[:, 0:128], bt2[:, QB0: QB0 + 128],
                                                    bt2[:, TT0: TT0 + 128], True, True),
                             inc="pe")
                    W("vector", "pe", v_pe)
                    v_d = R("vector", lambda e: e.tensor_copy(bt2[:, TT0: TT0 + 128],
                                                              p1a[:, 0:128]), inc="dve")
                    W("tensor", "dve", v_d)
                for ci in range(4):
                    v_pe = R("tensor", lambda e, ci=ci: mm(
                        e, p1a[0:32, 0:256], bt2[:, TT0 + ci * 32: TT0 + (ci + 1) * 32],
                        bt2[:, VB: VB + 256], True, True), inc="pe")
                    W("vector", "pe", v_pe)
                    v_d = R("vector", lambda e, ci=ci: e.tensor_copy(
                        bt2[0:32, UCMP + ci * 256: UCMP + (ci + 1) * 256], p1a[0:32, 0:256]),
                        inc="dve")
                    W("tensor", "dve", v_d)
                for mt in range(2):
                    v_pe = R("tensor", lambda e, mt=mt: mm(
                        e, p1b[:, 0:128], bt2[:, KBT + mt * 128: KBT + (mt + 1) * 128],
                        bt2[:, TT0: TT0 + 128], True, True), inc="pe")
                    W("vector", "pe", v_pe)
                    v_d = R("vector", lambda e, mt=mt: e.tensor_copy(
                        bt2[:, WFM + mt * 128: WFM + (mt + 1) * 128], p1b[:, 0:128]),
                        inc="dve")
                    W("tensor", "dve", v_d)
                for ci in range(4):
                    v_pe = 0
                    for mt in range(2):
                        v_pe = R("tensor", lambda e, ci=ci, mt=mt: mm(
                            e, p1b[0:32, 0:32],
                            bt2[:, KN + mt * 128 + ci * 32: KN + mt * 128 + ci * 32 + 32],
                            bt2[:, QN + mt * 128 + ci * 32: QN + mt * 128 + ci * 32 + 32],
                            mt == 0, mt == 1), inc="pe" if mt == 1 else None)
                    W("vector", "pe", v_pe)
                    v_d = R("vector", lambda e, ci=ci: e.tensor_tensor(
                        bt2[0:32, ACM + ci * 32: ACM + (ci + 1) * 32], p1b[0:32, 0:32],
                        masks[0:32, 384:416], ALU.mult), inc="dve")
                    W("tensor", "dve", v_d)
                for ci in range(4):
                    oi = ci * 32
                    for mt in range(2):
                        vp = R("tensor", lambda e, mt=mt, oi=oi: e.transpose(
                            ptr[0:32, 0:128], bt2[:, KN + mt * 128 + oi: KN + mt * 128 + oi + 32],
                            ident_b[:]), inc="pe")
                        W("vector", "pe", vp)
                        v_d = R("vector", lambda e, mt=mt: e.tensor_copy(
                            statb[0:32, 384 + mt * 128: 384 + (mt + 1) * 128],
                            ptr[0:32, 0:128]), inc="dve")
                        W("tensor", "dve", v_d)
                    v_pe = 0
                    for mt in range(2):
                        v_pe = R("tensor", lambda e, mt=mt, oi=oi: mm(
                            e, p1a[0:32, 0:256],
                            bt2[:, WFM + mt * 128 + oi: WFM + mt * 128 + oi + 32],
                            svt[:, 64 + mt * 256: 64 + (mt + 1) * 256], mt == 0, mt == 1),
                            inc="pe" if mt == 1 else None)
                    W("vector", "pe", v_pe)
                    v_d = R("vector", lambda e, ci=ci: e.tensor_sub(
                        statb[0:32, 640:896],
                        bt2[0:32, UCMP + ci * 256: UCMP + (ci + 1) * 256],
                        p1a[0:32, 0:256]), inc="dve")
                    W("tensor", "dve", v_d)
                    for mt in range(2):
                        R("tensor", lambda e, mt=mt, oi=oi: mm(
                            e, p1b[0:32, 0:256],
                            bt2[:, QN + mt * 128 + oi: QN + mt * 128 + oi + 32],
                            svt[:, 64 + mt * 256: 64 + (mt + 1) * 256], mt == 0, False))
                    v_pe = R("tensor", lambda e, ci=ci: mm(
                        e, p1b[0:32, 0:256], bt2[0:32, ACM + ci * 32: ACM + (ci + 1) * 32],
                        statb[0:32, 640:896], False, True), inc="pe")
                    W("vector", "pe", v_pe)
                    v_d = R("vector", lambda e: e.tensor_copy(
                        bt2[0:32, XP: XP + 256], p1b[0:32, 0:256]), inc="dve")
                    W("tensor", "dve", v_d)
                    for mt in range(2):
                        vp = R("tensor", lambda e, mt=mt: e.transpose(
                            ptr[:, 0:32], bt2[0:32, XP + mt * 128: XP + (mt + 1) * 128],
                            ident_b[0:32, 0:32]), inc="pe")
                        W("vector", "pe", vp)
                        v_d = R("vector", lambda e, mt=mt, t0=t0, oi=oi: e.tensor_copy(
                            bt3[:, 12288 + mt * 2048 + t0 + oi: 12288 + mt * 2048 + t0 + oi + 32],
                            ptr[:, 0:32]), inc="dve")
                        W("tensor", "dve", v_d)
                    v_pe = 0
                    for mt, pp in ((0, p2a), (1, p2b)):
                        R("tensor", lambda e, mt=mt, pp=pp, h=h: mm(
                            e, pp[:, 0:256], wstream[:, 2048 + (h * 2) * 128: 2048 + (h * 2 + 1) * 128],
                            svt[:, 64 + mt * 256: 64 + (mt + 1) * 256], True, False))
                        R("tensor", lambda e, mt=mt, pp=pp, h=h: mm(
                            e, pp[:, 0:256], wstream[:, 2048 + (h * 2 + 1) * 128: 2048 + (h * 2 + 2) * 128],
                            svt[:, 64 + mt * 256: 64 + (mt + 1) * 256], False, False))
                        v_pe = R("tensor", lambda e, mt=mt, pp=pp: mm(
                            e, pp[:, 0:256], statb[0:32, 384 + mt * 128: 384 + (mt + 1) * 128],
                            statb[0:32, 640:896], False, True), inc="pe" if mt == 1 else None)
                    W("scalar", "pe", v_pe)
                    v_a = 0
                    for mt, pp in ((0, p2a), (1, p2b)):
                        v_a = R("scalar", lambda e, mt=mt, pp=pp: e.activation(
                            Sst[:, mt * 256:(mt + 1) * 256], pp[:, 0:256], AF.Copy),
                            inc="act")
                    W("gpsimd", "act", v_a)
                    v_p = 0
                    for mt in range(2):
                        v_p = R("gpsimd", lambda e, mt=mt: e.tensor_copy(
                            svt[:, 64 + mt * 256: 64 + (mt + 1) * 256],
                            Sst[:, mt * 256:(mt + 1) * 256]), inc="pool")
                    W("tensor", "pool", v_p)
                    W("tensor", "act", v_a)
            for i in range(NTT):
                for mt in range(2):
                    W("tensor", "dve", kb.cnt["dve"])
                    vp = R("tensor", lambda e, i=i, mt=mt: e.transpose(
                        ptr[:, 0:128], bt3[:, 12288 + mt * 2048 + i * 128: 12288 + mt * 2048 + (i + 1) * 128],
                        ident_b[:]), inc="pe")
                    W("vector", "pe", vp)
                    v_d = R("vector", lambda e, mt=mt: e.tensor_copy(
                        svt[:, 576 + mt * 128: 576 + (mt + 1) * 128], ptr[:, 0:128]),
                        inc="dve")
                    W("tensor", "dve", v_d)
                emit_stats_tile(2, h, i, svt[:, 576:832])
                W("gpsimd", "dve", kb.cnt["dve"])
                v_sp = kb.dma(scr_delta_b[i * 128:(i + 1) * 128, h * 256:(h + 1) * 256],
                              svt[:, 576:832], sem="dmc", queue="gpsimd")
                W("vector", "dmc", v_sp)

        def emit_attn_b_gate():
            W("tensor", "dve", kb.cnt["dve"])
            W("vector", "pe", kb.cnt["pe"])
            W("vector", "act", kb.cnt["act"])
            W("vector", "pool", kb.cnt["pool"])
            for i in range(NTT):
                vp = R("tensor", lambda e, i=i: e.transpose(
                    ptr[0:64, 0:128], stats_tok[:, i * 64:(i + 1) * 64], ident_b[:]),
                    inc="pe")
                W("vector", "pe", vp)
                v_d = R("vector", lambda e, i=i: e.tensor_copy(
                    bt2[0:64, i * 128:(i + 1) * 128], ptr[0:64, 0:128]), inc="dve")
                W("tensor", "dve", v_d)
            W("gpsimd", "dve", kb.cnt["dve"])
            v_sp = kb.dma(scr_dbg3, bt2[0:64, 0:2048], sem="dmc", queue="gpsimd")
            W("vector", "dmc", v_sp)
            W("sync", "dve", kb.cnt["dve"])
            kb.dma(wstage[0:1, 0:4], log_temp_d, sem="dmb")
            kb.dma(wstage[0:1, 16:32], floor_d, sem="dmb")
            v_l = kb.dma(wstage[0:1, 32:48], b2_d, sem="dmb")
            W("vector", "dmb", v_l)
            W("scalar", "dmb", v_l)
            v_a = R("scalar", lambda e: e.activation(stat[0:1, 368:372], wstage[0:1, 0:4],
                                                     AF.Exp, scale=-1.0), inc="act")
            v_a = R("scalar", lambda e: e.activation(stat[0:1, 372:388], wstage[0:1, 16:32],
                                                     AF.Sigmoid), inc="act")
            W("vector", "act", v_a)
            v_d = R("vector", lambda e: e.tensor_scalar(
                stat[0:1, 372:388], stat[0:1, 372:388], MAX_FLOOR, None, ALU.mult),
                inc="dve")
            for j4 in range(4):
                v_d = R("vector", lambda e, j4=j4: e.tensor_copy(
                    stat[0:1, 388:404].rearrange("p (x r) -> p x r", r=4)[:, :, j4],
                    stat[0:1, 368:372]), inc="dve")
            W("vector", "dve", v_d)
            v_d = R("vector", lambda e: e.tensor_copy(statb[0:1, 0:16], stat[0:1, 388:404]),
                    inc="dve")
            v_d = R("vector", lambda e: e.tensor_copy(statb[0:1, 16:32], stat[0:1, 372:388]),
                    inc="dve")
            W("tensor", "dve", v_d)
            v_pe = R("tensor", lambda e: mm(e, p1c[:, 0:16], ones_row[0:1, 0:128],
                                            statb[0:1, 0:16], True, True), inc="pe")
            W("vector", "pe", v_pe)
            v_d = R("vector", lambda e: e.tensor_copy(wconv[:, 0:16], p1c[:, 0:16]),
                    inc="dve")
            W("tensor", "dve", v_d)
            v_pe = R("tensor", lambda e: mm(e, p1c[:, 0:16], ones_row[0:1, 0:128],
                                            statb[0:1, 16:32], True, True), inc="pe")
            W("vector", "pe", v_pe)
            v_d = R("vector", lambda e: e.tensor_copy(wconv[:, 16:32], p1c[:, 0:16]),
                    inc="dve")
            W("tensor", "dmb", kb.cnt["dmb"])
            vp = R("tensor", lambda e: e.transpose(p1c[0:16, 0:1], wstage[0:1, 32:48],
                                                   ident_f[0:1, 0:1]), inc="pe")
            W("vector", "pe", vp)
            v_d = R("vector", lambda e: e.tensor_copy(wconv[0:16, 656:657], p1c[0:16, 0:1]),
                    inc="dve")
            W("tensor", "dve", v_d)
            for hb in range(2):
                W("sync", "dve", kb.cnt["dve"])
                v_l = kb.dma(wstage[0:1, 0:1024], b1_d[:, hb * 1024:(hb + 1) * 1024],
                             sem="dmb")
                W("vector", "dmb", v_l)
                v_c = R("vector", lambda e: e.tensor_copy(statb[0:1, 0:1024],
                                                          wstage[0:1, 0:1024]), inc="dve")
                for c8 in range(8):
                    W("tensor", "dve", v_c)
                    vp = R("tensor", lambda e, c8=c8: e.transpose(
                        ptr[:, 0:1], statb[0:1, c8 * 128:(c8 + 1) * 128],
                        ident_b[0:1, 0:1]), inc="pe")
                    W("vector", "pe", vp)
                    v_c = R("vector", lambda e, hb=hb, c8=c8: e.tensor_copy(
                        b1cols[:, hb * 8 + c8: hb * 8 + c8 + 1], ptr[:, 0:1]), inc="dve")
                    W("tensor", "dve", v_c)
            def stage_w1(m, prev_pe):
                base = 3072 + (m % 2) * 1280
                W("sync", "dve", kb.cnt["dve"])
                for kc in range(9):
                    rows = 128 if kc < 8 else 64
                    kb.dma(wstage[0:rows, 0:128],
                           W1_d[kc * 128: kc * 128 + rows, m * 128:(m + 1) * 128],
                           sem="dmb")
                    W("vector", "dmb", kb.cnt["dmb"])
                    if kc == 0:
                        W("vector", "pe", prev_pe)
                    v_c = R("vector", lambda e, kc=kc, rows=rows, base=base: e.tensor_copy(
                        wstream[0:rows, base + kc * 128: base + (kc + 1) * 128],
                        wstage[0:rows, 0:128]), inc="dve")
                    W("sync", "dve", kb.cnt["dve"])
                kb.dma(wstage[:, 128:144], W2_d[m * 128:(m + 1) * 128, :], sem="dmb")
                W("vector", "dmb", kb.cnt["dmb"])
                v_c = R("vector", lambda e, base=base: e.tensor_copy(
                    wstream[:, base + 1152: base + 1168], wstage[:, 128:144]), inc="dve")
                return v_c

            for half in range(2):
                w2_pe = [0, 0]
                slot_pe = [0, 0]
                v_stage = {0: stage_w1(0, 0)}
                for m in range(16):
                    base = 3072 + (m % 2) * 1280
                    if m + 1 < 16:
                        v_stage[m + 1] = stage_w1(m + 1, slot_pe[(m + 1) % 2])
                    W("tensor", "dve", v_stage[m])
                    for sl2 in range(2):
                        sl = half * 2 + sl2
                        s0 = sl * 512
                        pp = p2a if sl2 == 0 else p2b
                        for kc in range(8):
                            R("tensor", lambda e, kc=kc, m=m, s0=s0, pp=pp, base=base: mm(
                                e, pp[:, 0:512],
                                wstream[:, base + kc * 128: base + (kc + 1) * 128],
                                x_ln[:, kc * T + s0: kc * T + s0 + 512], kc == 0, False))
                        v_pe = R("tensor", lambda e, m=m, s0=s0, pp=pp, base=base: mm(
                            e, pp[:, 0:512], wstream[0:64, base + 1024: base + 1152],
                            bt2[0:64, s0: s0 + 512], False, True), inc="pe")
                        W("scalar", "pe", v_pe)
                        W("scalar", "pe", w2_pe[sl2])
                        v_a = R("scalar", lambda e, m=m, sl2=sl2, pp=pp: e.activation(
                            wsmall[:, sl2 * 512:(sl2 + 1) * 512],
                            pp[:, 0:512], AF.Gelu, bias=b1cols[:, m: m + 1]), inc="act")
                        W("tensor", "act", v_a)
                        ppl = p1a if sl2 == 0 else p1b
                        v_pe = R("tensor", lambda e, m=m, sl2=sl2, ppl=ppl, base=base: mm(
                            e, ppl[0:16, 0:512], wstream[:, base + 1152: base + 1168],
                            wsmall[:, sl2 * 512:(sl2 + 1) * 512], m == 0, m == 15),
                            inc="pe")
                        w2_pe[sl2] = v_pe
                        slot_pe[m % 2] = v_pe
                        W("vector", "pe", v_pe) if m == 15 else None
                for sl2 in range(2):
                    sl = half * 2 + sl2
                    ppl = p1a if sl2 == 0 else p1b
                    W("vector", "dve", kb.cnt["dve"])
                    v_d = R("vector", lambda e, ppl=ppl: e.scalar_tensor_tensor(
                        stat[0:16, 0:512], ppl[0:16, 0:512], 1.0,
                        wconv[0:16, 656:657].broadcast_to((16, 512)), ALU.mult, ALU.add),
                        inc="dve")
                    W("tensor", "dve", v_d)
                    for st in range(4):
                        i = sl * 4 + st
                        vp = R("tensor", lambda e, st=st, ppl=ppl: e.transpose(
                            p1c[:, 0:16], stat[0:16, st * 128:(st + 1) * 128],
                            ident_f[0:16, 0:16]), inc="pe")
                        W("vector", "pe", vp)
                        v_d = R("vector", lambda e, i=i: e.tensor_copy(
                            ptile[:, i * 16:(i + 1) * 16], p1c[:, 0:16]), inc="dve")
                        W("tensor", "dve", v_d)
            W("gpsimd", "dve", kb.cnt["dve"])
            v_sp = kb.dma(scr_dbg, ptile[:], sem="dmc", queue="gpsimd")
            W("vector", "dmc", v_sp)
            for i in range(NTT):
                o0 = i * 16
                v_d = R("vector", lambda e, o0=o0: e.tensor_mul(
                    ptile[:, o0: o0 + 16], ptile[:, o0: o0 + 16], wconv[:, 0:16]),
                    inc="dve")
                W("vector", "dve", v_d)
                for hh in range(4):
                    c0 = o0 + hh * 4
                    v_d = R("vector", lambda e, c0=c0: e.tensor_reduce(
                        stat[:, 400:401], ptile[:, c0: c0 + 4], AX, ALU.max, negate=True),
                        inc="dve")
                    W("scalar", "dve", v_d)
                    v_a = R("scalar", lambda e, c0=c0: e.activation(
                        ptile[:, c0: c0 + 4], ptile[:, c0: c0 + 4], AF.Exp,
                        bias=stat[:, 400:401], accum_out=stat[:, 401:402]), inc="act")
                    W("vector", "act", v_a)
                    v_d = R("vector", lambda e: e.reciprocal(stat[:, 402:403],
                                                             stat[:, 401:402]), inc="dve")
                    W("vector", "dve", v_d)
                    v_d = R("vector", lambda e, c0=c0: e.tensor_scalar(
                        ptile[:, c0: c0 + 4], ptile[:, c0: c0 + 4], stat[:, 402:403],
                        None, ALU.mult), inc="dve")
                    W("vector", "dve", v_d)
                v_d = R("vector", lambda e, o0=o0: e.tensor_max(
                    ptile[:, o0: o0 + 16], ptile[:, o0: o0 + 16], wconv[:, 16:32]),
                    inc="dve")
                W("vector", "dve", v_d)
                for hh in range(4):
                    c0 = o0 + hh * 4
                    v_d = R("vector", lambda e, c0=c0: e.tensor_reduce(
                        stat[:, 401:402], ptile[:, c0: c0 + 4], AX, ALU.add), inc="dve")
                    W("vector", "dve", v_d)
                    v_d = R("vector", lambda e: e.reciprocal(stat[:, 402:403],
                                                             stat[:, 401:402]), inc="dve")
                    W("vector", "dve", v_d)
                    v_d = R("vector", lambda e, c0=c0: e.tensor_scalar(
                        ptile[:, c0: c0 + 4], ptile[:, c0: c0 + 4], stat[:, 402:403],
                        None, ALU.mult), inc="dve")
                    W("vector", "dve", v_d)
            W("gpsimd", "dve", kb.cnt["dve"])
            v_sp = kb.dma(scr_dbg2, ptile[:], sem="dmc", queue="gpsimd")
            W("vector", "dmc", v_sp)
            W("sync", "dve", kb.cnt["dve"])
            v_l = kb.dma(wstage[0:1, 0:256], norm_wd, sem="dmb")
            W("vector", "dmb", v_l)
            v_c = R("vector", lambda e: e.tensor_copy(statb[0:1, 576:832], wstage[0:1, 0:256]),
                    inc="dve")
            W("tensor", "dve", v_c)
            v_pe = R("tensor", lambda e: mm(e, p2a[:, 0:256], ones_row[0:1, 0:128],
                                            statb[0:1, 576:832], True, True), inc="pe")
            W("vector", "pe", v_pe)
            v_d = R("vector", lambda e: e.tensor_copy(statb[:, 0:256], p2a[:, 0:256]),
                    inc="dve")
            W("tensor", "dve", v_d)
            for i in range(NTT):
                W("sync", "dve", kb.cnt["dve"])
                W("sync", "pe", kb.cnt["pe"])
                kb.dma(bt3[:, 0:1024], scr_short_b[i * 128:(i + 1) * 128, :], sem="dmb")
                kb.dma(bt3[:, 1024:2048], scr_long_b[i * 128:(i + 1) * 128, :], sem="dmb")
                v_l = kb.dma(bt3[:, 2048:3072], scr_delta_b[i * 128:(i + 1) * 128, :],
                             sem="dmb")
                W("vector", "dmb", v_l)
                for hh in range(4):
                    d0 = 3072 + hh * 256
                    srcs = (bt3[:, hh * 256:(hh + 1) * 256],
                            bt3[:, 1024 + hh * 256: 1024 + (hh + 1) * 256],
                            bt3[:, 2048 + hh * 256: 2048 + (hh + 1) * 256],
                            bt1[:, i * 1024 + hh * 256: i * 1024 + (hh + 1) * 256])
                    for j4 in range(4):
                        pcol = ptile[:, i * 16 + hh * 4 + j4: i * 16 + hh * 4 + j4 + 1]
                        if j4 == 0:
                            v_d = R("vector", lambda e, d0=d0, src=srcs[0], pcol=pcol:
                                    e.tensor_tensor(bt3[:, d0: d0 + 256], src,
                                                    pcol.broadcast_to((128, 256)), ALU.mult),
                                    inc="dve")
                        else:
                            v_d = R("vector", lambda e, src=srcs[j4], pcol=pcol:
                                    e.tensor_tensor(stat[:, 0:256], src,
                                                    pcol.broadcast_to((128, 256)), ALU.mult),
                                    inc="dve")
                            W("vector", "dve", v_d)
                            v_d = R("vector", lambda e, d0=d0: e.tensor_tensor(
                                bt3[:, d0: d0 + 256], bt3[:, d0: d0 + 256], stat[:, 0:256],
                                ALU.add), inc="dve")
                        W("vector", "dve", v_d)
                    v_d = R("vector", lambda e, d0=d0: e.tensor_tensor(
                        stat[:, 0:256], bt3[:, d0: d0 + 256], bt3[:, d0: d0 + 256],
                        ALU.mult), inc="dve")
                    W("vector", "dve", v_d)
                    v_d = R("vector", lambda e: e.tensor_reduce(
                        stat[:, 400:401], stat[:, 0:256], AX, ALU.add), inc="dve")
                    W("vector", "dve", v_d)
                    v_d = R("vector", lambda e: e.tensor_scalar(
                        stat[:, 400:401], stat[:, 400:401], 1.0 / DV, 1e-5, ALU.mult,
                        ALU.add), inc="dve")
                    W("scalar", "dve", v_d)
                    v_a = R("scalar", lambda e: e.activation(
                        stat[:, 401:402], stat[:, 400:401], AF.Sqrt), inc="act")
                    W("vector", "act", v_a)
                    v_d = R("vector", lambda e: e.reciprocal(stat[:, 402:403],
                                                             stat[:, 401:402]), inc="dve")
                    W("vector", "dve", v_d)
                    v_d = R("vector", lambda e, d0=d0: e.tensor_tensor(
                        bt3[:, d0: d0 + 256], bt3[:, d0: d0 + 256],
                        stat[:, 402:403].broadcast_to((128, 256)), ALU.mult), inc="dve")
                    W("vector", "dve", v_d)
                    v_d = R("vector", lambda e, d0=d0: e.tensor_tensor(
                        bt3[:, d0: d0 + 256], bt3[:, d0: d0 + 256], statb[:, 0:256],
                        ALU.mult), inc="dve")
                    W("vector", "dve", v_d)
                for c in range(8):
                    W("tensor", "dve", kb.cnt["dve"])
                    vp = R("tensor", lambda e, c=c: e.transpose(
                        ptr[:, 0:128], bt3[:, 3072 + c * 128: 3072 + (c + 1) * 128],
                        ident_b[:]), inc="pe")
                    W("vector", "pe", vp)
                    v_d = R("vector", lambda e, c=c, i=i: e.tensor_copy(
                        x_ln[:, c * T + i * 128: c * T + i * 128 + 128], ptr[:, 0:128]),
                        inc="dve")
                    W("tensor", "dve", v_d)
            for o in range(8):
                W("sync", "dve", kb.cnt["dve"])
                for kc in range(8):
                    kb.dma(wstage[:, 0:128] if kc == 0 else wstage[:, 0:128],
                           Wo_d[kc * 128:(kc + 1) * 128, o * 128:(o + 1) * 128], sem="dmb")
                    W("vector", "dmb", kb.cnt["dmb"])
                    v_c = R("vector", lambda e, kc=kc: e.tensor_copy(
                        wstream[:, 3072 + kc * 128: 3072 + (kc + 1) * 128],
                        wstage[:, 0:128]), inc="dve")
                    W("sync", "dve", kb.cnt["dve"])
                W("tensor", "dve", kb.cnt["dve"])
                for s0 in range(0, T, 512):
                    v_pe = 0
                    for kc in range(8):
                        v_pe = R("tensor", lambda e, kc=kc, s0=s0: mm(
                            e, p2a[:, 0:512], wstream[:, 3072 + kc * 128: 3072 + (kc + 1) * 128],
                            x_ln[:, kc * T + s0: kc * T + s0 + 512], kc == 0, kc == 7),
                            inc="pe")
                    W("vector", "pe", v_pe)
                    W("vector", "dve", kb.cnt["dve"])
                    v_d = R("vector", lambda e, o=o, s0=s0: e.tensor_tensor(
                        x_res[:, o * T + s0: o * T + s0 + 512],
                        x_res[:, o * T + s0: o * T + s0 + 512], p2a[:, 0:512], ALU.add),
                        inc="dve")
                    W("vector", "dve", v_d)
                    W("tensor", "dve", v_d)

        def prep_conv_w():
            W("sync", "dve", kb.cnt["dve"])
            kb.dma(wstage[:, 560:592].rearrange("p (c k) -> p c k", k=4),
                   conv_qd.rearrange("(c p) k -> p c k", p=128), sem="dmb")
            kb.dma(wstage[:, 592:624].rearrange("p (c k) -> p c k", k=4),
                   conv_kd.rearrange("(c p) k -> p c k", p=128), sem="dmb")
            v_l = kb.dma(wstage[:, 624:656].rearrange("p (c k) -> p c k", k=4),
                         conv_vd.rearrange("(c p) k -> p c k", p=128), sem="dmb")
            W("vector", "dmb", v_l)
            v_c = R("vector", lambda e: e.tensor_copy(wconv[:, 560:656], wstage[:, 560:656]),
                    inc="dve")
            return v_c

        if stage >= 4:
            prep_conv_w()
            emit_attn_b(stage)
        if stage >= 7:
            emit_ln_slab()
            emit_ffn()

        W("gpsimd", "dve", kb.cnt["dve"])
        W("gpsimd", "pe", kb.cnt["pe"])
        W("gpsimd", "act", kb.cnt["act"])
        for c in range(8):
            kb.dma(out_fm.rearrange("(c p) t -> c p t", p=128)[c],
                   x_res[:, c * T:(c + 1) * T], sem="dmo", queue="gpsimd")
        W("gpsimd", "dmo", kb.cnt["dmo"])

        block = ctx.enter_context(nc.Block())
        block.sync(lambda e: [fn(e) for fn in kb.q["sync"]])
        block.tensor(lambda e: [fn(e) for fn in kb.q["tensor"]])
        block.vector(lambda e: [fn(e) for fn in kb.q["vector"]])
        block.scalar(lambda e: [fn(e) for fn in kb.q["scalar"]])
        block.gpsimd(lambda e: [fn(e) for fn in kb.q["gpsimd"]])

    return nc


def _layout_inputs(inputs, core):
    d = {}
    d["x_fm"] = np.ascontiguousarray(np.asarray(inputs["L0_attn_a_x"])[core].T)
    d["w_qkv"] = inputs["L0_attn_a_w_qkv"]
    d["w_out_a"] = inputs["L0_attn_a_w_out"]
    d["b_out_a"] = np.asarray(inputs["L0_attn_a_b_out"]).reshape(1, D)
    d["Wq"] = inputs["L0_attn_b_Wq"]; d["Wk"] = inputs["L0_attn_b_Wk"]; d["Wv"] = inputs["L0_attn_b_Wv"]
    d["Wb"] = inputs["L0_attn_b_Wb"]
    d["forget"] = np.asarray(inputs["L0_attn_b_forget_param"]).reshape(1, H_B)
    d["conv_q"] = inputs["L0_attn_b_conv_q"]; d["conv_k"] = inputs["L0_attn_b_conv_k"]
    d["conv_v"] = inputs["L0_attn_b_conv_v"]
    d["fir_s"] = np.asarray(inputs["L0_attn_b_fir_short_w"]).reshape(D, 3)
    d["fir_l"] = np.asarray(inputs["L0_attn_b_fir_long_w"]).reshape(D, KLONG)
    d["log_temp"] = np.asarray(inputs["L0_attn_b_log_temp"]).reshape(1, H_B)
    d["floor_p"] = np.asarray(inputs["L0_attn_b_floor_param"]).reshape(1, H_B * 4)
    d["W1"] = inputs["L0_attn_b_W1"]; d["b1"] = np.asarray(inputs["L0_attn_b_b1"]).reshape(1, DMID)
    d["W2"] = inputs["L0_attn_b_W2"]; d["b2"] = np.asarray(inputs["L0_attn_b_b2"]).reshape(1, H_B * 4)
    d["norm_w"] = np.asarray(inputs["L0_attn_b_norm_w"]).reshape(1, DV)
    d["Wo"] = inputs["L0_attn_b_Wo"]
    d["W_lin"] = inputs["L0_ffn_W_lin"]
    d["b_lin"] = np.asarray(inputs["L0_ffn_b_lin"]).reshape(1, D)
    d["gate_w"] = inputs["L0_ffn_gate_w"]
    d["lora_l"] = inputs["L0_ffn_left"]
    d["lora_r"] = np.asarray(inputs["L0_ffn_right"]).reshape(ER, D)
    for k, v in CONSTS.items():
        d["c_" + k] = v
    return {k: np.ascontiguousarray(np.asarray(v, dtype=np.float32)) for k, v in d.items()}


def kernel(**inputs):
    nc = build_program(stage=KSTAGE)
    in_maps = [_layout_inputs(inputs, core) for core in range(NCORE)]
    res = run_bass_kernel_spmd(nc, in_maps, core_ids=list(range(NCORE)))
    out = np.stack([np.asarray(r["out_fm"]).T for r in res.results], axis=0)
    return np.ascontiguousarray(out.astype(np.float32))
